# version 17 (final)
# baseline (speedup 1.0000x reference)
import numpy as np
import ml_dtypes
from contextlib import ExitStack

import concourse.bass as bass
import concourse.mybir as mybir
from concourse.bass_utils import run_bass_kernel_spmd
from concourse.alu_op_type import AluOpType as OP

F32 = mybir.dt.float32
BF16 = mybir.dt.bfloat16
AF = mybir.ActivationFunctionType

D = 1024
TOK = 2048
NCT = 8
NHEAD = 16
HD = 64
CHUNK = 128
SELF_SYNC = True

ENGINES = ("pe", "dve", "act", "pool", "sp")
DMA_QUEUES = ("sp", "pool", "act")
DMA_SLOTS = 6


class Tsr:
    __slots__ = ("h", "name", "parts", "cols", "dtype", "space")

    def __init__(self, h, name, parts, cols, dtype, space):
        self.h, self.name = h, name
        self.parts, self.cols = parts, cols
        self.dtype, self.space = dtype, space

    def __getitem__(self, idx):
        return self.h[idx]


class Rg:
    __slots__ = ("t", "p0", "p1", "c0", "c1", "ap")

    def __init__(self, t, p0=None, p1=None, c0=None, c1=None, ap=None):
        self.t = t
        self.p0 = 0 if p0 is None else p0
        self.p1 = t.parts if p1 is None else p1
        self.c0 = 0 if c0 is None else c0
        self.c1 = t.cols if c1 is None else c1
        if ap is None and t.space != "dram":
            ap = t.h[self.p0:self.p1, self.c0:self.c1]
        elif ap is None:
            ap = t.h
        self.ap = ap


def rg(t, ps=None, cs=None, ap=None):
    p0, p1 = (ps if ps is not None else (None, None))
    c0, c1 = (cs if cs is not None else (None, None))
    return Rg(t, p0, p1, c0, c1, ap)


def overlap(a, b):
    return not (a[1] <= b[0] or b[1] <= a[0])


def queue_engine(q):
    return {"sp": "sp", "pool": "pool", "act": "act"}[q]


class Prog:
    def __init__(self, nc):
        self.nc = nc
        self.ops = {e: [] for e in ENGINES}
        self.acc = {}
        self.slot_use = {(q, s): 0 for q in DMA_QUEUES for s in range(DMA_SLOTS)}
        self.slot_rr = {q: 0 for q in DMA_QUEUES}
        self.sbuf_bytes = 0
        self.n_inst = 0

    def sbuf(self, ctx, name, parts, cols, dtype):
        h = ctx.enter_context(self.nc.sbuf_tensor(name, [parts, cols], dtype))
        self.sbuf_bytes += cols * (4 if dtype == F32 else 2)
        return Tsr(h, name, parts, cols, dtype, "sbuf")

    def psum(self, ctx, name, parts, cols, dtype):
        h = ctx.enter_context(self.nc.psum_tensor(name, [parts, cols], dtype))
        return Tsr(h, name, parts, cols, dtype, "psum")

    def dram_in(self, name, shape, dtype=F32):
        ap = self.nc.dram_tensor(name, list(shape), dtype, kind="ExternalInput").ap()
        cols = int(np.prod(shape[1:])) if len(shape) > 1 else 1
        return Tsr(ap, name, shape[0], cols, dtype, "dram")

    def dram_out(self, name, shape, dtype=F32):
        ap = self.nc.dram_tensor(name, list(shape), dtype, kind="ExternalOutput").ap()
        cols = int(np.prod(shape[1:])) if len(shape) > 1 else 1
        return Tsr(ap, name, shape[0], cols, dtype, "dram")

    def _reg(self, r):
        if r.t.space == "psum":
            return (0, r.t.parts, 0, r.t.cols)
        return (r.p0, r.p1, r.c0, r.c1)

    @staticmethod
    def _ov(a, b):
        return overlap((a[0], a[1]), (b[0], b[1])) and overlap((a[2], a[3]), (b[2], b[3]))

    @staticmethod
    def _covers(a, b):
        return a[0] <= b[0] and a[1] >= b[1] and a[2] <= b[2] and a[3] >= b[3]

    def _touch(self, owner, reads, writes):
        deps = set()
        for r in reads:
            rec = self.acc.setdefault(r.t.name, {"w": [], "r": []})
            reg = self._reg(r)
            for (wreg, wo) in rec["w"]:
                if wo != owner and self._ov(reg, wreg):
                    deps.add(wo)
            rec["r"].append((reg, owner))
        for r in writes:
            rec = self.acc.setdefault(r.t.name, {"w": [], "r": []})
            reg = self._reg(r)
            for (wreg, wo) in rec["w"]:
                if wo != owner and self._ov(reg, wreg):
                    deps.add(wo)
            keep_r = []
            for (rreg, ro) in rec["r"]:
                if self._ov(reg, rreg):
                    if ro != owner:
                        deps.add(ro)
                    if not self._covers(reg, rreg):
                        keep_r.append((rreg, ro))
                else:
                    keep_r.append((rreg, ro))
            rec["r"] = keep_r
            rec["w"] = [wr for wr in rec["w"] if not self._covers(reg, wr[0])]
            rec["w"].append((reg, owner))
        return deps

    def _record(self, eng, rec, deps):
        for dep in deps:
            if dep[0] == "E":
                _, peng, pidx = dep
                if peng == eng and not SELF_SYNC and rec["dma"] is None:
                    continue
                self.ops[peng][pidx]["inc"] = True
                rec["edges"].add((peng, pidx))
            else:
                rec["dedges"].add(dep[1])
        self.ops[eng].append(rec)

    def op(self, eng, fn, reads=(), writes=(), tag="", n_inst=1):
        idx = len(self.ops[eng])
        rec = {"fn": fn, "edges": set(), "dedges": set(), "inc": False,
               "tag": tag, "dma": None}
        deps = self._touch(("E", eng, idx), reads, writes)
        self._record(eng, rec, deps)
        self.n_inst += n_inst
        return (eng, idx)

    def dma(self, queue, out, in_, tag=""):
        eng = queue_engine(queue)
        slot = self.slot_rr[queue]
        self.slot_rr[queue] = (slot + 1) % DMA_SLOTS
        prev = self.slot_use[(queue, slot)]
        self.slot_use[(queue, slot)] = prev + 1
        val = 16 * (prev + 1)
        rec = {"fn": None, "edges": set(), "dedges": set(), "inc": False,
               "tag": tag, "dma": (queue, slot, val, out, in_)}
        deps = self._touch(("D", (queue, slot, val)), [in_], [out])
        if prev > 0:
            rec["dedges"].add((queue, slot, 16 * prev))
        self._record(eng, rec, deps)
        self.n_inst += 1

    def barrier(self, tag="bar"):
        lasts = {}
        for e in ("pe", "dve", "act", "pool"):
            for i in range(len(self.ops[e]) - 1, -1, -1):
                o = self.ops[e][i]
                if o["dma"] is None:
                    if not o.get("bar"):
                        lasts[e] = i
                    break
        dwaits = {(q, s, 16 * u) for (q, s), u in self.slot_use.items() if u > 0}
        for e in ENGINES:
            rec = {"fn": (lambda eng: None), "edges": set(), "dedges": set(dwaits),
                   "inc": False, "tag": tag, "dma": None, "bar": True}
            for pe_, pidx in lasts.items():
                if pe_ == e:
                    continue
                self.ops[pe_][pidx]["inc"] = True
                rec["edges"].add((pe_, pidx))
            self.ops[e].append(rec)

    def emit(self, ctx):
        nc = self.nc
        esem = {e: ctx.enter_context(nc.semaphore(f"sem_{e}"))
                for e in ("pe", "dve", "act", "pool", "sp")}
        dsem = {}
        for q in DMA_QUEUES:
            for s in range(DMA_SLOTS):
                if self.slot_use[(q, s)] > 0:
                    dsem[(q, s)] = ctx.enter_context(nc.semaphore(f"dma_{q}{s}"))
        rank = {}
        for e in ENGINES:
            c = 0
            for i, op_ in enumerate(self.ops[e]):
                if op_["inc"]:
                    c += 1
                    rank[(e, i)] = c
        block = ctx.enter_context(nc.Block())
        prog = self

        def make_body(ename, tail=False):
            def body(eng):
                waited = {}
                for op_ in prog.ops[ename]:
                    for (peng, pidx) in sorted(op_["edges"]):
                        v = rank[(peng, pidx)]
                        if waited.get(("e", peng), 0) < v:
                            eng.wait_ge(esem[peng], v)
                            waited[("e", peng)] = v
                    for (q, s, v) in sorted(op_["dedges"]):
                        if waited.get(("d", q, s), 0) < v:
                            eng.wait_ge(dsem[(q, s)], v)
                            waited[("d", q, s)] = v
                    if op_["dma"] is not None:
                        q, s, v, out, in_ = op_["dma"]
                        eng.dma_start(out=out.ap, in_=in_.ap).then_inc(dsem[(q, s)], 16)
                    else:
                        inst = op_["fn"](eng)
                        if op_["inc"]:
                            assert inst is not None, f"inc w/o inst: {op_['tag']}"
                            inst.then_inc(esem[ename], 1)
                if tail:
                    for (q, s), uses in sorted(prog.slot_use.items()):
                        if uses > 0:
                            tv = 16 * uses
                            if waited.get(("d", q, s), 0) < tv:
                                eng.wait_ge(dsem[(q, s)], tv)
            return body

        block.tensor(make_body("pe"))
        block.vector(make_body("dve"))
        block.scalar(make_body("act"))
        block.gpsimd(make_body("pool"))
        block.sync(make_body("sp", tail=True))

    def mm_group(self, out, terms, tag="", tp=None, start=True, stop=True):
        base0 = terms[0][0].p0
        for (l, r) in terms:
            assert l.p0 == r.p0, f"mm operand base mismatch {tag}: {l.p0} vs {r.p0}"
            assert l.p0 == base0, f"mixed-base psum accumulation {tag}"
            assert l.p1 - l.p0 == r.p1 - r.p0, f"mm K mismatch {tag}"

        def fn(eng, out=out, terms=terms, tp=tp, start=start, stop=stop):
            inst = None
            n = len(terms)
            for i, (l, r) in enumerate(terms):
                inst = eng.matmul(out.ap, l.ap, r.ap,
                                  start=(start and i == 0),
                                  stop=(stop and i == n - 1), tile_position=tp)
            return inst

        reads = [x for lr in terms for x in lr]
        return self.op("pe", fn, reads=reads, writes=[out], tag=tag,
                       n_inst=len(terms))

    def transpose(self, out, in_, ident, tag="", tp=None):
        def fn(eng, out=out, in_=in_, ident=ident, tp=tp):
            return eng.transpose(out.ap, in_.ap, ident.ap, tile_position=tp)
        return self.op("pe", fn, reads=[in_, ident], writes=[out], tag=tag)

    def tt(self, eng, out, in0, in1, op, tag=""):
        def fn(e, out=out, in0=in0, in1=in1, op=op):
            return e.tensor_tensor(out.ap, in0.ap, in1.ap, op)
        return self.op(eng, fn, reads=[in0, in1], writes=[out], tag=tag)

    def ts(self, eng, out, in0, s1, op0, s2=None, op1=None, tag=""):
        reads = [in0] + [s for s in (s1, s2) if isinstance(s, Rg)]
        a1 = s1.ap if isinstance(s1, Rg) else s1
        a2 = s2.ap if isinstance(s2, Rg) else s2

        def fn(e, out=out, in0=in0, a1=a1, a2=a2, op0=op0, op1=op1):
            if op1 is None:
                return e.tensor_scalar(out.ap, in0.ap, a1, None, op0)
            return e.tensor_scalar(out.ap, in0.ap, a1, a2, op0, op1)
        return self.op(eng, fn, reads=reads, writes=[out], tag=tag)

    def stt(self, eng, out, in0, scalar, in1, op0, op1, tag=""):
        reads = [in0, in1] + ([scalar] if isinstance(scalar, Rg) else [])
        sap = scalar.ap if isinstance(scalar, Rg) else scalar

        def fn(e, out=out, in0=in0, sap=sap, in1=in1, op0=op0, op1=op1):
            return e.scalar_tensor_tensor(out.ap, in0.ap, sap, in1.ap, op0, op1)
        return self.op(eng, fn, reads=reads, writes=[out], tag=tag)

    def act(self, out, in_, func, bias=0.0, scale=1.0, tag=""):
        reads = [in_] + [b for b in (bias,) if isinstance(b, Rg)]
        bap = bias.ap if isinstance(bias, Rg) else bias

        def fn(e, out=out, in_=in_, func=func, bap=bap, scale=scale):
            return e.activation(out.ap, in_.ap, func, bias=bap, scale=scale)
        return self.op("act", fn, reads=reads, writes=[out], tag=tag)

    def copy(self, eng, out, in_, tag=""):
        if eng == "act":
            def fn(e, out=out, in_=in_):
                return e.copy(out.ap, in_.ap)
        else:
            def fn(e, out=out, in_=in_):
                return e.tensor_copy(out.ap, in_.ap)
        return self.op(eng, fn, reads=[in_], writes=[out], tag=tag)

    def memset(self, eng, out, const, tag=""):
        def fn(e, out=out, const=const):
            return e.memset(out.ap, const)
        return self.op(eng, fn, reads=[], writes=[out], tag=tag)

    def scan(self, out, d0, d1, initial, op0, op1, tag=""):
        reads = [d0, d1] + ([initial] if isinstance(initial, Rg) else [])
        iap = initial.ap if isinstance(initial, Rg) else initial

        def fn(e, out=out, d0=d0, d1=d1, iap=iap, op0=op0, op1=op1):
            return e.tensor_tensor_scan(out.ap, d0.ap, d1.ap, iap, op0, op1)
        return self.op("dve", fn, reads=reads, writes=[out], tag=tag)

    def reduce(self, out, in_, op, negate=None, tag=""):
        def fn(e, out=out, in_=in_, op=op, negate=negate):
            return e.tensor_reduce(out.ap, in_.ap, mybir.AxisListType.X, op,
                                   negate=negate)
        return self.op("dve", fn, reads=[in_], writes=[out], tag=tag)

    def recip(self, out, in_, tag=""):
        def fn(e, out=out, in_=in_):
            return e.reciprocal(out.ap, in_.ap)
        return self.op("dve", fn, reads=[in_], writes=[out], tag=tag)



def _v128(v):
    return np.ascontiguousarray(np.asarray(v, np.float32).reshape(8, 128).T)


def _consts_host():
    bf = ml_dtypes.bfloat16
    c = {}
    c["identity_b"] = np.eye(128, dtype=np.float32).astype(bf)
    c["identity_f"] = np.eye(128, dtype=np.float32)
    c["ones_col_b"] = np.ones((128, 1), np.float32).astype(bf)
    c["ones_row_f"] = np.ones((1, 128), np.float32)
    c["headsel_b"] = np.zeros((128, 2), np.float32)
    c["headsel_b"][0:64, 0] = 1.0
    c["headsel_b"][64:128, 1] = 1.0
    c["headsel_b"] = c["headsel_b"].astype(bf)
    c["pat2_f"] = np.zeros((2, 128), np.float32)
    c["pat2_f"][0, 0:64] = 1.0
    c["pat2_f"][1, 64:128] = 1.0
    c["ut_mask_f"] = np.triu(np.ones((CHUNK, CHUNK), np.float32), 1)
    c["dg_mask_f"] = np.eye(CHUNK, dtype=np.float32)
    sm = np.ones((128, 256), np.float32)
    sm[:, 0::CHUNK] = 0.0
    c["scan_mask_f"] = sm
    return c


def _l0_attn_host(inp):
    g = lambda k: np.asarray(inp[k], np.float32)
    h = {}
    for nm in ("maa_x", "maa_w", "maa_k", "maa_v", "maa_r", "maa_g",
               "time_decay", "ln_g", "ln_b"):
        h[f"l0a_{nm}"] = _v128(g(f"L0_attn_{nm}"))
    h["l0a_u"] = _v128(g("L0_attn_u").reshape(-1))
    h["l0a_maa_w1"] = g("L0_attn_maa_w1")
    h["l0a_maa_w2"] = np.ascontiguousarray(
        g("L0_attn_maa_w2").transpose(1, 0, 2).reshape(32, 5 * 1024))
    for nm in ("Wr", "Wk", "Wv", "Wo"):
        h[f"l0a_{nm}"] = g(f"L0_attn_{nm}")
    h["l0a_gw1"] = g("L0_attn_gate_w1")
    h["l0a_gw2"] = g("L0_attn_gate_w2")
    h["l0a_dw1"] = g("L0_attn_decay_w1")
    h["l0a_dw2"] = g("L0_attn_decay_w2")
    return h



def _load_rearr(P, sb, dram_t, r=8, direct=False):
    if direct:
        P.dma("pool", Rg(sb, 0, 128, 0, dram_t.cols,
                         ap=sb.h[:, 0:dram_t.cols]), rg(dram_t))
        return
    inap = dram_t.h.rearrange("(r p) f -> p r f", p=128)
    outap = sb.h[:, 0:sb.cols].rearrange("p (r f) -> p r f", r=r)
    P.dma("pool", Rg(sb, 0, sb.parts, 0, sb.cols, ap=outap),
          Rg(dram_t, ap=inap))


def _mkenv(P, ctx, blocks):
    env = {}
    nc = P.nc
    env["x_in"] = P.dram_in("x_t", [D, TOK])
    env["y_out"] = P.dram_out("y_t", [D, TOK])
    cdefs = [("identity_b", 128, 128, BF16), ("identity_f", 128, 128, F32),
             ("ones_col_b", 128, 1, BF16),
             ("ones_row_f", 1, 128, F32), ("headsel_b", 128, 2, BF16),
             ("pat2_f", 2, 128, F32), ("ut_mask_f", CHUNK, CHUNK, F32),
             ("dg_mask_f", CHUNK, CHUNK, F32), ("scan_mask_f", 128, 256, F32)]
    for nm, p, cc, dt in cdefs:
        dr = P.dram_in(nm, [p, cc], dt)
        sb = P.sbuf(ctx, "s_" + nm, p, cc, dt)
        P.dma("sp", rg(sb), rg(dr))
        env[nm] = sb
    env["X"] = [P.sbuf(ctx, f"X{t}", 128, TOK, F32) for t in range(NCT)]
    for t in range(NCT):
        P.dma("sp", rg(env["X"][t]),
              Rg(env["x_in"], t * 128, (t + 1) * 128, 0, TOK,
                 ap=env["x_in"].h[t * 128:(t + 1) * 128, :]))
    return env


def _finish(P, env):
    for t in range(NCT):
        P.dma("sp",
              Rg(env["y_out"], t * 128, (t + 1) * 128, 0, TOK,
                 ap=env["y_out"].h[t * 128:(t + 1) * 128, :]),
              rg(env["X"][t]))



def _post_ln(P, env, sc, r1_of, w, qcols, tag, out_of=None):
    X = env["X"]
    pmL1, pmL2, pmL3, pmL4 = sc["pmL1"], sc["pmL2"], sc["pmL3"], sc["pmL4"]
    r1b_of, r1s_of = sc["R1B"], sc["R1S"]
    LT1, LT2, LT3 = sc["LT1"], sc["LT2"], sc["LT3"]
    q0, q1 = qcols
    if out_of is None:
        out_of = lambda t, c0, c1: rg(X[t], cs=(q0 + c0, q0 + c1))
    for t in range(NCT):
        P.copy("dve", r1b_of(t, 0, w), r1_of(t, 0, w), tag=f"{tag}.r1b{t}")
        P.act(r1s_of(t, 0, w), r1_of(t, 0, w), AF.Square, tag=f"{tag}.r1s{t}")
    for c0 in range(0, w, 512):
        c1 = min(c0 + 512, w)
        wb = c1 - c0
        sb = f"{tag}.{c0}"
        P.mm_group(rg(pmL1, ps=(0, 1), cs=(0, wb)),
                   [(rg(env["ones_col_b"]), r1b_of(t, c0, c1))
                    for t in range(NCT)], tag=f"{sb}.s1")
        P.mm_group(rg(pmL2, ps=(0, 1), cs=(0, wb)),
                   [(rg(env["ones_col_b"]), r1s_of(t, c0, c1))
                    for t in range(NCT)], tag=f"{sb}.s2")
        P.ts("dve", rg(LT1, cs=(c0, c1)), rg(pmL1, ps=(0, 1), cs=(0, wb)),
             1.0 / D, OP.mult, tag=f"{sb}.mu")
        P.ts("dve", rg(LT2, cs=(c0, c1)), rg(pmL2, ps=(0, 1), cs=(0, wb)),
             1.0 / D, OP.mult, tag=f"{sb}.m2")
        P.tt("dve", rg(LT3, cs=(c0, c1)), rg(LT1, cs=(c0, c1)),
             rg(LT1, cs=(c0, c1)), OP.mult, tag=f"{sb}.mu2")
        P.tt("dve", rg(LT2, cs=(c0, c1)), rg(LT2, cs=(c0, c1)),
             rg(LT3, cs=(c0, c1)), OP.subtract, tag=f"{sb}.var")
        P.act(rg(LT2, cs=(c0, c1)), rg(LT2, cs=(c0, c1)), AF.Ln,
              bias=rg(sc["EPSLN"]), tag=f"{sb}.ln")
        P.act(rg(LT2, cs=(c0, c1)), rg(LT2, cs=(c0, c1)), AF.Exp, scale=-0.5,
              tag=f"{sb}.ex")
        P.tt("dve", rg(LT1, cs=(c0, c1)), rg(LT1, cs=(c0, c1)),
             rg(LT2, cs=(c0, c1)), OP.mult, tag=f"{sb}.b2")
        P.mm_group(rg(pmL3, cs=(0, wb)),
                   [(rg(env["ones_row_f"]), rg(LT2, cs=(c0, c1)))], tag=f"{sb}.bc1")
        P.mm_group(rg(pmL4, cs=(0, wb)),
                   [(rg(env["ones_row_f"]), rg(LT1, cs=(c0, c1)))], tag=f"{sb}.bc2")
        for t in range(NCT):
            P.tt("dve", out_of(t, c0, c1), r1_of(t, c0, c1),
                 rg(pmL3, cs=(0, wb)), OP.mult, tag=f"{sb}.a1_{t}")
            P.tt("dve", out_of(t, c0, c1), out_of(t, c0, c1),
                 rg(pmL4, cs=(0, wb)), OP.subtract, tag=f"{sb}.a2_{t}")



def emit_l0_attn(P, ctx, env):
    nc = P.nc
    X = env["X"]
    QW = 256
    NQB = TOK // QW
    CPQ = QW // CHUNK

    d = {}
    for nm in ("maa_x", "maa_w", "maa_k", "maa_v", "maa_r", "maa_g",
               "time_decay", "ln_g", "ln_b", "u"):
        d[nm] = P.dram_in(f"l0a_{nm}", [128, 8])
    d["maa_w1"] = P.dram_in("l0a_maa_w1", [D, 160])
    d["maa_w2"] = P.dram_in("l0a_maa_w2", [32, 5 * 1024])
    for nm in ("Wr", "Wk", "Wv", "Wo"):
        d[nm] = P.dram_in(f"l0a_{nm}", [D, D])
    d["gw1"] = P.dram_in("l0a_gw1", [D, 64])
    d["gw2"] = P.dram_in("l0a_gw2", [64, D])
    d["dw1"] = P.dram_in("l0a_dw1", [D, 64])
    d["dw2"] = P.dram_in("l0a_dw2", [64, D])

    V = {}
    for nm in ("maa_x", "maa_w", "maa_k", "maa_v", "maa_r", "maa_g",
               "time_decay", "ln_g", "ln_b", "u"):
        V[nm] = P.sbuf(ctx, f"v_{nm}", 128, 8, F32)
        P.dma("sp", rg(V[nm]), rg(d[nm]))
    MW1 = P.sbuf(ctx, "MW1", 128, 8 * 160, BF16)
    _load_rearr(P, MW1, d["maa_w1"])
    MW2 = P.sbuf(ctx, "MW2", 32, 5 * 1024, BF16)
    P.dma("pool", rg(MW2), rg(d["maa_w2"]))
    GW1 = P.sbuf(ctx, "GW1", 128, 8 * 64, BF16)
    _load_rearr(P, GW1, d["gw1"])
    DW1 = P.sbuf(ctx, "DW1", 128, 8 * 64, BF16)
    _load_rearr(P, DW1, d["dw1"])
    GW2 = P.sbuf(ctx, "GW2", 64, D, BF16)
    P.dma("pool", rg(GW2), rg(d["gw2"]))
    DW2 = P.sbuf(ctx, "DW2", 64, D, BF16)
    P.dma("pool", rg(DW2), rg(d["dw2"]))

    WBUF = [P.sbuf(ctx, f"l0a_wbuf{i}", 128, 8 * 1024, BF16) for i in range(2)]

    XXB = P.sbuf(ctx, "XXB", 128, NCT * QW, BF16)
    XXX = P.sbuf(ctx, "XXX", 128, NCT * QW, BF16)
    XF = [P.sbuf(ctx, "XF0", 128, NCT * QW, BF16), XXX]
    LORA = P.sbuf(ctx, "LORA", 32, 5 * QW, BF16)
    TGD = P.sbuf(ctx, "TGD", 64, QW, BF16)
    TGG = P.sbuf(ctx, "TGG", 64, QW, BF16)
    RB = P.sbuf(ctx, "RB_", 128, NCT * QW, BF16)
    KB = P.sbuf(ctx, "KB_", 128, NCT * QW, BF16)
    GB = P.sbuf(ctx, "GB_", 128, NCT * QW, BF16)
    VT = P.sbuf(ctx, "VT_", 128, (QW // 128) * D, BF16)
    PS = P.sbuf(ctx, "PS_", 128, NCT * QW, F32)
    CP = P.sbuf(ctx, "CP_", 128, NCT * QW, F32)
    RT = P.sbuf(ctx, "RT_", 128, NCT * QW, BF16)
    KH = P.sbuf(ctx, "KH_", 128, NCT * QW, BF16)
    K2 = P.sbuf(ctx, "K2_", 128, NCT * QW, BF16)
    KU = KB
    DT = P.sbuf(ctx, "DT_", 128, NCT * CPQ, F32)
    YY = P.sbuf(ctx, "YY_", 128, NCT * QW, F32)
    Y2B = P.sbuf(ctx, "Y2B", 128, NCT * QW, BF16)
    YGB = P.sbuf(ctx, "YGB", 128, NCT * QW, BF16)
    YB = YGB
    MT1 = [P.sbuf(ctx, f"MT1_{i}", 128, CHUNK, BF16) for i in range(2)]
    MT2 = [P.sbuf(ctx, f"MT2_{i}", 128, CHUNK, BF16) for i in range(2)]
    ASB = [P.sbuf(ctx, f"ASB_{i}", 128, CHUNK, BF16) for i in range(2)]
    K2TM = [P.sbuf(ctx, f"K2TM{i}", 128, 64, BF16) for i in range(2)]
    YT1 = [P.sbuf(ctx, f"YT1_{i}", 128, CHUNK, F32) for i in range(2)]
    SF = P.sbuf(ctx, "SF_", 128, 8 * 64, F32)
    SB = P.sbuf(ctx, "SB_", 128, 8 * 64, BF16)
    ZC = P.sbuf(ctx, "ZC_", 128, 1, F32)
    ST1 = [P.sbuf(ctx, f"gT1_{i}", 2, QW, F32) for i in range(2)]
    ST2 = [P.sbuf(ctx, f"gT2_{i}", 2, QW, F32) for i in range(2)]
    ST3 = [P.sbuf(ctx, f"gT3_{i}", 2, QW, F32) for i in range(2)]
    PAT2B = env["pat2_f"]
    sc = {
        "R1": YY,
        "R1B": (lambda t, c0, c1: rg(YB, cs=(t * QW + c0, t * QW + c1))),
        "R1S": (lambda t, c0, c1: rg(Y2B, cs=(t * QW + c0, t * QW + c1))),
        "LT1": Tsr(ST1[0].h, ST1[0].name, 1, QW, F32, "sbuf"),
        "LT2": Tsr(ST2[0].h, ST2[0].name, 1, QW, F32, "sbuf"),
        "LT3": Tsr(ST3[0].h, ST3[0].name, 1, QW, F32, "sbuf"),
    }
    pm = [P.psum(ctx, f"pm{i}", 128, 512, F32) for i in range(7)]
    PT = P.psum(ctx, "pmT", 128, 1024, BF16)
    sc["pmL1"], sc["pmL2"], sc["pmL3"], sc["pmL4"] = pm[0], pm[1], pm[2], pm[3]

    EPSGN = P.sbuf(ctx, "EPSGN", 2, 1, F32)
    sc["EPSLN"] = P.sbuf(ctx, "EPSLN", 1, 1, F32)
    P.memset("dve", rg(EPSGN), 0.00064, tag="epsg")
    P.memset("dve", rg(sc["EPSLN"]), 1e-5, tag="epsl")
    taps = {}
    for nm, dt_ in (("y", F32), ("r", BF16), ("k", BF16), ("cp", F32),
                    ("g", BF16), ("yn", BF16)):
        if nm in DEBUG_TAPS:
            taps[nm] = P.dram_out(f"dbg_{nm}", [128, NQB * NCT * QW], dt_)
    if "v" in DEBUG_TAPS:
        taps["v"] = P.dram_out("dbg_v", [128, NQB * 2048], BF16)

    P.memset("dve", rg(SF), 0.0, tag="s0")
    P.memset("dve", rg(SB), 0.0, tag="s0b")
    P.memset("dve", rg(ZC), 0.0, tag="zc")

    mnames = ("maa_w", "maa_k", "maa_v", "maa_r", "maa_g")

    for qb in range(NQB):
        q0 = qb * QW
        tg = f"a{qb}"
        wuse = {"Wk": WBUF[0], "Wv": WBUF[1], "Wr": WBUF[0], "Wo": WBUF[1]}
        _load_rearr(P, WBUF[0], d["Wk"])
        _load_rearr(P, WBUF[1], d["Wv"])
        for t in range(NCT):
            cs = (t * QW, t * QW + QW)
            if qb == 0:
                P.tt("dve", rg(XXB, cs=(cs[0], cs[0] + 1)), rg(ZC),
                     rg(X[t], cs=(0, 1)), OP.subtract, tag=f"{tg}.xx0_{t}")
                P.tt("dve", rg(XXB, cs=(cs[0] + 1, cs[1])),
                     rg(X[t], cs=(q0, q0 + QW - 1)),
                     rg(X[t], cs=(q0 + 1, q0 + QW)), OP.subtract,
                     tag=f"{tg}.xx_{t}")
            else:
                P.tt("dve", rg(XXB, cs=cs), rg(X[t], cs=(q0 - 1, q0 + QW - 1)),
                     rg(X[t], cs=(q0, q0 + QW)), OP.subtract, tag=f"{tg}.xx_{t}")
        for t in range(NCT):
            cs = (t * QW, t * QW + QW)
            P.stt("dve", rg(XXX, cs=cs), rg(XXB, cs=cs),
                  rg(V["maa_x"], cs=(t, t + 1)), rg(X[t], cs=(q0, q0 + QW)),
                  OP.mult, OP.add, tag=f"{tg}.xxx_{t}")
        P.mm_group(rg(pm[4], ps=(0, 96), cs=(0, QW)),
                   [(rg(MW1, cs=(t * 160, t * 160 + 96)),
                     rg(XXX, cs=(t * QW, (t + 1) * QW))) for t in range(NCT)],
                   tag=f"{tg}.lra")
        P.mm_group(rg(pm[5], ps=(0, 64), cs=(0, QW)),
                   [(rg(MW1, cs=(t * 160 + 96, t * 160 + 160)),
                     rg(XXX, cs=(t * QW, (t + 1) * QW))) for t in range(NCT)],
                   tag=f"{tg}.lrb")
        for f in range(3):
            P.act(rg(LORA, ps=(0, 32), cs=(f * QW, (f + 1) * QW)),
                  rg(pm[4], ps=(f * 32, f * 32 + 32), cs=(0, QW)), AF.Tanh,
                  tag=f"{tg}.lt{f}")
        for f in range(3, 5):
            P.act(rg(LORA, ps=(0, 32), cs=(f * QW, (f + 1) * QW)),
                  rg(pm[5], ps=((f - 3) * 32, (f - 3) * 32 + 32), cs=(0, QW)),
                  AF.Tanh, tag=f"{tg}.lt{f}")
        for f in range(5):
            xf = XF[f % 2]
            for tb in (0, 4):
                for t in range(tb, tb + 4):
                    P.mm_group(rg(pm[t % 4], cs=(0, QW)),
                               [(rg(MW2, cs=(f * 1024 + t * 128, f * 1024 + t * 128 + 128)),
                                 rg(LORA, ps=(0, 32), cs=(f * QW, (f + 1) * QW)))],
                               tag=f"{tg}.m{f}_{t}")
                for t in range(tb, tb + 4):
                    cs = (t * QW, (t + 1) * QW)
                    P.stt("dve", rg(xf, cs=cs), rg(pm[t % 4], cs=(0, QW)),
                          rg(V[mnames[f]], cs=(t, t + 1)), rg(XXB, cs=cs),
                          OP.add, OP.mult, tag=f"{tg}.s{f}_{t}")
            for t in range(NCT):
                cs = (t * QW, (t + 1) * QW)
                P.tt("dve", rg(xf, cs=cs), rg(xf, cs=cs),
                     rg(X[t], cs=(q0, q0 + QW)), OP.add, tag=f"{tg}.xf{f}_{t}")
            if f == 0:
                P.mm_group(rg(pm[6], ps=(0, 64), cs=(0, QW)),
                           [(rg(DW1, cs=(t * 64, (t + 1) * 64)),
                             rg(xf, cs=(t * QW, (t + 1) * QW))) for t in range(NCT)],
                           tag=f"{tg}.dc1")
                P.act(rg(TGD), rg(pm[6], ps=(0, 64), cs=(0, QW)), AF.Tanh,
                      tag=f"{tg}.dct")
                for t in range(NCT):
                    bank = pm[4 + (t % 2)]
                    P.mm_group(rg(bank, cs=(0, QW)),
                               [(rg(DW2, cs=(t * 128, (t + 1) * 128)), rg(TGD))],
                               tag=f"{tg}.dc2_{t}")
                    P.act(rg(PS, cs=(t * QW, (t + 1) * QW)), rg(bank, cs=(0, QW)),
                          AF.Exp, bias=rg(V["time_decay"], cs=(t, t + 1)),
                          tag=f"{tg}.dce_{t}")
            elif f == 1:
                for co in range(NCT):
                    bank = pm[4 + (co % 2)]
                    P.mm_group(rg(bank, cs=(0, QW)),
                               [(rg(wuse["Wk"], cs=(t * 1024 + co * 128, t * 1024 + co * 128 + 128)),
                                 rg(xf, cs=(t * QW, (t + 1) * QW))) for t in range(NCT)],
                               tag=f"{tg}.k_{co}")
                    P.copy("dve", rg(KB, cs=(co * QW, (co + 1) * QW)),
                           rg(bank, cs=(0, QW)), tag=f"{tg}.ke_{co}")
            elif f == 2:
                _load_rearr(P, WBUF[0], d["Wr"])
                for tt in range(QW // 128):
                    for nb in range(2):
                        bank = pm[4 + (nb % 2)]
                        P.mm_group(rg(bank),
                                   [(rg(xf, cs=(t * QW + tt * 128, t * QW + tt * 128 + 128)),
                                     rg(wuse["Wv"], cs=(t * 1024 + nb * 512, t * 1024 + nb * 512 + 512)))
                                    for t in range(NCT)], tag=f"{tg}.v_{tt}{nb}")
                        P.copy("dve", rg(VT, cs=(tt * 1024 + nb * 512, tt * 1024 + nb * 512 + 512)),
                               rg(bank), tag=f"{tg}.ve_{tt}{nb}")
            elif f == 3:
                _load_rearr(P, WBUF[1], d["Wo"])
                for co in range(NCT):
                    bank = pm[4 + (co % 2)]
                    P.mm_group(rg(bank, cs=(0, QW)),
                               [(rg(wuse["Wr"], cs=(t * 1024 + co * 128, t * 1024 + co * 128 + 128)),
                                 rg(xf, cs=(t * QW, (t + 1) * QW))) for t in range(NCT)],
                               tag=f"{tg}.r_{co}")
                    P.copy("dve", rg(RB, cs=(co * QW, (co + 1) * QW)),
                           rg(bank, cs=(0, QW)), tag=f"{tg}.re_{co}")
            elif f == 4:
                P.mm_group(rg(pm[6], ps=(0, 64), cs=(0, QW)),
                           [(rg(GW1, cs=(t * 64, (t + 1) * 64)),
                             rg(xf, cs=(t * QW, (t + 1) * QW))) for t in range(NCT)],
                           tag=f"{tg}.g1")
                P.act(rg(TGG), rg(pm[6], ps=(0, 64), cs=(0, QW)), AF.Tanh,
                      tag=f"{tg}.gt")
                for t in range(NCT):
                    bank = pm[4 + (t % 2)]
                    P.mm_group(rg(bank, cs=(0, QW)),
                               [(rg(GW2, cs=(t * 128, (t + 1) * 128)), rg(TGG))],
                               tag=f"{tg}.g2_{t}")
                    P.copy("dve", rg(GB, cs=(t * QW, (t + 1) * QW)),
                           rg(bank, cs=(0, QW)), tag=f"{tg}.ge_{t}")
        if "k" in taps:
            tp_ = taps["k"]
            P.dma("sp", Rg(tp_, 0, 128, qb * NCT * QW, (qb + 1) * NCT * QW,
                           ap=tp_.h[:, qb * NCT * QW:(qb + 1) * NCT * QW]),
                  rg(KB), tag=f"{tg}.tapk")
        CS = [(t * QW, (t + 1) * QW) for t in range(NCT)]
        for t in range(NCT):
            P.scan(rg(CP, cs=CS[t]), rg(env["scan_mask_f"]), rg(PS, cs=CS[t]),
                   0.0, OP.mult, OP.add, tag=f"{tg}.cp{t}")
        for t in range(NCT):
            P.tt("dve", rg(PS, cs=CS[t]), rg(PS, cs=CS[t]), rg(CP, cs=CS[t]),
                 OP.subtract, tag=f"{tg}.e1s{t}")
        for t in range(NCT):
            P.act(rg(PS, cs=CS[t]), rg(PS, cs=CS[t]), AF.Exp, tag=f"{tg}.e1e{t}")
        for t in range(NCT):
            P.tt("dve", rg(RT, cs=CS[t]), rg(RB, cs=CS[t]), rg(PS, cs=CS[t]),
                 OP.mult, tag=f"{tg}.rt{t}")
        for t in range(NCT):
            P.ts("dve", rg(PS, cs=CS[t]), rg(CP, cs=CS[t]), 80.0, OP.min,
                 tag=f"{tg}.khs{t}")
        for t in range(NCT):
            P.act(rg(PS, cs=CS[t]), rg(PS, cs=CS[t]), AF.Exp, tag=f"{tg}.khe{t}")
        for t in range(NCT):
            P.tt("dve", rg(KH, cs=CS[t]), rg(KB, cs=CS[t]), rg(PS, cs=CS[t]),
                 OP.mult, tag=f"{tg}.kh{t}")
        for t in range(NCT):
            for c in range(CPQ):
                ccs = (CS[t][0] + c * CHUNK, CS[t][0] + (c + 1) * CHUNK)
                P.ts("dve", rg(PS, cs=ccs), rg(CP, cs=ccs),
                     rg(CP, cs=(ccs[1] - 1, ccs[1])), OP.subtract,
                     tag=f"{tg}.k2s{t}_{c}")
        for t in range(NCT):
            P.act(rg(PS, cs=CS[t]), rg(PS, cs=CS[t]), AF.Exp, tag=f"{tg}.k2e{t}")
        for t in range(NCT):
            P.tt("dve", rg(K2, cs=CS[t]), rg(KB, cs=CS[t]), rg(PS, cs=CS[t]),
                 OP.mult, tag=f"{tg}.k2{t}")
        for t in range(NCT):
            P.act(rg(DT, cs=(t * CPQ, (t + 1) * CPQ)),
                  Rg(CP, 0, 128, CS[t][0], CS[t][1],
                     ap=CP.h[:, CS[t][0] + CHUNK - 1:CS[t][1]:CHUNK]),
                  AF.Exp, scale=-1.0, tag=f"{tg}.dt{t}")
        for t in range(NCT):
            P.ts("dve", rg(KU, cs=CS[t]), rg(KB, cs=CS[t]),
                 rg(V["u"], cs=(t, t + 1)), OP.mult, tag=f"{tg}.ku{t}")
        for c in range(CPQ):
            for hp2 in range(NHEAD // 2):
                hs = (2 * hp2, 2 * hp2 + 1)
                bA = {0: pm[0], 1: pm[1]}
                bA2 = {0: pm[2], 1: pm[3]}
                bY1 = {0: pm[4], 1: pm[5]}
                bY2 = {0: pm[6], 1: pm[0]}
                bKV = {0: pm[2], 1: pm[3]}
                def HV(h):
                    t = h // 2
                    return (t, 64 * (h % 2), h % 2,
                            (t * QW + c * CHUNK, t * QW + (c + 1) * CHUNK),
                            ((h // 2) * 64, (h // 2) * 64 + 64),
                            (c * 1024 + h * 64, c * 1024 + h * 64 + 64))
                for h in hs:
                    t, hp, p_, ccs, scs, vcs = HV(h)
                    htg = f"{tg}.w{c}_{h}"
                    P.mm_group(rg(bA[p_], cs=(0, CHUNK)),
                               [(rg(KH, ps=(hp, hp + 64), cs=ccs),
                                 rg(RT, ps=(hp, hp + 64), cs=ccs))], tag=htg + ".A")
                    P.mm_group(rg(bA2[p_], cs=(0, CHUNK)),
                               [(rg(KU, ps=(hp, hp + 64), cs=ccs),
                                 rg(RB, ps=(hp, hp + 64), cs=ccs))], tag=htg + ".A2")
                for h in hs:
                    t, hp, p_, ccs, scs, vcs = HV(h)
                    htg = f"{tg}.w{c}_{h}"
                    P.tt("dve", rg(MT1[p_]), rg(bA[p_], cs=(0, CHUNK)),
                         rg(env["ut_mask_f"]), OP.mult, tag=htg + ".m1")
                    P.tt("dve", rg(MT2[p_]), rg(bA2[p_], cs=(0, CHUNK)),
                         rg(env["dg_mask_f"]), OP.mult, tag=htg + ".m2")
                    P.tt("dve", rg(ASB[p_]), rg(MT1[p_]), rg(MT2[p_]), OP.add,
                         tag=htg + ".m3")
                for h in hs:
                    t, hp, p_, ccs, scs, vcs = HV(h)
                    htg = f"{tg}.w{c}_{h}"
                    P.mm_group(rg(bY1[p_], ps=(hp, hp + 64), cs=(0, CHUNK)),
                               [(rg(SB, ps=(hp, hp + 64), cs=scs),
                                 rg(RT, ps=(hp, hp + 64), cs=ccs))],
                               tp=(hp, hp), tag=htg + ".y1")
                    P.transpose(rg(PT, cs=(p_ * 64, p_ * 64 + 64)),
                                rg(K2, ps=(hp, hp + 64), cs=ccs),
                                rg(env["identity_b"], ps=(hp, hp + 64), cs=(hp, hp + 64)),
                                tp=(hp, 0), tag=htg + ".tr")
                for h in hs:
                    t, hp, p_, ccs, scs, vcs = HV(h)
                    htg = f"{tg}.w{c}_{h}"
                    P.copy("dve", rg(K2TM[p_]), rg(PT, cs=(p_ * 64, p_ * 64 + 64)),
                           tag=htg + ".tre")
                for h in hs:
                    t, hp, p_, ccs, scs, vcs = HV(h)
                    htg = f"{tg}.w{c}_{h}"
                    P.mm_group(rg(bY2[p_], ps=(hp, hp + 64), cs=(0, CHUNK)),
                               [(rg(VT, cs=vcs), rg(ASB[p_]))],
                               tp=(0, hp), tag=htg + ".y2")
                for h in hs:
                    t, hp, p_, ccs, scs, vcs = HV(h)
                    htg = f"{tg}.w{c}_{h}"
                    P.copy("act", rg(YT1[p_], ps=(hp, hp + 64)),
                           rg(bY1[p_], ps=(hp, hp + 64), cs=(0, CHUNK)),
                           tag=htg + ".ye1")
                for h in hs:
                    t, hp, p_, ccs, scs, vcs = HV(h)
                    htg = f"{tg}.w{c}_{h}"
                    P.mm_group(rg(bKV[p_], ps=(hp, hp + 64), cs=(0, 64)),
                               [(rg(K2TM[p_]), rg(VT, cs=vcs))],
                               tp=(0, hp), tag=htg + ".kv")
                for h in hs:
                    t, hp, p_, ccs, scs, vcs = HV(h)
                    htg = f"{tg}.w{c}_{h}"
                    P.tt("dve", rg(YY, ps=(hp, hp + 64), cs=ccs),
                         rg(YT1[p_], ps=(hp, hp + 64)),
                         rg(bY2[p_], ps=(hp, hp + 64), cs=(0, CHUNK)),
                         OP.add, tag=htg + ".ye2")
                    P.stt("dve", rg(SF, ps=(hp, hp + 64), cs=scs),
                          rg(SF, ps=(hp, hp + 64), cs=scs),
                          rg(DT, ps=(hp, hp + 64), cs=(t * CPQ + c, t * CPQ + c + 1)),
                          rg(bKV[p_], ps=(hp, hp + 64), cs=(0, 64)), OP.mult, OP.add,
                          tag=htg + ".su")
                    P.copy("dve", rg(SB, ps=(hp, hp + 64), cs=scs),
                           rg(SF, ps=(hp, hp + 64), cs=scs), tag=htg + ".sb")
        for nm, buf in (("y", YY), ("r", RB), ("g", GB), ("cp", CP)):
            if nm in taps:
                tp_ = taps[nm]
                P.dma("sp", Rg(tp_, 0, 128, qb * NCT * QW, (qb + 1) * NCT * QW,
                               ap=tp_.h[:, qb * NCT * QW:(qb + 1) * NCT * QW]),
                      rg(buf), tag=f"{tg}.tap{nm}")
        for t in range(NCT):
            p2 = t % 2
            s1, s2, s3 = ST1[p2], ST2[p2], ST3[p2]
            bs1, bs2 = pm[0 + 4 * p2], pm[1 + 4 * p2]
            bb1, bb2 = (pm[2], pm[3]) if p2 == 0 else (pm[6], pm[2])
            cs = (t * QW, (t + 1) * QW)
            P.copy("dve", rg(YB, cs=cs), rg(YY, cs=cs), tag=f"{tg}.yb{t}")
            P.act(rg(Y2B, cs=cs), rg(YY, cs=cs), AF.Square, tag=f"{tg}.y2{t}")
            P.mm_group(rg(bs1, ps=(0, 2), cs=(0, QW)),
                       [(rg(env["headsel_b"]), rg(YB, cs=cs))], tag=f"{tg}.gs1_{t}")
            P.mm_group(rg(bs2, ps=(0, 2), cs=(0, QW)),
                       [(rg(env["headsel_b"]), rg(Y2B, cs=cs))], tag=f"{tg}.gs2_{t}")
            P.ts("dve", rg(s1, cs=(0, QW)), rg(bs1, ps=(0, 2), cs=(0, QW)),
                 1.0 / 64, OP.mult, tag=f"{tg}.gmu{t}")
            P.ts("dve", rg(s2, cs=(0, QW)), rg(bs2, ps=(0, 2), cs=(0, QW)),
                 1.0 / 64, OP.mult, tag=f"{tg}.gm2{t}")
            P.tt("dve", rg(s3, cs=(0, QW)), rg(s1, cs=(0, QW)),
                 rg(s1, cs=(0, QW)), OP.mult, tag=f"{tg}.gmm{t}")
            P.tt("dve", rg(s2, cs=(0, QW)), rg(s2, cs=(0, QW)),
                 rg(s3, cs=(0, QW)), OP.subtract, tag=f"{tg}.gvr{t}")
            P.act(rg(s2, cs=(0, QW)), rg(s2, cs=(0, QW)), AF.Ln,
                  bias=rg(EPSGN), tag=f"{tg}.gln{t}")
            P.act(rg(s2, cs=(0, QW)), rg(s2, cs=(0, QW)), AF.Exp, scale=-0.5,
                  tag=f"{tg}.gex{t}")
            P.tt("dve", rg(s1, cs=(0, QW)), rg(s1, cs=(0, QW)),
                 rg(s2, cs=(0, QW)), OP.mult, tag=f"{tg}.gb2{t}")
            P.mm_group(rg(bb1, cs=(0, QW)),
                       [(rg(PAT2B), rg(s2, cs=(0, QW)))], tag=f"{tg}.gbc1_{t}")
            P.mm_group(rg(bb2, cs=(0, QW)),
                       [(rg(PAT2B), rg(s1, cs=(0, QW)))], tag=f"{tg}.gbc2_{t}")
            P.tt("dve", rg(YY, cs=cs), rg(YY, cs=cs), rg(bb1, cs=(0, QW)),
                 OP.mult, tag=f"{tg}.ga1_{t}")
            P.tt("dve", rg(YY, cs=cs), rg(YY, cs=cs), rg(bb2, cs=(0, QW)),
                 OP.subtract, tag=f"{tg}.ga2_{t}")
            P.ts("dve", rg(YY, cs=cs), rg(YY, cs=cs),
                 rg(V["ln_g"], cs=(t, t + 1)), OP.mult,
                 rg(V["ln_b"], cs=(t, t + 1)), OP.add, tag=f"{tg}.ga3_{t}")
            P.tt("dve", rg(YGB, cs=cs), rg(YY, cs=cs), rg(GB, cs=cs), OP.mult,
                 tag=f"{tg}.ga4_{t}")
        if "yn" in taps:
            tp_ = taps["yn"]
            P.dma("sp", Rg(tp_, 0, 128, qb * NCT * QW, (qb + 1) * NCT * QW,
                           ap=tp_.h[:, qb * NCT * QW:(qb + 1) * NCT * QW]),
                  rg(YGB), tag=f"{tg}.tapyn")
        for co in range(NCT):
            bank = pm[4 + (co % 2)]
            P.mm_group(rg(bank, cs=(0, QW)),
                       [(rg(wuse["Wo"], cs=(t * 1024 + co * 128, t * 1024 + co * 128 + 128)),
                         rg(YGB, cs=(t * QW, (t + 1) * QW))) for t in range(NCT)],
                       tag=f"{tg}.o_{co}")
            P.tt("dve", rg(sc["R1"], cs=(co * QW, (co + 1) * QW)),
                 rg(bank, cs=(0, QW)), rg(X[co], cs=(q0, q0 + QW)), OP.add,
                 tag=f"{tg}.res_{co}")
        _post_ln(P, env, sc,
                 lambda t, c0, c1: rg(sc["R1"], cs=(t * QW + c0, t * QW + c1)),
                 QW, (q0, q0 + QW), f"{tg}.ln")



DEBUG_TAPS = []
BLOCK_EMITTERS = {}


def _register():
    BLOCK_EMITTERS["l0_attn"] = emit_l0_attn
    BLOCK_EMITTERS["l0_ffn"] = emit_l0_ffn
    BLOCK_EMITTERS["l1_attn"] = emit_l1_attn
    BLOCK_EMITTERS["l1_ffn"] = emit_l1_ffn


def build_program(blocks):
    nc = bass.Bass("TRN2", target_bir_lowering=False)
    P = Prog(nc)
    with ExitStack() as ctx:
        env = _mkenv(P, ctx, blocks)
        for bi, b in enumerate(blocks):
            if bi > 0:
                P.barrier(tag=f"bar{bi}")
            with ExitStack() as bctx:
                BLOCK_EMITTERS[b](P, bctx, env)
        _finish(P, env)
        P.emit(ctx)
    return nc, P


def host_inputs(inputs):
    h = {}
    h.update(_consts_host())
    h.update(_l0_attn_host(inputs))
    h.update(_l0_ffn_host(inputs))
    h.update(_l1_attn_host(inputs))
    h.update(_l1_ffn_host(inputs))
    return h


LAUNCH_PLAN = [("l0_attn", "l0_ffn", "l1_attn", "l1_ffn")]


def kernel(**inputs):
    _register()
    shared = host_inputs(inputs)
    x = np.asarray(inputs["L0_attn_x"], np.float32)
    xt = [np.ascontiguousarray(x[b].T) for b in range(8)]
    for blocks in LAUNCH_PLAN:
        nc, P = build_program(list(blocks))
        in_maps = [dict(shared, x_t=xt[b]) for b in range(8)]
        res = run_bass_kernel_spmd(nc, in_maps, core_ids=list(range(8)))
        xt = [np.asarray(res.results[b]["y_t"]) for b in range(8)]
    out = np.stack([t.T for t in xt], axis=0)
    return np.ascontiguousarray(out.astype(np.float32))



def _l0_ffn_host(inp):
    g = lambda k: np.asarray(inp[k], np.float32)
    h = {}
    h["l0f_ct"] = np.ascontiguousarray(g("L0_ffn_centroids").T)
    h["l0f_ebias"] = np.ascontiguousarray(
        np.broadcast_to(g("L0_ffn_expert_biases"), (128, 64)))
    h["l0f_wsu"] = g("L0_ffn_Ws_up")
    h["l0f_wsd"] = g("L0_ffn_Ws_down")
    h["l0f_wup"] = np.ascontiguousarray(g("L0_ffn_W_up").reshape(64 * 1024, 512))
    h["l0f_wdn"] = np.ascontiguousarray(g("L0_ffn_W_down").reshape(64 * 512, 1024))
    return h


def emit_l0_ffn(P, ctx, env):
    X = env["X"]
    QB = 1024
    NQ = TOK // QB
    NTT = QB // 128
    E = 64
    EG = 4

    d = {}
    d["ct"] = P.dram_in("l0f_ct", [D, 64])
    d["ebias"] = P.dram_in("l0f_ebias", [128, 64])
    d["wsu"] = P.dram_in("l0f_wsu", [D, 512])
    d["wsd"] = P.dram_in("l0f_wsd", [512, D])
    d["wup"] = P.dram_in("l0f_wup", [E * 1024, 512])
    d["wdn"] = P.dram_in("l0f_wdn", [E * 512, 1024])

    CT = P.sbuf(ctx, "f_CT", 128, 8 * 64, F32)
    inap = d["ct"].h.rearrange("(r p) f -> p r f", p=128)
    outap = CT.h[:, :].rearrange("p (r f) -> p r f", r=8)
    P.dma("sp", Rg(CT, 0, 128, 0, CT.cols, ap=outap), Rg(d["ct"], ap=inap))
    EBI = P.sbuf(ctx, "f_EBI", 128, 64, F32)
    P.dma("sp", rg(EBI), rg(d["ebias"]))

    XB1 = P.sbuf(ctx, "f_XB", 128, NCT * QB, BF16)
    HG = P.sbuf(ctx, "f_HG", 128, EG * 4 * QB, BF16)
    WU = [P.sbuf(ctx, f"f_WU{i}", 128, 8 * 512, BF16) for i in range(2)]
    WD = [P.sbuf(ctx, f"f_WD{i}", 128, 4 * 1024, BF16) for i in range(4)]
    AFM = P.sbuf(ctx, "f_AFM", 128, NTT * 64, F32)
    BSD = P.sbuf(ctx, "f_BSD", 128, NTT * 64, F32)
    MSK = P.sbuf(ctx, "f_MSK", 128, NTT * 64, F32)
    EQT = P.sbuf(ctx, "f_EQT", 128, 64, F32)
    MXT = P.sbuf(ctx, "f_MXT", 128, 1, F32)
    SGA = P.sbuf(ctx, "f_SGA", 128, NTT * 64, F32)
    SS = P.sbuf(ctx, "f_SS", 128, NTT, F32)
    CWE = P.sbuf(ctx, "f_CWE", 64, QB, F32)
    CWR = [P.sbuf(ctx, f"f_CWR{i}", 1, QB, F32) for i in range(2)]
    CWBS = P.sbuf(ctx, "f_CWBS", 128, QB, BF16)
    sc = {
        "LT1": P.sbuf(ctx, "f_LT1", 1, QB, F32),
        "LT2": P.sbuf(ctx, "f_LT2", 1, QB, F32),
        "LT3": P.sbuf(ctx, "f_LT3", 1, QB, F32),
        "EPSLN": P.sbuf(ctx, "f_EPS", 1, 1, F32),
    }
    HGV = Tsr(HG.h, HG.name, 128, EG * 4 * QB, BF16, "sbuf")
    sc["R1B"] = (lambda t, c0, c1: rg(HGV, cs=(t * QB + c0, t * QB + c1)))
    sc["R1S"] = (lambda t, c0, c1: rg(HGV, cs=(8 * QB + t * QB + c0, 8 * QB + t * QB + c1)))
    P.memset("dve", rg(sc["EPSLN"]), 1e-5, tag="f.eps")

    pm = [P.psum(ctx, f"f_pm{i}", 128, 512, F32) for i in range(8)]
    sc["pmL1"], sc["pmL2"], sc["pmL3"], sc["pmL4"] = pm[0], pm[1], pm[2], pm[3]

    for qb in range(NQ):
        q0 = qb * QB
        tg = f"f{qb}"
        for t in range(NCT):
            P.copy("dve", rg(XB1, cs=(t * QB, (t + 1) * QB)),
                   rg(X[t], cs=(q0, q0 + QB)), tag=f"{tg}.xb{t}")
        for tt in range(NTT):
            P.mm_group(rg(pm[4], cs=(0, 64)),
                       [(Rg(X[t], 0, 128, q0 + tt * 128, q0 + tt * 128 + 128),
                         rg(CT, cs=(t * 64, (t + 1) * 64))) for t in range(NCT)],
                       tag=f"{tg}.af{tt}")
            P.copy("dve", rg(AFM, cs=(tt * 64, (tt + 1) * 64)),
                   rg(pm[4], cs=(0, 64)), tag=f"{tg}.afe{tt}")
            P.tt("dve", rg(BSD, cs=(tt * 64, (tt + 1) * 64)),
                 rg(AFM, cs=(tt * 64, (tt + 1) * 64)), rg(EBI), OP.add,
                 tag=f"{tg}.bi{tt}")
        P.memset("dve", rg(MSK), 0.0, tag=f"{tg}.mz")
        for tt in range(NTT):
            bcs = (tt * 64, (tt + 1) * 64)
            for it in range(8):
                P.reduce(rg(MXT), rg(BSD, cs=bcs), OP.max, tag=f"{tg}.mx{tt}_{it}")
                P.ts("dve", rg(EQT), rg(BSD, cs=bcs), rg(MXT), OP.is_ge,
                     tag=f"{tg}.eq{tt}_{it}")
                P.tt("dve", rg(MSK, cs=bcs), rg(MSK, cs=bcs), rg(EQT), OP.add,
                     tag=f"{tg}.ma{tt}_{it}")
                P.stt("dve", rg(BSD, cs=bcs), rg(EQT), -1e30, rg(BSD, cs=bcs),
                      OP.mult, OP.add, tag=f"{tg}.ms{tt}_{it}")
        for tt in range(NTT):
            bcs = (tt * 64, (tt + 1) * 64)
            P.act(rg(SGA, cs=bcs), rg(AFM, cs=bcs), AF.Sigmoid, tag=f"{tg}.sg{tt}")
            P.tt("dve", rg(SGA, cs=bcs), rg(SGA, cs=bcs), rg(MSK, cs=bcs),
                 OP.mult, tag=f"{tg}.wm{tt}")
            P.reduce(rg(SS, cs=(tt, tt + 1)), rg(SGA, cs=bcs), OP.add,
                     tag=f"{tg}.ws{tt}")
        P.ts("dve", rg(SS), rg(SS), 1e-8, OP.add, tag=f"{tg}.se")
        P.recip(rg(SS), rg(SS), tag=f"{tg}.sr")
        for tt in range(NTT):
            bcs = (tt * 64, (tt + 1) * 64)
            P.ts("dve", rg(SGA, cs=bcs), rg(SGA, cs=bcs), rg(SS, cs=(tt, tt + 1)),
                 OP.mult, tag=f"{tg}.cw{tt}")
            P.transpose(rg(pm[5], ps=(0, 64), cs=(0, 128)),
                        rg(SGA, cs=bcs), rg(env["identity_f"]),
                        tag=f"{tg}.ct{tt}")
            P.copy("dve", rg(CWE, ps=(0, 64), cs=(tt * 128, (tt + 1) * 128)),
                   rg(pm[5], ps=(0, 64), cs=(0, 128)), tag=f"{tg}.cte{tt}")
        P.dma("pool", rg(WU[0], cs=(0, 8 * 512)),
              Rg(d["wsu"], ap=d["wsu"].h.rearrange("(r p) f -> p r f", p=128)),
              tag=f"{tg}.lwsu")
        P.dma("pool", rg(WD[0], cs=(0, 4 * 1024)),
              Rg(d["wsd"], ap=d["wsd"].h.rearrange("(r p) f -> p r f", p=128)),
              tag=f"{tg}.lwsd")
        for e in range(-1, E):
            par = (e + 1) % 2
            gslot = (e % EG) if e >= 0 else 0
            wu = WU[par]
            wd = WD[gslot] if e >= 0 else WD[0]
            etg = f"{tg}.e{e}"
            if e >= 0:
                P.dma("sp", rg(CWR[e % 2]),
                      Rg(CWE, e, e + 1, 0, QB, ap=CWE.h[e:e + 1, :]),
                      tag=f"{etg}.cwx")
                for nb in range(2):
                    P.mm_group(rg(pm[6], cs=(0, 512)),
                               [(rg(env["ones_row_f"]),
                                 rg(CWR[e % 2], cs=(nb * 512, nb * 512 + 512)))],
                               tag=f"{etg}.cb{nb}")
                    P.copy("dve", rg(CWBS, cs=(nb * 512, nb * 512 + 512)),
                           rg(pm[6], cs=(0, 512)), tag=f"{etg}.cbe{nb}")
            for fb in range(4):
                for nb in range(2):
                    bank = pm[4 + (nb % 2)]
                    P.mm_group(rg(bank),
                               [(rg(wu, cs=(t * 512 + fb * 128, t * 512 + fb * 128 + 128)),
                                 rg(XB1, cs=(t * QB + nb * 512, t * QB + nb * 512 + 512)))
                                for t in range(NCT)], tag=f"{etg}.up{fb}{nb}")
                    hcs = (gslot * 4 * QB + fb * QB + nb * 512,
                           gslot * 4 * QB + fb * QB + nb * 512 + 512)
                    P.act(rg(HG, cs=hcs), rg(bank), AF.Silu, tag=f"{etg}.si{fb}{nb}")
                    if e >= 0:
                        P.tt("dve", rg(HG, cs=hcs), rg(HG, cs=hcs),
                             rg(CWBS, cs=(nb * 512, nb * 512 + 512)), OP.mult,
                             tag=f"{etg}.sc{fb}{nb}")
            if e == -1 or (e % EG) == EG - 1:
                glo = 0 if e == -1 else e - EG + 1
                ne = 1 if e == -1 else EG
                for dt in range(NCT):
                    for nb in range(2):
                        bank = pm[4 + (nb % 2)]
                        terms = []
                        for ge in range(ne):
                            wdx = WD[ge] if e >= 0 else WD[0]
                            for fb in range(4):
                                terms.append(
                                    (rg(wdx, cs=(fb * 1024 + dt * 128, fb * 1024 + dt * 128 + 128)),
                                     rg(HG, cs=(ge * 4 * QB + fb * QB + nb * 512,
                                                ge * 4 * QB + fb * QB + nb * 512 + 512))))
                        P.mm_group(rg(bank), terms, tag=f"{etg}.dn{dt}{nb}")
                        P.tt("dve", rg(X[dt], cs=(q0 + nb * 512, q0 + nb * 512 + 512)),
                             rg(X[dt], cs=(q0 + nb * 512, q0 + nb * 512 + 512)),
                             rg(bank), OP.add, tag=f"{etg}.ac{dt}{nb}")
            if e + 1 < E:
                en = e + 1
                P.dma("pool", rg(WU[(en + 1) % 2], cs=(0, 8 * 512)),
                      Rg(d["wup"], en * 1024, (en + 1) * 1024, 0, 512,
                         ap=d["wup"].h[en * 1024:(en + 1) * 1024, :]
                         .rearrange("(r p) f -> p r f", p=128)),
                      tag=f"{etg}.lwu")
                P.dma("pool", rg(WD[en % EG], cs=(0, 4 * 1024)),
                      Rg(d["wdn"], en * 512, (en + 1) * 512, 0, 1024,
                         ap=d["wdn"].h[en * 512:(en + 1) * 512, :]
                         .rearrange("(r p) f -> p r f", p=128)),
                      tag=f"{etg}.lwd")
        _post_ln(P, env, sc,
                 lambda t, c0, c1: rg(X[t], cs=(q0 + c0, q0 + c1)),
                 QB, (q0, q0 + QB), f"{tg}.ln")



def _l1_attn_host(inp):
    g = lambda k: np.asarray(inp[k], np.float32)
    h = {}
    h["l1a_wq"] = g("L1_attn_Wq")
    h["l1a_wk"] = g("L1_attn_Wk")
    h["l1a_wv"] = g("L1_attn_Wv")
    wo = g("L1_attn_Wo").reshape(16, 64, 1024)
    h["l1a_wo2"] = np.ascontiguousarray(
        np.concatenate([wo[:8].transpose(1, 0, 2),
                        wo[8:].transpose(1, 0, 2)], axis=0).reshape(128, 8 * 1024))
    h["l1a_bq"] = _v128(g("L1_attn_bq")[:1024])
    h["l1a_bqg"] = np.ascontiguousarray(g("L1_attn_bq")[1024:].reshape(16, 1))
    h["l1a_bk"] = _v128(g("L1_attn_bk"))
    h["l1a_bvbc"] = np.ascontiguousarray(
        np.broadcast_to(g("L1_attn_bv"), (128, 1024)))
    h["l1a_bo"] = _v128(g("L1_attn_bo"))
    h["l1a_qn"] = _v128(np.tile(g("L1_attn_qn_w"), 16))
    h["l1a_kn"] = _v128(np.tile(g("L1_attn_kn_w"), 16))
    return h


def emit_l1_attn(P, ctx, env):
    X = env["X"]
    QB = 512
    NQ = TOK // QB
    NKT = TOK // 128

    d = {}
    for nm, shp in (("wq", [D, 1040]), ("wk", [D, D]), ("wv", [D, D]),
                    ("wo2", [128, 8 * 1024]), ("bq", [128, 8]),
                    ("bqg", [16, 1]), ("bk", [128, 8]), ("bvbc", [128, 1024]),
                    ("bo", [128, 8]), ("qn", [128, 8]), ("kn", [128, 8])):
        d[nm] = P.dram_in(f"l1a_{nm}", shp)

    BQ = P.sbuf(ctx, "a_BQ", 128, 8, F32)
    BK = P.sbuf(ctx, "a_BK", 128, 8, F32)
    BO = P.sbuf(ctx, "a_BO", 128, 8, F32)
    BQG = P.sbuf(ctx, "a_BQG", 16, 1, F32)
    BVB = P.sbuf(ctx, "a_BVB", 128, 1024, BF16)
    QN = P.sbuf(ctx, "a_QN", 128, 8, F32)
    KN = P.sbuf(ctx, "a_KN", 128, 8, F32)
    for nm, t in (("bq", BQ), ("bk", BK), ("bo", BO), ("bqg", BQG),
                  ("qn", QN), ("kn", KN)):
        P.dma("sp", rg(t), rg(d[nm]))
    P.dma("pool", rg(BVB), rg(d["bvbc"]))
    QNKN = P.sbuf(ctx, "a_QNKN", 128, 8, F32)
    P.tt("dve", rg(QNKN), rg(QN), rg(KN), OP.mult, tag="a.qnkn")

    KH = P.sbuf(ctx, "a_KH", 128, NCT * TOK, BF16)
    VT = P.sbuf(ctx, "a_VT", 128, NKT * NHEAD * 65, BF16)
    XBQ = P.sbuf(ctx, "a_XBQ", 128, NCT * QB, BF16)
    WQB = P.sbuf(ctx, "a_WQB", 128, 8 * 1040, BF16)
    WB1 = Tsr(WQB.h, WQB.name, 128, 8 * 1024, BF16, "sbuf")
    WB2 = P.sbuf(ctx, "a_WB2", 128, 8 * 1024, BF16)
    KR = [P.sbuf(ctx, f"a_KR{i}", 128, 512, F32) for i in range(2)]
    _k2b = P.sbuf(ctx, "a_K2B", 128, 512, BF16)
    K2B = [_k2b, _k2b]
    QH = [P.sbuf(ctx, f"a_QH{i}", 128, 512, BF16) for i in range(2)]
    EK = [P.sbuf(ctx, f"a_EK{i}", 128, 512, BF16) for i in range(4)]
    OTMP = P.sbuf(ctx, "a_OTMP", 64, 512, BF16)
    OC = [P.sbuf(ctx, f"a_OC{i}", 64, 512, BF16) for i in range(2)]
    GRB = P.sbuf(ctx, "a_GRB", 65, 2 * 512, F32)
    SGATE = Tsr(GRB.h, GRB.name, 16, 512, F32, "sbuf")
    RZ = P.sbuf(ctx, "a_RZ", 65, 512, F32)
    ONESF = P.sbuf(ctx, "a_ONES", 128, 128, F32)
    ST1 = [P.sbuf(ctx, f"a_ST1{i}", 2, 512, F32) for i in range(2)]
    ST2 = ST1
    EPSA = P.sbuf(ctx, "a_EPSA", 2, 1, F32)
    R1B = P.sbuf(ctx, "a_R1B", 128, NCT * QB, BF16)
    OT2 = R1B
    R1S = None
    sc = {
        "R1B": (lambda t, c0, c1: rg(R1B, cs=(t * QB + c0, t * QB + c1))),
        "R1S": (lambda t, c0, c1: rg(XBQ, cs=(t * QB + c0, t * QB + c1))),
        "LT1": Tsr(KR[0].h, KR[0].name, 1, QB, F32, "sbuf"),
        "LT2": Tsr(RZ.h, RZ.name, 1, QB, F32, "sbuf"),
        "LT3": Tsr(GRB.h, GRB.name, 1, QB, F32, "sbuf"),
        "EPSLN": P.sbuf(ctx, "a_EPS", 1, 1, F32),
    }
    P.memset("dve", rg(sc["EPSLN"]), 1e-5, tag="a.eps")
    P.memset("dve", rg(EPSA), 1e-6, tag="a.eps2")
    P.memset("dve", rg(ONESF), 1.0, tag="a.ones")

    pm = [P.psum(ctx, f"a_pm{i}", 128, 512, F32) for i in range(8)]
    sc["pmL1"], sc["pmL2"], sc["pmL3"], sc["pmL4"] = pm[0], pm[1], pm[2], pm[3]

    _load_rearr(P, WB1, d["wk"])
    _load_rearr(P, WB2, d["wv"])

    for qb in range(NQ):
        q0 = qb * QB
        tg = f"a1_{qb}"
        for t in range(NCT):
            P.copy("dve", rg(XBQ, cs=(t * QB, (t + 1) * QB)),
                   rg(X[t], cs=(q0, q0 + QB)), tag=f"{tg}.xb{t}")
        for co in range(NCT):
            p2 = co % 2
            bank = pm[4 + p2]
            bst, bbc = pm[0 + p2], pm[2 + p2]
            P.mm_group(rg(bank),
                       [(rg(WB1, cs=(t * 1024 + co * 128, t * 1024 + co * 128 + 128)),
                         rg(XBQ, cs=(t * QB, (t + 1) * QB))) for t in range(NCT)],
                       tag=f"{tg}.k{co}")
            P.ts("dve", rg(KR[p2]), rg(bank), rg(BK, cs=(co, co + 1)), OP.add,
                 tag=f"{tg}.kb{co}")
            P.act(rg(K2B[p2]), rg(KR[p2]), AF.Square, tag=f"{tg}.k2{co}")
            P.mm_group(rg(bst, ps=(0, 2)),
                       [(rg(env["headsel_b"]), rg(K2B[p2]))], tag=f"{tg}.ks{co}")
            P.ts("dve", rg(ST1[p2]), rg(bst, ps=(0, 2)), 1.0 / 64, OP.mult,
                 tag=f"{tg}.km{co}")
            P.act(rg(ST1[p2]), rg(ST1[p2]), AF.Ln, bias=rg(EPSA), tag=f"{tg}.kl{co}")
            P.act(rg(ST1[p2]), rg(ST1[p2]), AF.Exp, scale=-0.5, tag=f"{tg}.ke{co}")
            P.mm_group(rg(bbc), [(rg(env["pat2_f"]), rg(ST1[p2]))],
                       tag=f"{tg}.kc{co}")
            P.stt("dve", rg(KH, cs=(co * TOK + q0, co * TOK + q0 + QB)),
                  rg(KR[p2]), rg(QNKN, cs=(co, co + 1)), rg(bbc), OP.mult,
                  OP.mult, tag=f"{tg}.kh{co}")
        for tt in range(QB // 128):
            gt = qb * 4 + tt
            for nb in range(2):
                bank = pm[4 + (nb % 2)]
                P.mm_group(rg(bank),
                           [(rg(XBQ, cs=(t * QB + tt * 128, t * QB + tt * 128 + 128)),
                             rg(WB2, cs=(t * 1024 + nb * 512, t * 1024 + nb * 512 + 512)))
                            for t in range(NCT)], tag=f"{tg}.v{tt}{nb}")
                vap = VT.h[:, gt * NHEAD * 65:(gt + 1) * NHEAD * 65] \
                    .rearrange("p (h c) -> p h c", h=NHEAD)[:, nb * 8:(nb + 1) * 8, 0:64]
                P.stt("dve",
                      Rg(VT, 0, 128, gt * NHEAD * 65, (gt + 1) * NHEAD * 65, ap=vap),
                      rg(bank), 1.0, rg(BVB, cs=(nb * 512, nb * 512 + 512)),
                      OP.mult, OP.add, tag=f"{tg}.vb{tt}{nb}")
            onesap = VT.h[:, gt * NHEAD * 65:(gt + 1) * NHEAD * 65] \
                .rearrange("p (h c) -> p h c", h=NHEAD)[:, :, 64:65]
            P.memset("dve",
                     Rg(VT, 0, 128, gt * NHEAD * 65, (gt + 1) * NHEAD * 65, ap=onesap),
                     1.0, tag=f"{tg}.vo{tt}")

    _load_rearr(P, WQB, d["wq"], r=8)
    P.dma("pool", rg(WB2, cs=(0, 8 * 1024)), rg(d["wo2"]), tag="a.lwo")
    for qb in range(NQ):
        q0 = qb * QB
        tg = f"a2_{qb}"
        for t in range(NCT):
            P.copy("dve", rg(XBQ, cs=(t * QB, (t + 1) * QB)),
                   rg(X[t], cs=(q0, q0 + QB)), tag=f"{tg}.xb{t}")
        P.mm_group(rg(pm[6], ps=(0, 16)),
                   [(rg(WQB, cs=(t * 1040 + 1024, t * 1040 + 1040)),
                     rg(XBQ, cs=(t * QB, (t + 1) * QB))) for t in range(NCT)],
                   tag=f"{tg}.gt")
        P.act(rg(SGATE), rg(pm[6], ps=(0, 16)), AF.Sigmoid, bias=rg(BQG),
              tag=f"{tg}.gs")
        for ct in range(NCT):
            p2 = ct % 2
            bank = pm[4 + p2]
            P.mm_group(rg(bank),
                       [(rg(WQB, cs=(t * 1040 + ct * 128, t * 1040 + ct * 128 + 128)),
                         rg(XBQ, cs=(t * QB, (t + 1) * QB))) for t in range(NCT)],
                       tag=f"{tg}.q{ct}")
            P.ts("dve", rg(KR[p2]), rg(bank), rg(BQ, cs=(ct, ct + 1)), OP.add,
                 tag=f"{tg}.qb{ct}")
            P.act(rg(K2B[p2]), rg(KR[p2]), AF.Square, tag=f"{tg}.q2{ct}")
            P.mm_group(rg(bank, ps=(0, 2)),
                       [(rg(env["headsel_b"]), rg(K2B[p2]))], tag=f"{tg}.qs{ct}")
            P.ts("dve", rg(ST2[p2]), rg(bank, ps=(0, 2)), 1.0 / 64, OP.mult,
                 tag=f"{tg}.qm{ct}")
            P.act(rg(ST2[p2]), rg(ST2[p2]), AF.Ln, bias=rg(EPSA), tag=f"{tg}.ql{ct}")
            P.act(rg(ST2[p2]), rg(ST2[p2]), AF.Exp, scale=-0.5, tag=f"{tg}.qe{ct}")
            P.mm_group(rg(bank), [(rg(env["pat2_f"]), rg(ST2[p2]))],
                       tag=f"{tg}.qc{ct}")
            P.tt("dve", rg(QH[p2]), rg(KR[p2]), rg(bank), OP.mult,
                 tag=f"{tg}.qh{ct}")
            avb = {0: pm[6], 1: pm[7]}
            sbk = {0: (pm[0], pm[1]), 1: (pm[2], pm[3])}
            for ho in range(2):
                h = 2 * ct + ho
                P.dma("sp",
                      Rg(GRB, 64, 65, ho * QB, (ho + 1) * QB,
                         ap=GRB.h[64:65, ho * QB:(ho + 1) * QB]),
                      Rg(SGATE, h, h + 1, 0, QB, ap=SGATE.h[h:h + 1, 0:QB]),
                      tag=f"{tg}.gx{h}")
            for kt in range(NKT + 1):
                if kt < NKT:
                    for ho in range(2):
                        h = 2 * ct + ho
                        hp = 64 * ho
                        P.mm_group(rg(sbk[ho][kt % 2]),
                                   [(rg(KH, ps=(hp, hp + 64),
                                        cs=(ct * TOK + kt * 128, ct * TOK + kt * 128 + 128)),
                                     rg(QH[ct % 2], ps=(hp, hp + 64)))],
                                   tag=f"{tg}.h{2 * ct + ho}.s{kt}")
                    for ho in range(2):
                        P.act(rg(EK[2 * ho + kt % 2]), rg(sbk[ho][kt % 2]), AF.Exp,
                              scale=0.125, tag=f"{tg}.h{2 * ct + ho}.e{kt}")
                if kt >= 1:
                    kp = kt - 1
                    for ho in range(2):
                        h = 2 * ct + ho
                        P.mm_group(rg(avb[ho], ps=(0, 65)),
                                   [(Rg(VT, 0, 128, kp * NHEAD * 65 + h * 65,
                                        kp * NHEAD * 65 + (h + 1) * 65),
                                     rg(EK[2 * ho + kp % 2]))],
                                   start=(kp == 0), stop=(kp == NKT - 1),
                                   tag=f"{tg}.h{h}.av{kp}")
            for ho in range(2):
                h = 2 * ct + ho
                htg = f"{tg}.h{h}"
                pav = avb[ho]
                rzap = RZ.h[64:65, 0:QB]
                gbap = GRB.h[64:65, ho * QB:(ho + 1) * QB]
                rzrg = Rg(RZ, 64, 65, 0, QB, ap=rzap)
                gbrg = Rg(GRB, 64, 65, ho * QB, (ho + 1) * QB, ap=gbap)
                P.act(rzrg, Rg(pav, 64, 65, 0, 512, ap=pav.h[64:65, 0:512]),
                      AF.Ln, tag=f"{htg}.rzl")
                P.act(rzrg, rzrg, AF.Exp, scale=-1.0, tag=f"{htg}.rze")
                P.tt("dve", gbrg, rzrg, gbrg, OP.mult, tag=f"{htg}.f")
                P.mm_group(rg(sbk[ho][0]),
                           [(Rg(ONESF, 64, 65, 0, 128, ap=ONESF.h[64:65, :]),
                             gbrg)],
                           tag=f"{htg}.fb")
                P.copy("act", rg(OC[ho]), rg(pav, ps=(0, 64)), tag=f"{htg}.oc")
                slot = h % 8
                if h < 8:
                    P.tt("dve", rg(OT2, ps=(0, 64), cs=(slot * 512, slot * 512 + 512)),
                         rg(OC[ho]), rg(sbk[ho][0], ps=(0, 64)), OP.mult,
                         tag=f"{htg}.om")
                else:
                    P.tt("dve", rg(OTMP), rg(OC[ho]), rg(sbk[ho][0], ps=(0, 64)),
                         OP.mult, tag=f"{htg}.om")
                    P.dma("sp",
                          Rg(OT2, 64, 128, slot * 512, slot * 512 + 512,
                             ap=OT2.h[64:128, slot * 512:slot * 512 + 512]),
                          rg(OTMP), tag=f"{htg}.omv")
        for co in range(NCT):
            P.mm_group(rg(pm[4]),
                       [(rg(WB2, ps=(0, 64), cs=(h * 1024 + co * 128, h * 1024 + co * 128 + 128)),
                         rg(OT2, ps=(0, 64), cs=(h * 512, (h + 1) * 512)))
                        for h in range(8)], tag=f"{tg}.wo_lo{co}")
            P.mm_group(rg(pm[5]),
                       [(rg(WB2, ps=(64, 128), cs=(h * 1024 + co * 128, h * 1024 + co * 128 + 128)),
                         rg(OT2, ps=(64, 128), cs=(h * 512, (h + 1) * 512)))
                        for h in range(8)], tag=f"{tg}.wo_hi{co}")
            P.stt("dve", rg(X[co], cs=(q0, q0 + QB)), rg(pm[4]),
                  rg(BO, cs=(co, co + 1)), rg(X[co], cs=(q0, q0 + QB)),
                  OP.add, OP.add, tag=f"{tg}.res1_{co}")
            P.tt("dve", rg(X[co], cs=(q0, q0 + QB)),
                 rg(X[co], cs=(q0, q0 + QB)), rg(pm[5]), OP.add,
                 tag=f"{tg}.res2_{co}")
        _post_ln(P, env, sc,
                 lambda t, c0, c1: rg(X[t], cs=(q0 + c0, q0 + c1)),
                 QB, (q0, q0 + QB), f"{tg}.ln")



def _l1_ffn_host(inp):
    g = lambda k: np.asarray(inp[k], np.float32)
    h = {}
    h["l1f_wq"] = g("L1_ffn_wq")
    h["l1f_wk"] = g("L1_ffn_wk")
    h["l1f_wv"] = g("L1_ffn_wv")
    wo = g("L1_ffn_wo").reshape(16, 64, 1024)
    h["l1f_wo2"] = np.ascontiguousarray(
        np.concatenate([wo[:8].transpose(1, 0, 2),
                        wo[8:].transpose(1, 0, 2)], axis=0).reshape(128, 8 * 1024))
    h["l1f_rw"] = np.ascontiguousarray(g("L1_ffn_router_w"))
    h["l1f_fc1"] = np.ascontiguousarray(g("L1_ffn_exp_fc1").reshape(8 * 1024, 2048))
    h["l1f_fc2"] = np.ascontiguousarray(g("L1_ffn_exp_fc2").reshape(8 * 2048, 1024))
    cm = np.zeros((128, 128), np.float32)
    for k in range(128):
        cm[k, :k] = -1e30
    h["cmask_f"] = cm
    return h


def _rms_qb(P, env, src_of, RB, istds, w, eps_t, pms, sq, tag, rb_w=None):
    pmS, pmB = pms
    if rb_w is None:
        rb_w = w
    for c0 in range(0, w, 512):
        wb = min(512, w - c0)
        for t in range(NCT):
            P.act(rg(sq[t % 2], cs=(0, wb)), src_of(t, c0, c0 + wb), AF.Square,
                  tag=f"{tag}.sq{t}_{c0}")
            P.mm_group(rg(pmS, ps=(0, 1), cs=(0, wb)),
                       [(rg(env["ones_col_b"]), rg(sq[t % 2], cs=(0, wb)))],
                       start=(t == 0), stop=(t == NCT - 1), tag=f"{tag}.ss{t}_{c0}")
        P.ts("dve", rg(istds, cs=(c0, c0 + wb)), rg(pmS, ps=(0, 1), cs=(0, wb)),
             1.0 / D, OP.mult, tag=f"{tag}.v{c0}")
        P.act(rg(istds, cs=(c0, c0 + wb)), rg(istds, cs=(c0, c0 + wb)), AF.Ln,
              bias=eps_t, tag=f"{tag}.l{c0}")
        P.act(rg(istds, cs=(c0, c0 + wb)), rg(istds, cs=(c0, c0 + wb)), AF.Exp,
              scale=-0.5, tag=f"{tag}.e{c0}")
        P.mm_group(rg(pmB, cs=(0, wb)),
                   [(rg(env["ones_row_f"]), rg(istds, cs=(c0, c0 + wb)))],
                   tag=f"{tag}.b{c0}")
        for t in range(NCT):
            P.tt("dve", rg(RB, cs=(t * rb_w + c0, t * rb_w + c0 + wb)),
                 src_of(t, c0, c0 + wb),
                 rg(pmB, cs=(0, wb)), OP.mult, tag=f"{tag}.a{t}_{c0}")


def emit_l1_ffn(P, ctx, env):
    X = env["X"]
    QB = 512
    NQ = TOK // QB
    NKT = TOK // 128

    d = {}
    for nm, shp in (("wq", [D, D]), ("wk", [D, D]), ("wv", [D, D]),
                    ("wo2", [128, 8 * 1024]), ("rw", [D, 8]),
                    ("fc1", [8 * 1024, 2048]), ("fc2", [8 * 2048, 1024]),
                    ("cmask", [128, 128])):
        key = "cmask_f" if nm == "cmask" else f"l1f_{nm}"
        d[nm] = P.dram_in(key, shp)

    Xb = P.sbuf(ctx, "n_Xb", 128, NCT * TOK, BF16)
    for t in range(NCT):
        P.copy("dve", Rg(Xb, 0, 128, t * TOK, (t + 1) * TOK,
                         ap=Xb.h[:, t * TOK:(t + 1) * TOK]),
               rg(X[t]), tag=f"n.xb{t}")
    CM = P.sbuf(ctx, "n_CM", 128, 128, F32)
    P.dma("sp", rg(CM), rg(d["cmask"]))
    EPSR = P.sbuf(ctx, "n_EPSR", 1, 1, F32)
    P.memset("dve", rg(EPSR), 1e-5, tag="n.eps")

    with ExitStack() as atx:
        KB = P.sbuf(atx, "n_KB", 128, NCT * TOK, BF16)
        VT = P.sbuf(atx, "n_VT", 128, NKT * NHEAD * 65, BF16)
        WBUF = P.sbuf(atx, "n_WB", 128, 8 * 1024, BF16)
        WO2x = WBUF
        RB = P.sbuf(atx, "n_RB", 128, NCT * QB, BF16)
        EK = [P.sbuf(atx, f"n_EK{i}", 128, 512, BF16) for i in range(4)]
        OT2 = RB
        OC = [P.sbuf(atx, f"n_OC{i}", 64, 512, BF16) for i in range(2)]
        RZ = P.sbuf(atx, "n_RZ", 65, 2 * 512, F32)
        ONESF = P.sbuf(atx, "n_ONES", 128, 128, F32)
        QH8 = P.sbuf(atx, "n_QH8", 128, 8 * 512, BF16)
        SQ = [EK[0], EK[1]]
        OTMP = Tsr(EK[2].h, EK[2].name, 64, 512, BF16, "sbuf")
        ISTD = Tsr(RZ.h, RZ.name, 1, QB, F32, "sbuf")
        P.memset("dve", rg(ONESF), 1.0, tag="n.ones")
        pm = [P.psum(atx, f"n_pm{i}", 128, 512, F32) for i in range(8)]

        _load_rearr(P, WBUF, d["wk"])
        for qb in range(NQ):
            q0 = qb * QB
            tg = f"nk{qb}"
            _rms_qb(P, env,
                    (lambda t, c0, c1, q0=q0: Rg(Xb, 0, 128, t * TOK + q0 + c0,
                     t * TOK + q0 + c1, ap=Xb.h[:, t * TOK + q0 + c0:t * TOK + q0 + c1])),
                    RB, ISTD, QB, rg(EPSR), (pm[6], pm[7]), SQ, tg)
            for co in range(NCT):
                bank = pm[4 + (co % 2)]
                P.mm_group(rg(bank),
                           [(rg(WBUF, cs=(t * 1024 + co * 128, t * 1024 + co * 128 + 128)),
                             rg(RB, cs=(t * QB, (t + 1) * QB))) for t in range(NCT)],
                           tag=f"{tg}.k{co}")
                P.copy("dve", Rg(KB, 0, 128, co * TOK + q0, co * TOK + q0 + QB,
                                 ap=KB.h[:, co * TOK + q0:co * TOK + q0 + QB]),
                       rg(bank), tag=f"{tg}.ke{co}")
        _load_rearr(P, WBUF, d["wv"])
        for qb in range(NQ):
            q0 = qb * QB
            tg = f"nv{qb}"
            _rms_qb(P, env,
                    (lambda t, c0, c1, q0=q0: Rg(Xb, 0, 128, t * TOK + q0 + c0,
                     t * TOK + q0 + c1, ap=Xb.h[:, t * TOK + q0 + c0:t * TOK + q0 + c1])),
                    RB, ISTD, QB, rg(EPSR), (pm[6], pm[7]), SQ, tg)
            for tt in range(QB // 128):
                gt = qb * 4 + tt
                for nb in range(2):
                    bank = pm[4 + (nb % 2)]
                    P.mm_group(rg(bank),
                               [(rg(RB, cs=(t * QB + tt * 128, t * QB + tt * 128 + 128)),
                                 rg(WBUF, cs=(t * 1024 + nb * 512, t * 1024 + nb * 512 + 512)))
                                for t in range(NCT)], tag=f"{tg}.v{tt}{nb}")
                    vap = VT.h[:, gt * NHEAD * 65:(gt + 1) * NHEAD * 65] \
                        .rearrange("p (h c) -> p h c", h=NHEAD)[:, nb * 8:(nb + 1) * 8, 0:64]
                    P.copy("dve",
                           Rg(VT, 0, 128, gt * NHEAD * 65, (gt + 1) * NHEAD * 65, ap=vap),
                           rg(bank), tag=f"{tg}.ve{tt}{nb}")
                onesap = VT.h[:, gt * NHEAD * 65:(gt + 1) * NHEAD * 65] \
                    .rearrange("p (h c) -> p h c", h=NHEAD)[:, :, 64:65]
                P.memset("dve",
                         Rg(VT, 0, 128, gt * NHEAD * 65, (gt + 1) * NHEAD * 65, ap=onesap),
                         1.0, tag=f"{tg}.vo{tt}")

        for qb in range(NQ):
            q0 = qb * QB
            tg = f"nq{qb}"
            _load_rearr(P, WBUF, d["wq"])
            _rms_qb(P, env,
                    (lambda t, c0, c1, q0=q0: Rg(Xb, 0, 128, t * TOK + q0 + c0,
                     t * TOK + q0 + c1, ap=Xb.h[:, t * TOK + q0 + c0:t * TOK + q0 + c1])),
                    RB, ISTD, QB, rg(EPSR), (pm[6], pm[7]), SQ, tg)
            nkt = 4 * (qb + 1)
            for ct in range(NCT):
                P.mm_group(rg(pm[4 + ct % 2]),
                           [(rg(WBUF, cs=(t * 1024 + ct * 128, t * 1024 + ct * 128 + 128)),
                             rg(RB, cs=(t * QB, (t + 1) * QB))) for t in range(NCT)],
                           tag=f"{tg}.q{ct}")
                P.copy("dve", rg(QH8, cs=(ct * 512, (ct + 1) * 512)),
                       rg(pm[4 + ct % 2]), tag=f"{tg}.qe{ct}")
            _load_rearr(P, WO2x, d["wo2"], direct=True)
            for ct in range(NCT):
                avb = {0: pm[6], 1: pm[7]}
                sbk = {0: (pm[0], pm[1]), 1: (pm[2], pm[3])}
                for kt in range(nkt + 1):
                    if kt < nkt:
                        dj = kt - 4 * qb
                        for ho in range(2):
                            h = 2 * ct + ho
                            hp = 64 * ho
                            P.mm_group(rg(sbk[ho][kt % 2]),
                                       [(rg(KB, ps=(hp, hp + 64),
                                            cs=(ct * TOK + kt * 128, ct * TOK + kt * 128 + 128)),
                                         rg(QH8, ps=(hp, hp + 64),
                                            cs=(ct * 512, (ct + 1) * 512)))],
                                       tag=f"{tg}.h{h}.s{kt}")
                        for ho in range(2):
                            htg = f"{tg}.h{2 * ct + ho}"
                            sb = sbk[ho][kt % 2]
                            ek = EK[2 * ho + kt % 2]
                            if dj >= 0:
                                P.tt("dve",
                                     rg(sb, cs=(dj * 128, dj * 128 + 128)),
                                     rg(sb, cs=(dj * 128, dj * 128 + 128)),
                                     rg(CM), OP.add, tag=f"{htg}.cm{kt}")
                                if dj > 0:
                                    P.memset("dve", rg(ek, cs=(0, dj * 128)),
                                             0.0, tag=f"{htg}.z{kt}")
                                P.act(rg(ek, cs=(dj * 128, 512)),
                                      rg(sb, cs=(dj * 128, 512)), AF.Exp,
                                      scale=0.125, tag=f"{htg}.e{kt}")
                            else:
                                P.act(rg(ek), rg(sb), AF.Exp, scale=0.125,
                                      tag=f"{htg}.e{kt}")
                    if kt >= 1:
                        kp = kt - 1
                        for ho in range(2):
                            h = 2 * ct + ho
                            P.mm_group(rg(avb[ho], ps=(0, 65)),
                                       [(Rg(VT, 0, 128, kp * NHEAD * 65 + h * 65,
                                            kp * NHEAD * 65 + (h + 1) * 65),
                                         rg(EK[2 * ho + kp % 2]))],
                                       start=(kp == 0), stop=(kp == nkt - 1),
                                       tag=f"{tg}.h{h}.av{kp}")
                for ho in range(2):
                    h = 2 * ct + ho
                    htg = f"{tg}.h{h}"
                    pav = avb[ho]
                    rzap = RZ.h[64:65, ho * QB:(ho + 1) * QB]
                    rzrg = Rg(RZ, 64, 65, ho * QB, (ho + 1) * QB, ap=rzap)
                    P.act(rzrg, Rg(pav, 64, 65, 0, 512, ap=pav.h[64:65, 0:512]),
                          AF.Ln, tag=f"{htg}.rzl")
                    P.act(rzrg, rzrg, AF.Exp, scale=-1.0, tag=f"{htg}.rze")
                    P.mm_group(rg(sbk[ho][0]),
                               [(Rg(ONESF, 64, 65, 0, 128, ap=ONESF.h[64:65, :]),
                                 rzrg)],
                               tag=f"{htg}.fb")
                    P.copy("act", rg(OC[ho]), rg(pav, ps=(0, 64)), tag=f"{htg}.oc")
                    slot = h % 8
                    if h < 8:
                        P.tt("dve", rg(OT2, ps=(0, 64), cs=(slot * 512, slot * 512 + 512)),
                             rg(OC[ho]), rg(sbk[ho][0], ps=(0, 64)), OP.mult,
                             tag=f"{htg}.om")
                    else:
                        P.tt("dve", rg(OTMP), rg(OC[ho]), rg(sbk[ho][0], ps=(0, 64)),
                             OP.mult, tag=f"{htg}.om")
                        P.dma("sp",
                              Rg(OT2, 64, 128, slot * 512, slot * 512 + 512,
                                 ap=OT2.h[64:128, slot * 512:slot * 512 + 512]),
                              rg(OTMP), tag=f"{htg}.omv")
            for co in range(NCT):
                P.mm_group(rg(pm[4]),
                           [(rg(WO2x, ps=(0, 64), cs=(h * 1024 + co * 128, h * 1024 + co * 128 + 128)),
                             rg(OT2, ps=(0, 64), cs=(h * 512, (h + 1) * 512)))
                            for h in range(8)], tag=f"{tg}.wo_lo{co}")
                P.mm_group(rg(pm[5]),
                           [(rg(WO2x, ps=(64, 128), cs=(h * 1024 + co * 128, h * 1024 + co * 128 + 128)),
                             rg(OT2, ps=(64, 128), cs=(h * 512, (h + 1) * 512)))
                            for h in range(8)], tag=f"{tg}.wo_hi{co}")
                P.tt("dve", rg(X[co], cs=(q0, q0 + QB)),
                     rg(X[co], cs=(q0, q0 + QB)), rg(pm[4]), OP.add,
                     tag=f"{tg}.r1_{co}")
                P.tt("dve", rg(X[co], cs=(q0, q0 + QB)),
                     rg(X[co], cs=(q0, q0 + QB)), rg(pm[5]), OP.add,
                     tag=f"{tg}.r2_{co}")

    P.barrier(tag="n_moebar")

    with ExitStack() as mtx:
        RW = P.sbuf(mtx, "n_RW", 128, 8 * 8, BF16)
        _load_rearr(P, RW, d["rw"])
        F1 = [P.sbuf(mtx, f"n_F1{i}", 128, 8 * 1024, BF16) for i in range(2)]
        F2 = [P.sbuf(mtx, f"n_F2{i}", 128, 8 * 1024, BF16) for i in range(2)]
        RB = P.sbuf(mtx, "n_RB2", 128, NCT * QB, BF16)
        ISTD = P.sbuf(mtx, "n_IST2", 1, QB, F32)
        SQ = [P.sbuf(mtx, f"n_SQ2{i}", 128, 512, BF16) for i in range(2)]
        H1 = P.sbuf(mtx, "n_H1", 128, 16 * QB, BF16)
        HS = P.sbuf(mtx, "n_HS", 128, 512, BF16)
        HT = P.sbuf(mtx, "n_HT", 128, 512, BF16)
        LGT = P.sbuf(mtx, "n_LG", 128, 4 * 8, F32)
        MXT = P.sbuf(mtx, "n_MX", 128, 1, F32)
        EQT = P.sbuf(mtx, "n_EQ", 128, 8, F32)
        MSK = P.sbuf(mtx, "n_MS", 128, 4 * 8, F32)
        SS2 = P.sbuf(mtx, "n_SS", 128, 1, F32)
        CWE = P.sbuf(mtx, "n_CWE", 8, QB, F32)
        CWR = [P.sbuf(mtx, f"n_CWR{i}", 1, QB, F32) for i in range(2)]
        CWB = P.sbuf(mtx, "n_CWB", 128, QB, BF16)
        pm = [P.psum(mtx, f"n2_pm{i}", 128, 512, F32) for i in range(8)]
        sc = {
            "R1B": (lambda t, c0, c1: rg(RB, cs=(t * QB + c0, t * QB + c1))),
            "R1S": (lambda t, c0, c1: rg(H1, cs=(8 * QB + t * QB + c0,
                                                 8 * QB + t * QB + c1))),
            "LT1": ISTD,
            "LT2": P.sbuf(mtx, "n_LT2", 1, QB, F32),
            "LT3": P.sbuf(mtx, "n_LT3", 1, QB, F32),
            "EPSLN": EPSR,
            "pmL1": pm[0], "pmL2": pm[1], "pmL3": pm[2], "pmL4": pm[3],
        }
        for qb in range(NQ):
            q0 = qb * QB
            tg = f"nm{qb}"
            _rms_qb(P, env,
                    (lambda t, c0, c1, q0=q0: rg(X[t], cs=(q0 + c0, q0 + c1))),
                    RB, ISTD, QB, rg(EPSR), (pm[6], pm[7]), SQ, tg)
            for tt in range(QB // 128):
                P.mm_group(rg(pm[6], cs=(0, 8)),
                           [(rg(RB, cs=(t * QB + tt * 128, t * QB + tt * 128 + 128)),
                             rg(RW, cs=(t * 8, (t + 1) * 8))) for t in range(NCT)],
                           tag=f"{tg}.lg{tt}")
                lcs = (tt * 8, (tt + 1) * 8)
                def _mx(e, pm6=pm[6]):
                    return e.reduce_max(MXT.h[:, 0:1], pm6.h[:, 0:8],
                                        mybir.AxisListType.X, negate=True)
                P.op("dve", _mx, reads=[rg(pm[6])], writes=[rg(MXT)],
                     tag=f"{tg}.mx{tt}")
                P.act(rg(LGT, cs=lcs), rg(pm[6], cs=(0, 8)), AF.Exp,
                      bias=rg(MXT), tag=f"{tg}.ex{tt}")
                P.memset("dve", rg(MSK, cs=lcs), 0.0, tag=f"{tg}.m0{tt}")
                P.copy("dve", rg(EQT), rg(LGT, cs=lcs), tag=f"{tg}.cp{tt}")
                for it in range(2):
                    P.reduce(rg(MXT), rg(EQT), OP.max, tag=f"{tg}.tm{tt}_{it}")
                    P.ts("dve", rg(SQ[0], ps=(0, 128), cs=(0, 8)), rg(EQT),
                         rg(MXT), OP.is_ge, tag=f"{tg}.te{tt}_{it}")
                    P.tt("dve", rg(MSK, cs=lcs), rg(MSK, cs=lcs),
                         rg(SQ[0], ps=(0, 128), cs=(0, 8)), OP.add,
                         tag=f"{tg}.ta{tt}_{it}")
                    P.stt("dve", rg(EQT), rg(SQ[0], ps=(0, 128), cs=(0, 8)),
                          -1e30, rg(EQT), OP.mult, OP.add, tag=f"{tg}.ts{tt}_{it}")
                P.tt("dve", rg(LGT, cs=lcs), rg(LGT, cs=lcs), rg(MSK, cs=lcs),
                     OP.mult, tag=f"{tg}.wm{tt}")
                P.reduce(rg(SS2), rg(LGT, cs=lcs), OP.add, tag=f"{tg}.sw{tt}")
                P.recip(rg(SS2), rg(SS2), tag=f"{tg}.sr{tt}")
                P.ts("dve", rg(LGT, cs=lcs), rg(LGT, cs=lcs), rg(SS2), OP.mult,
                     tag=f"{tg}.cn{tt}")
                P.transpose(rg(pm[7], ps=(0, 8), cs=(0, 128)),
                            rg(LGT, cs=lcs), rg(env["identity_f"]),
                            tag=f"{tg}.tr{tt}")
                P.copy("dve", rg(CWE, ps=(0, 8), cs=(tt * 128, (tt + 1) * 128)),
                       rg(pm[7], ps=(0, 8), cs=(0, 128)), tag=f"{tg}.tre{tt}")
            P.dma("pool", rg(F1[0], cs=(0, 8 * 1024)),
                  Rg(d["fc1"], 0, 1024, 0, 1024,
                     ap=d["fc1"].h[0:1024, 0:1024]
                     .rearrange("(r p) f -> p r f", p=128)),
                  tag=f"{tg}.l10")
            for fhn in range(2):
                P.dma("pool", rg(F2[fhn], cs=(0, 8 * 1024)),
                      Rg(d["fc2"], fhn * 1024, (fhn + 1) * 1024, 0, 1024,
                         ap=d["fc2"].h[fhn * 1024:(fhn + 1) * 1024, :]
                         .rearrange("(r p) f -> p r f", p=128)),
                      tag=f"{tg}.l20_{fhn}")
            for e in range(8):
                etg = f"{tg}.e{e}"
                P.dma("sp", rg(CWR[e % 2]),
                      Rg(CWE, e, e + 1, 0, QB, ap=CWE.h[e:e + 1, :]),
                      tag=f"{etg}.cwx")
                P.mm_group(rg(pm[7]),
                           [(rg(env["ones_row_f"]), rg(CWR[e % 2]))],
                           tag=f"{etg}.cb")
                P.copy("dve", rg(CWB), rg(pm[7]), tag=f"{etg}.cbe")
                for fh in range(2):
                    par = (2 * e + fh) % 2
                    f1 = F1[par]
                    for ft in range(8):
                        bank = pm[4 + (ft % 2)]
                        P.mm_group(rg(bank),
                                   [(rg(f1, cs=(t * 1024 + ft * 128, t * 1024 + ft * 128 + 128)),
                                     rg(RB, cs=(t * QB, (t + 1) * QB)))
                                    for t in range(NCT)], tag=f"{etg}.up{fh}{ft}")
                        P.act(rg(HS), rg(bank), AF.Relu, tag=f"{etg}.rl{fh}{ft}")
                        P.tt("dve", rg(HT), rg(HS), rg(CWB), OP.mult,
                             tag=f"{etg}.s1{fh}{ft}")
                        P.tt("dve", rg(H1, cs=((fh * 8 + ft) * QB, (fh * 8 + ft + 1) * QB)),
                             rg(HT), rg(HS), OP.mult, tag=f"{etg}.s2{fh}{ft}")
                    nxt = 2 * e + fh + 1
                    if nxt < 16:
                        en, fhn = nxt // 2, nxt % 2
                        P.dma("pool", rg(F1[nxt % 2], cs=(0, 8 * 1024)),
                              Rg(d["fc1"], en * 1024, (en + 1) * 1024,
                                 fhn * 1024, (fhn + 1) * 1024,
                                 ap=d["fc1"].h[en * 1024:(en + 1) * 1024,
                                               fhn * 1024:(fhn + 1) * 1024]
                                 .rearrange("(r p) f -> p r f", p=128)),
                              tag=f"{etg}.l1n{fh}")
                for dth in range(2):
                    for dq in range(4):
                        dt = dth * 4 + dq
                        bank = pm[dq]
                        P.mm_group(rg(bank),
                                   [(rg(F2[fh],
                                        cs=(ft * 1024 + dt * 128, ft * 1024 + dt * 128 + 128)),
                                     rg(H1, cs=((fh * 8 + ft) * QB, (fh * 8 + ft + 1) * QB)))
                                    for fh in range(2) for ft in range(8)],
                                   tag=f"{etg}.dn{dt}")
                        P.tt("dve", rg(X[dt], cs=(q0, q0 + QB)),
                             rg(X[dt], cs=(q0, q0 + QB)), rg(bank), OP.add,
                             tag=f"{etg}.ac{dt}")
                if e + 1 < 8:
                    en = e + 1
                    for fhn in range(2):
                        P.dma("pool", rg(F2[fhn], cs=(0, 8 * 1024)),
                              Rg(d["fc2"], en * 2048 + fhn * 1024,
                                 en * 2048 + (fhn + 1) * 1024, 0, 1024,
                                 ap=d["fc2"].h[en * 2048 + fhn * 1024:
                                               en * 2048 + (fhn + 1) * 1024, :]
                                 .rearrange("(r p) f -> p r f", p=128)),
                              tag=f"{etg}.l2n{fhn}")
            for t in range(NCT):
                P.tt("dve", rg(X[t], cs=(q0, q0 + QB)),
                     rg(X[t], cs=(q0, q0 + QB)),
                     Rg(Xb, 0, 128, t * TOK + q0, t * TOK + q0 + QB,
                        ap=Xb.h[:, t * TOK + q0:t * TOK + q0 + QB]),
                     OP.add, tag=f"{tg}.xr{t}")
            _post_ln(P, env, sc,
                     (lambda t, c0, c1, q0=q0: rg(X[t], cs=(q0 + c0, q0 + c1))),
                     QB, (q0, q0 + QB), f"{tg}.ln")


# version 18
# speedup vs baseline: 1.2253x; 1.2253x over previous
import numpy as np
import ml_dtypes
from contextlib import ExitStack

import concourse.bass as bass
import concourse.mybir as mybir
from concourse.bass_utils import run_bass_kernel_spmd
from concourse.alu_op_type import AluOpType as OP

F32 = mybir.dt.float32
BF16 = mybir.dt.bfloat16
AF = mybir.ActivationFunctionType

D = 1024
TOK = 2048
NCT = 8
NHEAD = 16
HD = 64
CHUNK = 128
SELF_SYNC = True

ENGINES = ("pe", "dve", "act", "pool", "sp")
DMA_QUEUES = ("sp", "pool", "act")
DMA_SLOTS = 6


class Tsr:
    __slots__ = ("h", "name", "parts", "cols", "dtype", "space")

    def __init__(self, h, name, parts, cols, dtype, space):
        self.h, self.name = h, name
        self.parts, self.cols = parts, cols
        self.dtype, self.space = dtype, space

    def __getitem__(self, idx):
        return self.h[idx]


class Rg:
    __slots__ = ("t", "p0", "p1", "c0", "c1", "ap")

    def __init__(self, t, p0=None, p1=None, c0=None, c1=None, ap=None):
        self.t = t
        self.p0 = 0 if p0 is None else p0
        self.p1 = t.parts if p1 is None else p1
        self.c0 = 0 if c0 is None else c0
        self.c1 = t.cols if c1 is None else c1
        if ap is None and t.space != "dram":
            ap = t.h[self.p0:self.p1, self.c0:self.c1]
        elif ap is None:
            ap = t.h
        self.ap = ap


def rg(t, ps=None, cs=None, ap=None):
    p0, p1 = (ps if ps is not None else (None, None))
    c0, c1 = (cs if cs is not None else (None, None))
    return Rg(t, p0, p1, c0, c1, ap)


def overlap(a, b):
    return not (a[1] <= b[0] or b[1] <= a[0])


def queue_engine(q):
    return {"sp": "sp", "pool": "pool", "act": "act"}[q]


class Prog:
    def __init__(self, nc):
        self.nc = nc
        self.ops = {e: [] for e in ENGINES}
        self.acc = {}
        self.slot_use = {(q, s): 0 for q in DMA_QUEUES for s in range(DMA_SLOTS)}
        self.slot_rr = {q: 0 for q in DMA_QUEUES}
        self.sbuf_bytes = 0
        self.n_inst = 0

    def sbuf(self, ctx, name, parts, cols, dtype):
        h = ctx.enter_context(self.nc.sbuf_tensor(name, [parts, cols], dtype))
        self.sbuf_bytes += cols * (4 if dtype == F32 else 2)
        return Tsr(h, name, parts, cols, dtype, "sbuf")

    def psum(self, ctx, name, parts, cols, dtype):
        h = ctx.enter_context(self.nc.psum_tensor(name, [parts, cols], dtype))
        return Tsr(h, name, parts, cols, dtype, "psum")

    def dram_in(self, name, shape, dtype=F32):
        ap = self.nc.dram_tensor(name, list(shape), dtype, kind="ExternalInput").ap()
        cols = int(np.prod(shape[1:])) if len(shape) > 1 else 1
        return Tsr(ap, name, shape[0], cols, dtype, "dram")

    def dram_out(self, name, shape, dtype=F32):
        ap = self.nc.dram_tensor(name, list(shape), dtype, kind="ExternalOutput").ap()
        cols = int(np.prod(shape[1:])) if len(shape) > 1 else 1
        return Tsr(ap, name, shape[0], cols, dtype, "dram")

    def _reg(self, r):
        if r.t.space == "psum":
            return (0, r.t.parts, 0, r.t.cols)
        return (r.p0, r.p1, r.c0, r.c1)

    @staticmethod
    def _ov(a, b):
        return overlap((a[0], a[1]), (b[0], b[1])) and overlap((a[2], a[3]), (b[2], b[3]))

    @staticmethod
    def _covers(a, b):
        return a[0] <= b[0] and a[1] >= b[1] and a[2] <= b[2] and a[3] >= b[3]

    def _touch(self, owner, reads, writes):
        deps = set()
        for r in reads:
            rec = self.acc.setdefault(r.t.name, {"w": [], "r": []})
            reg = self._reg(r)
            for (wreg, wo) in rec["w"]:
                if wo != owner and self._ov(reg, wreg):
                    deps.add(wo)
            rec["r"].append((reg, owner))
        for r in writes:
            rec = self.acc.setdefault(r.t.name, {"w": [], "r": []})
            reg = self._reg(r)
            for (wreg, wo) in rec["w"]:
                if wo != owner and self._ov(reg, wreg):
                    deps.add(wo)
            keep_r = []
            for (rreg, ro) in rec["r"]:
                if self._ov(reg, rreg):
                    if ro != owner:
                        deps.add(ro)
                    if not self._covers(reg, rreg):
                        keep_r.append((rreg, ro))
                else:
                    keep_r.append((rreg, ro))
            rec["r"] = keep_r
            rec["w"] = [wr for wr in rec["w"] if not self._covers(reg, wr[0])]
            rec["w"].append((reg, owner))
        return deps

    def _record(self, eng, rec, deps):
        for dep in deps:
            if dep[0] == "E":
                _, peng, pidx = dep
                if peng == eng and not SELF_SYNC and rec["dma"] is None:
                    continue
                if peng == "pe" and eng == "pe" and rec["dma"] is None:
                    continue
                self.ops[peng][pidx]["inc"] = True
                rec["edges"].add((peng, pidx))
            else:
                rec["dedges"].add(dep[1])
        self.ops[eng].append(rec)

    def op(self, eng, fn, reads=(), writes=(), tag="", n_inst=1):
        idx = len(self.ops[eng])
        rec = {"fn": fn, "edges": set(), "dedges": set(), "inc": False,
               "tag": tag, "dma": None}
        deps = self._touch(("E", eng, idx), reads, writes)
        self._record(eng, rec, deps)
        self.n_inst += n_inst
        return (eng, idx)

    def dma(self, queue, out, in_, tag=""):
        eng = queue_engine(queue)
        slot = self.slot_rr[queue]
        self.slot_rr[queue] = (slot + 1) % DMA_SLOTS
        prev = self.slot_use[(queue, slot)]
        self.slot_use[(queue, slot)] = prev + 1
        val = 16 * (prev + 1)
        rec = {"fn": None, "edges": set(), "dedges": set(), "inc": False,
               "tag": tag, "dma": (queue, slot, val, out, in_)}
        deps = self._touch(("D", (queue, slot, val)), [in_], [out])
        if prev > 0:
            rec["dedges"].add((queue, slot, 16 * prev))
        self._record(eng, rec, deps)
        self.n_inst += 1

    def barrier(self, tag="bar"):
        lasts = {}
        for e in ("pe", "dve", "act", "pool"):
            for i in range(len(self.ops[e]) - 1, -1, -1):
                o = self.ops[e][i]
                if o["dma"] is None:
                    if not o.get("bar"):
                        lasts[e] = i
                    break
        dwaits = {(q, s, 16 * u) for (q, s), u in self.slot_use.items() if u > 0}
        for e in ENGINES:
            rec = {"fn": (lambda eng: None), "edges": set(), "dedges": set(dwaits),
                   "inc": False, "tag": tag, "dma": None, "bar": True}
            for pe_, pidx in lasts.items():
                if pe_ == e:
                    continue
                self.ops[pe_][pidx]["inc"] = True
                rec["edges"].add((pe_, pidx))
            self.ops[e].append(rec)

    def emit(self, ctx):
        nc = self.nc
        esem = {e: ctx.enter_context(nc.semaphore(f"sem_{e}"))
                for e in ("pe", "dve", "act", "pool", "sp")}
        dsem = {}
        for q in DMA_QUEUES:
            for s in range(DMA_SLOTS):
                if self.slot_use[(q, s)] > 0:
                    dsem[(q, s)] = ctx.enter_context(nc.semaphore(f"dma_{q}{s}"))
        rank = {}
        for e in ENGINES:
            c = 0
            for i, op_ in enumerate(self.ops[e]):
                if op_["inc"]:
                    c += 1
                    rank[(e, i)] = c
        block = ctx.enter_context(nc.Block())
        prog = self

        def make_body(ename, tail=False):
            def body(eng):
                waited = {}
                for op_ in prog.ops[ename]:
                    for (peng, pidx) in sorted(op_["edges"]):
                        v = rank[(peng, pidx)]
                        if waited.get(("e", peng), 0) < v:
                            eng.wait_ge(esem[peng], v)
                            waited[("e", peng)] = v
                    for (q, s, v) in sorted(op_["dedges"]):
                        if waited.get(("d", q, s), 0) < v:
                            eng.wait_ge(dsem[(q, s)], v)
                            waited[("d", q, s)] = v
                    if op_["dma"] is not None:
                        q, s, v, out, in_ = op_["dma"]
                        eng.dma_start(out=out.ap, in_=in_.ap).then_inc(dsem[(q, s)], 16)
                    else:
                        inst = op_["fn"](eng)
                        if op_["inc"]:
                            assert inst is not None, f"inc w/o inst: {op_['tag']}"
                            inst.then_inc(esem[ename], 1)
                if tail:
                    for (q, s), uses in sorted(prog.slot_use.items()):
                        if uses > 0:
                            tv = 16 * uses
                            if waited.get(("d", q, s), 0) < tv:
                                eng.wait_ge(dsem[(q, s)], tv)
            return body

        block.tensor(make_body("pe"))
        block.vector(make_body("dve"))
        block.scalar(make_body("act"))
        block.gpsimd(make_body("pool"))
        block.sync(make_body("sp", tail=True))

    def mm_group(self, out, terms, tag="", tp=None, start=True, stop=True):
        base0 = terms[0][0].p0
        for (l, r) in terms:
            assert l.p0 == r.p0, f"mm operand base mismatch {tag}: {l.p0} vs {r.p0}"
            assert l.p0 == base0, f"mixed-base psum accumulation {tag}"
            assert l.p1 - l.p0 == r.p1 - r.p0, f"mm K mismatch {tag}"

        def fn(eng, out=out, terms=terms, tp=tp, start=start, stop=stop):
            inst = None
            n = len(terms)
            for i, (l, r) in enumerate(terms):
                inst = eng.matmul(out.ap, l.ap, r.ap,
                                  start=(start and i == 0),
                                  stop=(stop and i == n - 1), tile_position=tp)
            return inst

        reads = [x for lr in terms for x in lr]
        return self.op("pe", fn, reads=reads, writes=[out], tag=tag,
                       n_inst=len(terms))

    def transpose(self, out, in_, ident, tag="", tp=None):
        def fn(eng, out=out, in_=in_, ident=ident, tp=tp):
            return eng.transpose(out.ap, in_.ap, ident.ap, tile_position=tp)
        return self.op("pe", fn, reads=[in_, ident], writes=[out], tag=tag)

    def tt(self, eng, out, in0, in1, op, tag=""):
        def fn(e, out=out, in0=in0, in1=in1, op=op):
            return e.tensor_tensor(out.ap, in0.ap, in1.ap, op)
        return self.op(eng, fn, reads=[in0, in1], writes=[out], tag=tag)

    def ts(self, eng, out, in0, s1, op0, s2=None, op1=None, tag=""):
        reads = [in0] + [s for s in (s1, s2) if isinstance(s, Rg)]
        a1 = s1.ap if isinstance(s1, Rg) else s1
        a2 = s2.ap if isinstance(s2, Rg) else s2

        def fn(e, out=out, in0=in0, a1=a1, a2=a2, op0=op0, op1=op1):
            if op1 is None:
                return e.tensor_scalar(out.ap, in0.ap, a1, None, op0)
            return e.tensor_scalar(out.ap, in0.ap, a1, a2, op0, op1)
        return self.op(eng, fn, reads=reads, writes=[out], tag=tag)

    def stt(self, eng, out, in0, scalar, in1, op0, op1, tag=""):
        reads = [in0, in1] + ([scalar] if isinstance(scalar, Rg) else [])
        sap = scalar.ap if isinstance(scalar, Rg) else scalar

        def fn(e, out=out, in0=in0, sap=sap, in1=in1, op0=op0, op1=op1):
            return e.scalar_tensor_tensor(out.ap, in0.ap, sap, in1.ap, op0, op1)
        return self.op(eng, fn, reads=reads, writes=[out], tag=tag)

    def act(self, out, in_, func, bias=0.0, scale=1.0, tag=""):
        reads = [in_] + [b for b in (bias,) if isinstance(b, Rg)]
        bap = bias.ap if isinstance(bias, Rg) else bias

        def fn(e, out=out, in_=in_, func=func, bap=bap, scale=scale):
            return e.activation(out.ap, in_.ap, func, bias=bap, scale=scale)
        return self.op("act", fn, reads=reads, writes=[out], tag=tag)

    def copy(self, eng, out, in_, tag=""):
        if eng == "act":
            def fn(e, out=out, in_=in_):
                return e.copy(out.ap, in_.ap)
        else:
            def fn(e, out=out, in_=in_):
                return e.tensor_copy(out.ap, in_.ap)
        return self.op(eng, fn, reads=[in_], writes=[out], tag=tag)

    def memset(self, eng, out, const, tag=""):
        def fn(e, out=out, const=const):
            return e.memset(out.ap, const)
        return self.op(eng, fn, reads=[], writes=[out], tag=tag)

    def scan(self, out, d0, d1, initial, op0, op1, tag=""):
        reads = [d0, d1] + ([initial] if isinstance(initial, Rg) else [])
        iap = initial.ap if isinstance(initial, Rg) else initial

        def fn(e, out=out, d0=d0, d1=d1, iap=iap, op0=op0, op1=op1):
            return e.tensor_tensor_scan(out.ap, d0.ap, d1.ap, iap, op0, op1)
        return self.op("dve", fn, reads=reads, writes=[out], tag=tag)

    def reduce(self, out, in_, op, negate=None, tag=""):
        def fn(e, out=out, in_=in_, op=op, negate=negate):
            return e.tensor_reduce(out.ap, in_.ap, mybir.AxisListType.X, op,
                                   negate=negate)
        return self.op("dve", fn, reads=[in_], writes=[out], tag=tag)

    def recip(self, out, in_, tag=""):
        def fn(e, out=out, in_=in_):
            return e.reciprocal(out.ap, in_.ap)
        return self.op("dve", fn, reads=[in_], writes=[out], tag=tag)



def _v128(v):
    return np.ascontiguousarray(np.asarray(v, np.float32).reshape(8, 128).T)


def _consts_host():
    bf = ml_dtypes.bfloat16
    c = {}
    c["identity_b"] = np.eye(128, dtype=np.float32).astype(bf)
    c["identity_f"] = np.eye(128, dtype=np.float32)
    c["ones_col_b"] = np.ones((128, 1), np.float32).astype(bf)
    c["ones_row_f"] = np.ones((1, 128), np.float32)
    c["headsel_b"] = np.zeros((128, 2), np.float32)
    c["headsel_b"][0:64, 0] = 1.0
    c["headsel_b"][64:128, 1] = 1.0
    c["headsel_b"] = c["headsel_b"].astype(bf)
    c["pat2_f"] = np.zeros((2, 128), np.float32)
    c["pat2_f"][0, 0:64] = 1.0
    c["pat2_f"][1, 64:128] = 1.0
    c["ut_mask_f"] = np.triu(np.ones((CHUNK, CHUNK), np.float32), 1)
    c["dg_mask_f"] = np.eye(CHUNK, dtype=np.float32)
    sm = np.ones((128, 256), np.float32)
    sm[:, 0::CHUNK] = 0.0
    c["scan_mask_f"] = sm
    return c


def _l0_attn_host(inp):
    g = lambda k: np.asarray(inp[k], np.float32)
    h = {}
    for nm in ("maa_x", "maa_w", "maa_k", "maa_v", "maa_r", "maa_g",
               "time_decay", "ln_g", "ln_b"):
        h[f"l0a_{nm}"] = _v128(g(f"L0_attn_{nm}"))
    h["l0a_u"] = _v128(g("L0_attn_u").reshape(-1))
    h["l0a_maa_w1"] = g("L0_attn_maa_w1")
    h["l0a_maa_w2"] = np.ascontiguousarray(
        g("L0_attn_maa_w2").transpose(1, 0, 2).reshape(32, 5 * 1024))
    for nm in ("Wr", "Wk", "Wv", "Wo"):
        h[f"l0a_{nm}"] = g(f"L0_attn_{nm}")
    h["l0a_gw1"] = g("L0_attn_gate_w1")
    h["l0a_gw2"] = g("L0_attn_gate_w2")
    h["l0a_dw1"] = g("L0_attn_decay_w1")
    h["l0a_dw2"] = g("L0_attn_decay_w2")
    return h



def _load_rearr(P, sb, dram_t, r=8, direct=False):
    if direct:
        P.dma("pool", Rg(sb, 0, 128, 0, dram_t.cols,
                         ap=sb.h[:, 0:dram_t.cols]), rg(dram_t))
        return
    inap = dram_t.h.rearrange("(r p) f -> p r f", p=128)
    outap = sb.h[:, 0:sb.cols].rearrange("p (r f) -> p r f", r=r)
    P.dma("pool", Rg(sb, 0, sb.parts, 0, sb.cols, ap=outap),
          Rg(dram_t, ap=inap))


def _mkenv(P, ctx, blocks):
    env = {}
    nc = P.nc
    env["x_in"] = P.dram_in("x_t", [D, TOK])
    env["y_out"] = P.dram_out("y_t", [D, TOK])
    cdefs = [("identity_b", 128, 128, BF16), ("identity_f", 128, 128, F32),
             ("ones_col_b", 128, 1, BF16),
             ("ones_row_f", 1, 128, F32), ("headsel_b", 128, 2, BF16),
             ("pat2_f", 2, 128, F32), ("ut_mask_f", CHUNK, CHUNK, F32),
             ("dg_mask_f", CHUNK, CHUNK, F32), ("scan_mask_f", 128, 256, F32)]
    for nm, p, cc, dt in cdefs:
        dr = P.dram_in(nm, [p, cc], dt)
        sb = P.sbuf(ctx, "s_" + nm, p, cc, dt)
        P.dma("sp", rg(sb), rg(dr))
        env[nm] = sb
    env["X"] = [P.sbuf(ctx, f"X{t}", 128, TOK, F32) for t in range(NCT)]
    for t in range(NCT):
        P.dma("sp", rg(env["X"][t]),
              Rg(env["x_in"], t * 128, (t + 1) * 128, 0, TOK,
                 ap=env["x_in"].h[t * 128:(t + 1) * 128, :]))
    return env


def _finish(P, env):
    for t in range(NCT):
        P.dma("sp",
              Rg(env["y_out"], t * 128, (t + 1) * 128, 0, TOK,
                 ap=env["y_out"].h[t * 128:(t + 1) * 128, :]),
              rg(env["X"][t]))



def _post_ln(P, env, sc, r1_of, w, qcols, tag, out_of=None):
    X = env["X"]
    pmL1, pmL2, pmL3, pmL4 = sc["pmL1"], sc["pmL2"], sc["pmL3"], sc["pmL4"]
    r1b_of, r1s_of = sc["R1B"], sc["R1S"]
    LT1, LT2, LT3 = sc["LT1"], sc["LT2"], sc["LT3"]
    q0, q1 = qcols
    if out_of is None:
        out_of = lambda t, c0, c1: rg(X[t], cs=(q0 + c0, q0 + c1))
    for t in range(NCT):
        P.copy("dve", r1b_of(t, 0, w), r1_of(t, 0, w), tag=f"{tag}.r1b{t}")
        P.act(r1s_of(t, 0, w), r1_of(t, 0, w), AF.Square, tag=f"{tag}.r1s{t}")
    for c0 in range(0, w, 512):
        c1 = min(c0 + 512, w)
        wb = c1 - c0
        sb = f"{tag}.{c0}"
        P.mm_group(rg(pmL1, ps=(0, 1), cs=(0, wb)),
                   [(rg(env["ones_col_b"]), r1b_of(t, c0, c1))
                    for t in range(NCT)], tag=f"{sb}.s1")
        P.mm_group(rg(pmL2, ps=(0, 1), cs=(0, wb)),
                   [(rg(env["ones_col_b"]), r1s_of(t, c0, c1))
                    for t in range(NCT)], tag=f"{sb}.s2")
        P.ts("dve", rg(LT1, cs=(c0, c1)), rg(pmL1, ps=(0, 1), cs=(0, wb)),
             1.0 / D, OP.mult, tag=f"{sb}.mu")
        P.ts("dve", rg(LT2, cs=(c0, c1)), rg(pmL2, ps=(0, 1), cs=(0, wb)),
             1.0 / D, OP.mult, tag=f"{sb}.m2")
        P.tt("dve", rg(LT3, cs=(c0, c1)), rg(LT1, cs=(c0, c1)),
             rg(LT1, cs=(c0, c1)), OP.mult, tag=f"{sb}.mu2")
        P.tt("dve", rg(LT2, cs=(c0, c1)), rg(LT2, cs=(c0, c1)),
             rg(LT3, cs=(c0, c1)), OP.subtract, tag=f"{sb}.var")
        P.act(rg(LT2, cs=(c0, c1)), rg(LT2, cs=(c0, c1)), AF.Ln,
              bias=rg(sc["EPSLN"]), tag=f"{sb}.ln")
        P.act(rg(LT2, cs=(c0, c1)), rg(LT2, cs=(c0, c1)), AF.Exp, scale=-0.5,
              tag=f"{sb}.ex")
        P.tt("dve", rg(LT1, cs=(c0, c1)), rg(LT1, cs=(c0, c1)),
             rg(LT2, cs=(c0, c1)), OP.mult, tag=f"{sb}.b2")
        P.mm_group(rg(pmL3, cs=(0, wb)),
                   [(rg(env["ones_row_f"]), rg(LT2, cs=(c0, c1)))], tag=f"{sb}.bc1")
        P.mm_group(rg(pmL4, cs=(0, wb)),
                   [(rg(env["ones_row_f"]), rg(LT1, cs=(c0, c1)))], tag=f"{sb}.bc2")
        for t in range(NCT):
            P.tt("dve", out_of(t, c0, c1), r1_of(t, c0, c1),
                 rg(pmL3, cs=(0, wb)), OP.mult, tag=f"{sb}.a1_{t}")
            P.tt("dve", out_of(t, c0, c1), out_of(t, c0, c1),
                 rg(pmL4, cs=(0, wb)), OP.subtract, tag=f"{sb}.a2_{t}")



def emit_l0_attn(P, ctx, env):
    nc = P.nc
    X = env["X"]
    QW = 256
    NQB = TOK // QW
    CPQ = QW // CHUNK

    d = {}
    for nm in ("maa_x", "maa_w", "maa_k", "maa_v", "maa_r", "maa_g",
               "time_decay", "ln_g", "ln_b", "u"):
        d[nm] = P.dram_in(f"l0a_{nm}", [128, 8])
    d["maa_w1"] = P.dram_in("l0a_maa_w1", [D, 160])
    d["maa_w2"] = P.dram_in("l0a_maa_w2", [32, 5 * 1024])
    for nm in ("Wr", "Wk", "Wv", "Wo"):
        d[nm] = P.dram_in(f"l0a_{nm}", [D, D])
    d["gw1"] = P.dram_in("l0a_gw1", [D, 64])
    d["gw2"] = P.dram_in("l0a_gw2", [64, D])
    d["dw1"] = P.dram_in("l0a_dw1", [D, 64])
    d["dw2"] = P.dram_in("l0a_dw2", [64, D])

    V = {}
    for nm in ("maa_x", "maa_w", "maa_k", "maa_v", "maa_r", "maa_g",
               "time_decay", "ln_g", "ln_b", "u"):
        V[nm] = P.sbuf(ctx, f"v_{nm}", 128, 8, F32)
        P.dma("sp", rg(V[nm]), rg(d[nm]))
    MW1 = P.sbuf(ctx, "MW1", 128, 8 * 160, BF16)
    _load_rearr(P, MW1, d["maa_w1"])
    MW2 = P.sbuf(ctx, "MW2", 32, 5 * 1024, BF16)
    P.dma("pool", rg(MW2), rg(d["maa_w2"]))
    GW1 = P.sbuf(ctx, "GW1", 128, 8 * 64, BF16)
    _load_rearr(P, GW1, d["gw1"])
    DW1 = P.sbuf(ctx, "DW1", 128, 8 * 64, BF16)
    _load_rearr(P, DW1, d["dw1"])
    GW2 = P.sbuf(ctx, "GW2", 64, D, BF16)
    P.dma("pool", rg(GW2), rg(d["gw2"]))
    DW2 = P.sbuf(ctx, "DW2", 64, D, BF16)
    P.dma("pool", rg(DW2), rg(d["dw2"]))

    WBUF = [P.sbuf(ctx, f"l0a_wbuf{i}", 128, 8 * 1024, BF16) for i in range(2)]

    XXB = P.sbuf(ctx, "XXB", 128, NCT * QW, BF16)
    XXX = P.sbuf(ctx, "XXX", 128, NCT * QW, BF16)
    XF = [P.sbuf(ctx, "XF0", 128, NCT * QW, BF16), XXX]
    LORA = P.sbuf(ctx, "LORA", 32, 5 * QW, BF16)
    TGD = P.sbuf(ctx, "TGD", 64, QW, BF16)
    TGG = P.sbuf(ctx, "TGG", 64, QW, BF16)
    RB = P.sbuf(ctx, "RB_", 128, NCT * QW, BF16)
    KB = P.sbuf(ctx, "KB_", 128, NCT * QW, BF16)
    GB = P.sbuf(ctx, "GB_", 128, NCT * QW, BF16)
    VT = P.sbuf(ctx, "VT_", 128, (QW // 128) * D, BF16)
    PS = P.sbuf(ctx, "PS_", 128, NCT * QW, F32)
    CP = P.sbuf(ctx, "CP_", 128, NCT * QW, F32)
    RT = P.sbuf(ctx, "RT_", 128, NCT * QW, BF16)
    KH = P.sbuf(ctx, "KH_", 128, NCT * QW, BF16)
    K2 = P.sbuf(ctx, "K2_", 128, NCT * QW, BF16)
    KU = KB
    DT = P.sbuf(ctx, "DT_", 128, NCT * CPQ, F32)
    YY = P.sbuf(ctx, "YY_", 128, NCT * QW, F32)
    Y2B = P.sbuf(ctx, "Y2B", 128, NCT * QW, BF16)
    YGB = P.sbuf(ctx, "YGB", 128, NCT * QW, BF16)
    YB = YGB
    MT1 = [P.sbuf(ctx, f"MT1_{i}", 128, CHUNK, BF16) for i in range(2)]
    MT2 = [P.sbuf(ctx, f"MT2_{i}", 128, CHUNK, BF16) for i in range(2)]
    ASB = [P.sbuf(ctx, f"ASB_{i}", 128, CHUNK, BF16) for i in range(2)]
    K2TM = [P.sbuf(ctx, f"K2TM{i}", 128, 64, BF16) for i in range(2)]
    YT1 = [P.sbuf(ctx, f"YT1_{i}", 128, CHUNK, F32) for i in range(2)]
    SF = P.sbuf(ctx, "SF_", 128, 8 * 64, F32)
    SB = P.sbuf(ctx, "SB_", 128, 8 * 64, BF16)
    ZC = P.sbuf(ctx, "ZC_", 128, 1, F32)
    ST1 = [P.sbuf(ctx, f"gT1_{i}", 2, QW, F32) for i in range(2)]
    ST2 = [P.sbuf(ctx, f"gT2_{i}", 2, QW, F32) for i in range(2)]
    ST3 = [P.sbuf(ctx, f"gT3_{i}", 2, QW, F32) for i in range(2)]
    PAT2B = env["pat2_f"]
    sc = {
        "R1": YY,
        "R1B": (lambda t, c0, c1: rg(YB, cs=(t * QW + c0, t * QW + c1))),
        "R1S": (lambda t, c0, c1: rg(Y2B, cs=(t * QW + c0, t * QW + c1))),
        "LT1": Tsr(ST1[0].h, ST1[0].name, 1, QW, F32, "sbuf"),
        "LT2": Tsr(ST2[0].h, ST2[0].name, 1, QW, F32, "sbuf"),
        "LT3": Tsr(ST3[0].h, ST3[0].name, 1, QW, F32, "sbuf"),
    }
    pm = [P.psum(ctx, f"pm{i}", 128, 512, F32) for i in range(7)]
    PT = P.psum(ctx, "pmT", 128, 1024, BF16)
    sc["pmL1"], sc["pmL2"], sc["pmL3"], sc["pmL4"] = pm[0], pm[1], pm[2], pm[3]

    EPSGN = P.sbuf(ctx, "EPSGN", 2, 1, F32)
    sc["EPSLN"] = P.sbuf(ctx, "EPSLN", 1, 1, F32)
    P.memset("dve", rg(EPSGN), 0.00064, tag="epsg")
    P.memset("dve", rg(sc["EPSLN"]), 1e-5, tag="epsl")
    taps = {}
    for nm, dt_ in (("y", F32), ("r", BF16), ("k", BF16), ("cp", F32),
                    ("g", BF16), ("yn", BF16)):
        if nm in DEBUG_TAPS:
            taps[nm] = P.dram_out(f"dbg_{nm}", [128, NQB * NCT * QW], dt_)
    if "v" in DEBUG_TAPS:
        taps["v"] = P.dram_out("dbg_v", [128, NQB * 2048], BF16)

    P.memset("dve", rg(SF), 0.0, tag="s0")
    P.memset("dve", rg(SB), 0.0, tag="s0b")
    P.memset("dve", rg(ZC), 0.0, tag="zc")

    mnames = ("maa_w", "maa_k", "maa_v", "maa_r", "maa_g")

    for qb in range(NQB):
        q0 = qb * QW
        tg = f"a{qb}"
        wuse = {"Wk": WBUF[0], "Wv": WBUF[1], "Wr": WBUF[0], "Wo": WBUF[1]}
        _load_rearr(P, WBUF[0], d["Wk"])
        _load_rearr(P, WBUF[1], d["Wv"])
        for t in range(NCT):
            cs = (t * QW, t * QW + QW)
            if qb == 0:
                P.tt("dve", rg(XXB, cs=(cs[0], cs[0] + 1)), rg(ZC),
                     rg(X[t], cs=(0, 1)), OP.subtract, tag=f"{tg}.xx0_{t}")
                P.tt("dve", rg(XXB, cs=(cs[0] + 1, cs[1])),
                     rg(X[t], cs=(q0, q0 + QW - 1)),
                     rg(X[t], cs=(q0 + 1, q0 + QW)), OP.subtract,
                     tag=f"{tg}.xx_{t}")
            else:
                P.tt("dve", rg(XXB, cs=cs), rg(X[t], cs=(q0 - 1, q0 + QW - 1)),
                     rg(X[t], cs=(q0, q0 + QW)), OP.subtract, tag=f"{tg}.xx_{t}")
        for t in range(NCT):
            cs = (t * QW, t * QW + QW)
            P.stt("dve", rg(XXX, cs=cs), rg(XXB, cs=cs),
                  rg(V["maa_x"], cs=(t, t + 1)), rg(X[t], cs=(q0, q0 + QW)),
                  OP.mult, OP.add, tag=f"{tg}.xxx_{t}")
        P.mm_group(rg(pm[4], ps=(0, 96), cs=(0, QW)),
                   [(rg(MW1, cs=(t * 160, t * 160 + 96)),
                     rg(XXX, cs=(t * QW, (t + 1) * QW))) for t in range(NCT)],
                   tag=f"{tg}.lra")
        P.mm_group(rg(pm[5], ps=(0, 64), cs=(0, QW)),
                   [(rg(MW1, cs=(t * 160 + 96, t * 160 + 160)),
                     rg(XXX, cs=(t * QW, (t + 1) * QW))) for t in range(NCT)],
                   tag=f"{tg}.lrb")
        for f in range(3):
            P.act(rg(LORA, ps=(0, 32), cs=(f * QW, (f + 1) * QW)),
                  rg(pm[4], ps=(f * 32, f * 32 + 32), cs=(0, QW)), AF.Tanh,
                  tag=f"{tg}.lt{f}")
        for f in range(3, 5):
            P.act(rg(LORA, ps=(0, 32), cs=(f * QW, (f + 1) * QW)),
                  rg(pm[5], ps=((f - 3) * 32, (f - 3) * 32 + 32), cs=(0, QW)),
                  AF.Tanh, tag=f"{tg}.lt{f}")
        for f in range(5):
            xf = XF[f % 2]
            for tb in (0, 4):
                for t in range(tb, tb + 4):
                    P.mm_group(rg(pm[t % 4], cs=(0, QW)),
                               [(rg(MW2, cs=(f * 1024 + t * 128, f * 1024 + t * 128 + 128)),
                                 rg(LORA, ps=(0, 32), cs=(f * QW, (f + 1) * QW)))],
                               tag=f"{tg}.m{f}_{t}")
                for t in range(tb, tb + 4):
                    cs = (t * QW, (t + 1) * QW)
                    P.stt("dve", rg(xf, cs=cs), rg(pm[t % 4], cs=(0, QW)),
                          rg(V[mnames[f]], cs=(t, t + 1)), rg(XXB, cs=cs),
                          OP.add, OP.mult, tag=f"{tg}.s{f}_{t}")
            for t in range(NCT):
                cs = (t * QW, (t + 1) * QW)
                P.tt("dve", rg(xf, cs=cs), rg(xf, cs=cs),
                     rg(X[t], cs=(q0, q0 + QW)), OP.add, tag=f"{tg}.xf{f}_{t}")
            if f == 0:
                P.mm_group(rg(pm[6], ps=(0, 64), cs=(0, QW)),
                           [(rg(DW1, cs=(t * 64, (t + 1) * 64)),
                             rg(xf, cs=(t * QW, (t + 1) * QW))) for t in range(NCT)],
                           tag=f"{tg}.dc1")
                P.act(rg(TGD), rg(pm[6], ps=(0, 64), cs=(0, QW)), AF.Tanh,
                      tag=f"{tg}.dct")
                for t in range(NCT):
                    bank = pm[4 + (t % 2)]
                    P.mm_group(rg(bank, cs=(0, QW)),
                               [(rg(DW2, cs=(t * 128, (t + 1) * 128)), rg(TGD))],
                               tag=f"{tg}.dc2_{t}")
                    P.act(rg(PS, cs=(t * QW, (t + 1) * QW)), rg(bank, cs=(0, QW)),
                          AF.Exp, bias=rg(V["time_decay"], cs=(t, t + 1)),
                          tag=f"{tg}.dce_{t}")
            elif f == 1:
                for co in range(NCT):
                    bank = pm[4 + (co % 2)]
                    P.mm_group(rg(bank, cs=(0, QW)),
                               [(rg(wuse["Wk"], cs=(t * 1024 + co * 128, t * 1024 + co * 128 + 128)),
                                 rg(xf, cs=(t * QW, (t + 1) * QW))) for t in range(NCT)],
                               tag=f"{tg}.k_{co}")
                    P.copy("dve", rg(KB, cs=(co * QW, (co + 1) * QW)),
                           rg(bank, cs=(0, QW)), tag=f"{tg}.ke_{co}")
            elif f == 2:
                _load_rearr(P, WBUF[0], d["Wr"])
                for tt in range(QW // 128):
                    for nb in range(2):
                        bank = pm[4 + (nb % 2)]
                        P.mm_group(rg(bank),
                                   [(rg(xf, cs=(t * QW + tt * 128, t * QW + tt * 128 + 128)),
                                     rg(wuse["Wv"], cs=(t * 1024 + nb * 512, t * 1024 + nb * 512 + 512)))
                                    for t in range(NCT)], tag=f"{tg}.v_{tt}{nb}")
                        P.copy("dve", rg(VT, cs=(tt * 1024 + nb * 512, tt * 1024 + nb * 512 + 512)),
                               rg(bank), tag=f"{tg}.ve_{tt}{nb}")
            elif f == 3:
                _load_rearr(P, WBUF[1], d["Wo"])
                for co in range(NCT):
                    bank = pm[4 + (co % 2)]
                    P.mm_group(rg(bank, cs=(0, QW)),
                               [(rg(wuse["Wr"], cs=(t * 1024 + co * 128, t * 1024 + co * 128 + 128)),
                                 rg(xf, cs=(t * QW, (t + 1) * QW))) for t in range(NCT)],
                               tag=f"{tg}.r_{co}")
                    P.copy("dve", rg(RB, cs=(co * QW, (co + 1) * QW)),
                           rg(bank, cs=(0, QW)), tag=f"{tg}.re_{co}")
            elif f == 4:
                P.mm_group(rg(pm[6], ps=(0, 64), cs=(0, QW)),
                           [(rg(GW1, cs=(t * 64, (t + 1) * 64)),
                             rg(xf, cs=(t * QW, (t + 1) * QW))) for t in range(NCT)],
                           tag=f"{tg}.g1")
                P.act(rg(TGG), rg(pm[6], ps=(0, 64), cs=(0, QW)), AF.Tanh,
                      tag=f"{tg}.gt")
                for t in range(NCT):
                    bank = pm[4 + (t % 2)]
                    P.mm_group(rg(bank, cs=(0, QW)),
                               [(rg(GW2, cs=(t * 128, (t + 1) * 128)), rg(TGG))],
                               tag=f"{tg}.g2_{t}")
                    P.copy("dve", rg(GB, cs=(t * QW, (t + 1) * QW)),
                           rg(bank, cs=(0, QW)), tag=f"{tg}.ge_{t}")
        if "k" in taps:
            tp_ = taps["k"]
            P.dma("sp", Rg(tp_, 0, 128, qb * NCT * QW, (qb + 1) * NCT * QW,
                           ap=tp_.h[:, qb * NCT * QW:(qb + 1) * NCT * QW]),
                  rg(KB), tag=f"{tg}.tapk")
        CS = [(t * QW, (t + 1) * QW) for t in range(NCT)]
        for t in range(NCT):
            P.scan(rg(CP, cs=CS[t]), rg(env["scan_mask_f"]), rg(PS, cs=CS[t]),
                   0.0, OP.mult, OP.add, tag=f"{tg}.cp{t}")
        for t in range(NCT):
            P.tt("dve", rg(PS, cs=CS[t]), rg(PS, cs=CS[t]), rg(CP, cs=CS[t]),
                 OP.subtract, tag=f"{tg}.e1s{t}")
        for t in range(NCT):
            P.act(rg(PS, cs=CS[t]), rg(PS, cs=CS[t]), AF.Exp, tag=f"{tg}.e1e{t}")
        for t in range(NCT):
            P.tt("dve", rg(RT, cs=CS[t]), rg(RB, cs=CS[t]), rg(PS, cs=CS[t]),
                 OP.mult, tag=f"{tg}.rt{t}")
        for t in range(NCT):
            P.ts("dve", rg(PS, cs=CS[t]), rg(CP, cs=CS[t]), 80.0, OP.min,
                 tag=f"{tg}.khs{t}")
        for t in range(NCT):
            P.act(rg(PS, cs=CS[t]), rg(PS, cs=CS[t]), AF.Exp, tag=f"{tg}.khe{t}")
        for t in range(NCT):
            P.tt("dve", rg(KH, cs=CS[t]), rg(KB, cs=CS[t]), rg(PS, cs=CS[t]),
                 OP.mult, tag=f"{tg}.kh{t}")
        for t in range(NCT):
            for c in range(CPQ):
                ccs = (CS[t][0] + c * CHUNK, CS[t][0] + (c + 1) * CHUNK)
                P.ts("dve", rg(PS, cs=ccs), rg(CP, cs=ccs),
                     rg(CP, cs=(ccs[1] - 1, ccs[1])), OP.subtract,
                     tag=f"{tg}.k2s{t}_{c}")
        for t in range(NCT):
            P.act(rg(PS, cs=CS[t]), rg(PS, cs=CS[t]), AF.Exp, tag=f"{tg}.k2e{t}")
        for t in range(NCT):
            P.tt("dve", rg(K2, cs=CS[t]), rg(KB, cs=CS[t]), rg(PS, cs=CS[t]),
                 OP.mult, tag=f"{tg}.k2{t}")
        for t in range(NCT):
            P.act(rg(DT, cs=(t * CPQ, (t + 1) * CPQ)),
                  Rg(CP, 0, 128, CS[t][0], CS[t][1],
                     ap=CP.h[:, CS[t][0] + CHUNK - 1:CS[t][1]:CHUNK]),
                  AF.Exp, scale=-1.0, tag=f"{tg}.dt{t}")
        for t in range(NCT):
            P.ts("dve", rg(KU, cs=CS[t]), rg(KB, cs=CS[t]),
                 rg(V["u"], cs=(t, t + 1)), OP.mult, tag=f"{tg}.ku{t}")
        for c in range(CPQ):
            for hp2 in range(NHEAD // 2):
                hs = (2 * hp2, 2 * hp2 + 1)
                bA = {0: pm[0], 1: pm[1]}
                bA2 = {0: pm[2], 1: pm[3]}
                bY1 = {0: pm[4], 1: pm[5]}
                bY2 = {0: pm[6], 1: pm[0]}
                bKV = {0: pm[2], 1: pm[3]}
                def HV(h):
                    t = h // 2
                    return (t, 64 * (h % 2), h % 2,
                            (t * QW + c * CHUNK, t * QW + (c + 1) * CHUNK),
                            ((h // 2) * 64, (h // 2) * 64 + 64),
                            (c * 1024 + h * 64, c * 1024 + h * 64 + 64))
                for h in hs:
                    t, hp, p_, ccs, scs, vcs = HV(h)
                    htg = f"{tg}.w{c}_{h}"
                    P.mm_group(rg(bA[p_], cs=(0, CHUNK)),
                               [(rg(KH, ps=(hp, hp + 64), cs=ccs),
                                 rg(RT, ps=(hp, hp + 64), cs=ccs))], tag=htg + ".A")
                    P.mm_group(rg(bA2[p_], cs=(0, CHUNK)),
                               [(rg(KU, ps=(hp, hp + 64), cs=ccs),
                                 rg(RB, ps=(hp, hp + 64), cs=ccs))], tag=htg + ".A2")
                for h in hs:
                    t, hp, p_, ccs, scs, vcs = HV(h)
                    htg = f"{tg}.w{c}_{h}"
                    P.tt("dve", rg(MT1[p_]), rg(bA[p_], cs=(0, CHUNK)),
                         rg(env["ut_mask_f"]), OP.mult, tag=htg + ".m1")
                    P.tt("dve", rg(MT2[p_]), rg(bA2[p_], cs=(0, CHUNK)),
                         rg(env["dg_mask_f"]), OP.mult, tag=htg + ".m2")
                    P.tt("dve", rg(ASB[p_]), rg(MT1[p_]), rg(MT2[p_]), OP.add,
                         tag=htg + ".m3")
                for h in hs:
                    t, hp, p_, ccs, scs, vcs = HV(h)
                    htg = f"{tg}.w{c}_{h}"
                    P.mm_group(rg(bY1[p_], ps=(hp, hp + 64), cs=(0, CHUNK)),
                               [(rg(SB, ps=(hp, hp + 64), cs=scs),
                                 rg(RT, ps=(hp, hp + 64), cs=ccs))],
                               tp=(hp, hp), tag=htg + ".y1")
                    P.transpose(rg(PT, cs=(p_ * 64, p_ * 64 + 64)),
                                rg(K2, ps=(hp, hp + 64), cs=ccs),
                                rg(env["identity_b"], ps=(hp, hp + 64), cs=(hp, hp + 64)),
                                tp=(hp, 0), tag=htg + ".tr")
                for h in hs:
                    t, hp, p_, ccs, scs, vcs = HV(h)
                    htg = f"{tg}.w{c}_{h}"
                    P.copy("dve", rg(K2TM[p_]), rg(PT, cs=(p_ * 64, p_ * 64 + 64)),
                           tag=htg + ".tre")
                for h in hs:
                    t, hp, p_, ccs, scs, vcs = HV(h)
                    htg = f"{tg}.w{c}_{h}"
                    P.mm_group(rg(bY2[p_], ps=(hp, hp + 64), cs=(0, CHUNK)),
                               [(rg(VT, cs=vcs), rg(ASB[p_]))],
                               tp=(0, hp), tag=htg + ".y2")
                for h in hs:
                    t, hp, p_, ccs, scs, vcs = HV(h)
                    htg = f"{tg}.w{c}_{h}"
                    P.copy("act", rg(YT1[p_], ps=(hp, hp + 64)),
                           rg(bY1[p_], ps=(hp, hp + 64), cs=(0, CHUNK)),
                           tag=htg + ".ye1")
                for h in hs:
                    t, hp, p_, ccs, scs, vcs = HV(h)
                    htg = f"{tg}.w{c}_{h}"
                    P.mm_group(rg(bKV[p_], ps=(hp, hp + 64), cs=(0, 64)),
                               [(rg(K2TM[p_]), rg(VT, cs=vcs))],
                               tp=(0, hp), tag=htg + ".kv")
                for h in hs:
                    t, hp, p_, ccs, scs, vcs = HV(h)
                    htg = f"{tg}.w{c}_{h}"
                    P.tt("dve", rg(YY, ps=(hp, hp + 64), cs=ccs),
                         rg(YT1[p_], ps=(hp, hp + 64)),
                         rg(bY2[p_], ps=(hp, hp + 64), cs=(0, CHUNK)),
                         OP.add, tag=htg + ".ye2")
                    P.stt("dve", rg(SF, ps=(hp, hp + 64), cs=scs),
                          rg(SF, ps=(hp, hp + 64), cs=scs),
                          rg(DT, ps=(hp, hp + 64), cs=(t * CPQ + c, t * CPQ + c + 1)),
                          rg(bKV[p_], ps=(hp, hp + 64), cs=(0, 64)), OP.mult, OP.add,
                          tag=htg + ".su")
                    P.copy("dve", rg(SB, ps=(hp, hp + 64), cs=scs),
                           rg(SF, ps=(hp, hp + 64), cs=scs), tag=htg + ".sb")
        for nm, buf in (("y", YY), ("r", RB), ("g", GB), ("cp", CP)):
            if nm in taps:
                tp_ = taps[nm]
                P.dma("sp", Rg(tp_, 0, 128, qb * NCT * QW, (qb + 1) * NCT * QW,
                               ap=tp_.h[:, qb * NCT * QW:(qb + 1) * NCT * QW]),
                      rg(buf), tag=f"{tg}.tap{nm}")
        for t in range(NCT):
            p2 = t % 2
            s1, s2, s3 = ST1[p2], ST2[p2], ST3[p2]
            bs1, bs2 = pm[0 + 4 * p2], pm[1 + 4 * p2]
            bb1, bb2 = (pm[2], pm[3]) if p2 == 0 else (pm[6], pm[2])
            cs = (t * QW, (t + 1) * QW)
            P.copy("dve", rg(YB, cs=cs), rg(YY, cs=cs), tag=f"{tg}.yb{t}")
            P.act(rg(Y2B, cs=cs), rg(YY, cs=cs), AF.Square, tag=f"{tg}.y2{t}")
            P.mm_group(rg(bs1, ps=(0, 2), cs=(0, QW)),
                       [(rg(env["headsel_b"]), rg(YB, cs=cs))], tag=f"{tg}.gs1_{t}")
            P.mm_group(rg(bs2, ps=(0, 2), cs=(0, QW)),
                       [(rg(env["headsel_b"]), rg(Y2B, cs=cs))], tag=f"{tg}.gs2_{t}")
            P.ts("dve", rg(s1, cs=(0, QW)), rg(bs1, ps=(0, 2), cs=(0, QW)),
                 1.0 / 64, OP.mult, tag=f"{tg}.gmu{t}")
            P.ts("dve", rg(s2, cs=(0, QW)), rg(bs2, ps=(0, 2), cs=(0, QW)),
                 1.0 / 64, OP.mult, tag=f"{tg}.gm2{t}")
            P.tt("dve", rg(s3, cs=(0, QW)), rg(s1, cs=(0, QW)),
                 rg(s1, cs=(0, QW)), OP.mult, tag=f"{tg}.gmm{t}")
            P.tt("dve", rg(s2, cs=(0, QW)), rg(s2, cs=(0, QW)),
                 rg(s3, cs=(0, QW)), OP.subtract, tag=f"{tg}.gvr{t}")
            P.act(rg(s2, cs=(0, QW)), rg(s2, cs=(0, QW)), AF.Ln,
                  bias=rg(EPSGN), tag=f"{tg}.gln{t}")
            P.act(rg(s2, cs=(0, QW)), rg(s2, cs=(0, QW)), AF.Exp, scale=-0.5,
                  tag=f"{tg}.gex{t}")
            P.tt("dve", rg(s1, cs=(0, QW)), rg(s1, cs=(0, QW)),
                 rg(s2, cs=(0, QW)), OP.mult, tag=f"{tg}.gb2{t}")
            P.mm_group(rg(bb1, cs=(0, QW)),
                       [(rg(PAT2B), rg(s2, cs=(0, QW)))], tag=f"{tg}.gbc1_{t}")
            P.mm_group(rg(bb2, cs=(0, QW)),
                       [(rg(PAT2B), rg(s1, cs=(0, QW)))], tag=f"{tg}.gbc2_{t}")
            P.tt("dve", rg(YY, cs=cs), rg(YY, cs=cs), rg(bb1, cs=(0, QW)),
                 OP.mult, tag=f"{tg}.ga1_{t}")
            P.tt("dve", rg(YY, cs=cs), rg(YY, cs=cs), rg(bb2, cs=(0, QW)),
                 OP.subtract, tag=f"{tg}.ga2_{t}")
            P.ts("dve", rg(YY, cs=cs), rg(YY, cs=cs),
                 rg(V["ln_g"], cs=(t, t + 1)), OP.mult,
                 rg(V["ln_b"], cs=(t, t + 1)), OP.add, tag=f"{tg}.ga3_{t}")
            P.tt("dve", rg(YGB, cs=cs), rg(YY, cs=cs), rg(GB, cs=cs), OP.mult,
                 tag=f"{tg}.ga4_{t}")
        if "yn" in taps:
            tp_ = taps["yn"]
            P.dma("sp", Rg(tp_, 0, 128, qb * NCT * QW, (qb + 1) * NCT * QW,
                           ap=tp_.h[:, qb * NCT * QW:(qb + 1) * NCT * QW]),
                  rg(YGB), tag=f"{tg}.tapyn")
        for co in range(NCT):
            bank = pm[4 + (co % 2)]
            P.mm_group(rg(bank, cs=(0, QW)),
                       [(rg(wuse["Wo"], cs=(t * 1024 + co * 128, t * 1024 + co * 128 + 128)),
                         rg(YGB, cs=(t * QW, (t + 1) * QW))) for t in range(NCT)],
                       tag=f"{tg}.o_{co}")
            P.tt("dve", rg(sc["R1"], cs=(co * QW, (co + 1) * QW)),
                 rg(bank, cs=(0, QW)), rg(X[co], cs=(q0, q0 + QW)), OP.add,
                 tag=f"{tg}.res_{co}")
        _post_ln(P, env, sc,
                 lambda t, c0, c1: rg(sc["R1"], cs=(t * QW + c0, t * QW + c1)),
                 QW, (q0, q0 + QW), f"{tg}.ln")



DEBUG_TAPS = []
BLOCK_EMITTERS = {}


def _register():
    BLOCK_EMITTERS["l0_attn"] = emit_l0_attn
    BLOCK_EMITTERS["l0_ffn"] = emit_l0_ffn
    BLOCK_EMITTERS["l1_attn"] = emit_l1_attn
    BLOCK_EMITTERS["l1_ffn"] = emit_l1_ffn


def build_program(blocks):
    nc = bass.Bass("TRN2", target_bir_lowering=False)
    P = Prog(nc)
    with ExitStack() as ctx:
        env = _mkenv(P, ctx, blocks)
        for bi, b in enumerate(blocks):
            if bi > 0:
                P.barrier(tag=f"bar{bi}")
            with ExitStack() as bctx:
                BLOCK_EMITTERS[b](P, bctx, env)
        _finish(P, env)
        P.emit(ctx)
    return nc, P


def host_inputs(inputs):
    h = {}
    h.update(_consts_host())
    h.update(_l0_attn_host(inputs))
    h.update(_l0_ffn_host(inputs))
    h.update(_l1_attn_host(inputs))
    h.update(_l1_ffn_host(inputs))
    return h


LAUNCH_PLAN = [("l0_attn", "l0_ffn", "l1_attn", "l1_ffn")]


def kernel(**inputs):
    _register()
    shared = host_inputs(inputs)
    x = np.asarray(inputs["L0_attn_x"], np.float32)
    xt = [np.ascontiguousarray(x[b].T) for b in range(8)]
    for blocks in LAUNCH_PLAN:
        nc, P = build_program(list(blocks))
        in_maps = [dict(shared, x_t=xt[b]) for b in range(8)]
        res = run_bass_kernel_spmd(nc, in_maps, core_ids=list(range(8)))
        xt = [np.asarray(res.results[b]["y_t"]) for b in range(8)]
    out = np.stack([t.T for t in xt], axis=0)
    return np.ascontiguousarray(out.astype(np.float32))



def _l0_ffn_host(inp):
    g = lambda k: np.asarray(inp[k], np.float32)
    h = {}
    h["l0f_ct"] = np.ascontiguousarray(g("L0_ffn_centroids").T)
    h["l0f_ebias"] = np.ascontiguousarray(
        np.broadcast_to(g("L0_ffn_expert_biases"), (128, 64)))
    h["l0f_wsu"] = g("L0_ffn_Ws_up")
    h["l0f_wsd"] = g("L0_ffn_Ws_down")
    h["l0f_wup"] = np.ascontiguousarray(g("L0_ffn_W_up").reshape(64 * 1024, 512))
    h["l0f_wdn"] = np.ascontiguousarray(g("L0_ffn_W_down").reshape(64 * 512, 1024))
    return h


def emit_l0_ffn(P, ctx, env):
    X = env["X"]
    QB = 1024
    NQ = TOK // QB
    NTT = QB // 128
    E = 64
    EG = 4

    d = {}
    d["ct"] = P.dram_in("l0f_ct", [D, 64])
    d["ebias"] = P.dram_in("l0f_ebias", [128, 64])
    d["wsu"] = P.dram_in("l0f_wsu", [D, 512])
    d["wsd"] = P.dram_in("l0f_wsd", [512, D])
    d["wup"] = P.dram_in("l0f_wup", [E * 1024, 512])
    d["wdn"] = P.dram_in("l0f_wdn", [E * 512, 1024])

    CT = P.sbuf(ctx, "f_CT", 128, 8 * 64, F32)
    inap = d["ct"].h.rearrange("(r p) f -> p r f", p=128)
    outap = CT.h[:, :].rearrange("p (r f) -> p r f", r=8)
    P.dma("sp", Rg(CT, 0, 128, 0, CT.cols, ap=outap), Rg(d["ct"], ap=inap))
    EBI = P.sbuf(ctx, "f_EBI", 128, 64, F32)
    P.dma("sp", rg(EBI), rg(d["ebias"]))

    XB1 = P.sbuf(ctx, "f_XB", 128, NCT * QB, BF16)
    HG = P.sbuf(ctx, "f_HG", 128, EG * 4 * QB, BF16)
    WU = [P.sbuf(ctx, f"f_WU{i}", 128, 8 * 512, BF16) for i in range(2)]
    WD = [P.sbuf(ctx, f"f_WD{i}", 128, 4 * 1024, BF16) for i in range(4)]
    AFM = P.sbuf(ctx, "f_AFM", 128, NTT * 64, F32)
    BSD = P.sbuf(ctx, "f_BSD", 128, NTT * 64, F32)
    MSK = P.sbuf(ctx, "f_MSK", 128, NTT * 64, F32)
    EQT = P.sbuf(ctx, "f_EQT", 128, 64, F32)
    MXT = P.sbuf(ctx, "f_MXT", 128, 1, F32)
    SGA = P.sbuf(ctx, "f_SGA", 128, NTT * 64, F32)
    SS = P.sbuf(ctx, "f_SS", 128, NTT, F32)
    CWE = P.sbuf(ctx, "f_CWE", 64, QB, F32)
    CWR = [P.sbuf(ctx, f"f_CWR{i}", 1, QB, F32) for i in range(2)]
    CWBS = P.sbuf(ctx, "f_CWBS", 128, QB, BF16)
    sc = {
        "LT1": P.sbuf(ctx, "f_LT1", 1, QB, F32),
        "LT2": P.sbuf(ctx, "f_LT2", 1, QB, F32),
        "LT3": P.sbuf(ctx, "f_LT3", 1, QB, F32),
        "EPSLN": P.sbuf(ctx, "f_EPS", 1, 1, F32),
    }
    HGV = Tsr(HG.h, HG.name, 128, EG * 4 * QB, BF16, "sbuf")
    sc["R1B"] = (lambda t, c0, c1: rg(HGV, cs=(t * QB + c0, t * QB + c1)))
    sc["R1S"] = (lambda t, c0, c1: rg(HGV, cs=(8 * QB + t * QB + c0, 8 * QB + t * QB + c1)))
    P.memset("dve", rg(sc["EPSLN"]), 1e-5, tag="f.eps")

    pm = [P.psum(ctx, f"f_pm{i}", 128, 512, F32) for i in range(8)]
    sc["pmL1"], sc["pmL2"], sc["pmL3"], sc["pmL4"] = pm[0], pm[1], pm[2], pm[3]

    for qb in range(NQ):
        q0 = qb * QB
        tg = f"f{qb}"
        for t in range(NCT):
            P.copy("dve", rg(XB1, cs=(t * QB, (t + 1) * QB)),
                   rg(X[t], cs=(q0, q0 + QB)), tag=f"{tg}.xb{t}")
        for tt in range(NTT):
            P.mm_group(rg(pm[4], cs=(0, 64)),
                       [(Rg(X[t], 0, 128, q0 + tt * 128, q0 + tt * 128 + 128),
                         rg(CT, cs=(t * 64, (t + 1) * 64))) for t in range(NCT)],
                       tag=f"{tg}.af{tt}")
            P.copy("dve", rg(AFM, cs=(tt * 64, (tt + 1) * 64)),
                   rg(pm[4], cs=(0, 64)), tag=f"{tg}.afe{tt}")
            P.tt("dve", rg(BSD, cs=(tt * 64, (tt + 1) * 64)),
                 rg(AFM, cs=(tt * 64, (tt + 1) * 64)), rg(EBI), OP.add,
                 tag=f"{tg}.bi{tt}")
        P.memset("dve", rg(MSK), 0.0, tag=f"{tg}.mz")
        for tt in range(NTT):
            bcs = (tt * 64, (tt + 1) * 64)
            for it in range(8):
                P.reduce(rg(MXT), rg(BSD, cs=bcs), OP.max, tag=f"{tg}.mx{tt}_{it}")
                P.ts("dve", rg(EQT), rg(BSD, cs=bcs), rg(MXT), OP.is_ge,
                     tag=f"{tg}.eq{tt}_{it}")
                P.tt("dve", rg(MSK, cs=bcs), rg(MSK, cs=bcs), rg(EQT), OP.add,
                     tag=f"{tg}.ma{tt}_{it}")
                P.stt("dve", rg(BSD, cs=bcs), rg(EQT), -1e30, rg(BSD, cs=bcs),
                      OP.mult, OP.add, tag=f"{tg}.ms{tt}_{it}")
        for tt in range(NTT):
            bcs = (tt * 64, (tt + 1) * 64)
            P.act(rg(SGA, cs=bcs), rg(AFM, cs=bcs), AF.Sigmoid, tag=f"{tg}.sg{tt}")
            P.tt("dve", rg(SGA, cs=bcs), rg(SGA, cs=bcs), rg(MSK, cs=bcs),
                 OP.mult, tag=f"{tg}.wm{tt}")
            P.reduce(rg(SS, cs=(tt, tt + 1)), rg(SGA, cs=bcs), OP.add,
                     tag=f"{tg}.ws{tt}")
        P.ts("dve", rg(SS), rg(SS), 1e-8, OP.add, tag=f"{tg}.se")
        P.recip(rg(SS), rg(SS), tag=f"{tg}.sr")
        for tt in range(NTT):
            bcs = (tt * 64, (tt + 1) * 64)
            P.ts("dve", rg(SGA, cs=bcs), rg(SGA, cs=bcs), rg(SS, cs=(tt, tt + 1)),
                 OP.mult, tag=f"{tg}.cw{tt}")
            P.transpose(rg(pm[5], ps=(0, 64), cs=(0, 128)),
                        rg(SGA, cs=bcs), rg(env["identity_f"]),
                        tag=f"{tg}.ct{tt}")
            P.copy("dve", rg(CWE, ps=(0, 64), cs=(tt * 128, (tt + 1) * 128)),
                   rg(pm[5], ps=(0, 64), cs=(0, 128)), tag=f"{tg}.cte{tt}")
        P.dma("pool", rg(WU[0], cs=(0, 8 * 512)),
              Rg(d["wsu"], ap=d["wsu"].h.rearrange("(r p) f -> p r f", p=128)),
              tag=f"{tg}.lwsu")
        P.dma("pool", rg(WD[0], cs=(0, 4 * 1024)),
              Rg(d["wsd"], ap=d["wsd"].h.rearrange("(r p) f -> p r f", p=128)),
              tag=f"{tg}.lwsd")
        for e in range(-1, E):
            par = (e + 1) % 2
            gslot = (e % EG) if e >= 0 else 0
            wu = WU[par]
            wd = WD[gslot] if e >= 0 else WD[0]
            etg = f"{tg}.e{e}"
            if e >= 0:
                P.dma("sp", rg(CWR[e % 2]),
                      Rg(CWE, e, e + 1, 0, QB, ap=CWE.h[e:e + 1, :]),
                      tag=f"{etg}.cwx")
                for nb in range(2):
                    P.mm_group(rg(pm[6], cs=(0, 512)),
                               [(rg(env["ones_row_f"]),
                                 rg(CWR[e % 2], cs=(nb * 512, nb * 512 + 512)))],
                               tag=f"{etg}.cb{nb}")
                    P.copy("dve", rg(CWBS, cs=(nb * 512, nb * 512 + 512)),
                           rg(pm[6], cs=(0, 512)), tag=f"{etg}.cbe{nb}")
            for fb in range(4):
                for nb in range(2):
                    bank = pm[4 + (nb % 2)]
                    P.mm_group(rg(bank),
                               [(rg(wu, cs=(t * 512 + fb * 128, t * 512 + fb * 128 + 128)),
                                 rg(XB1, cs=(t * QB + nb * 512, t * QB + nb * 512 + 512)))
                                for t in range(NCT)], tag=f"{etg}.up{fb}{nb}")
                    hcs = (gslot * 4 * QB + fb * QB + nb * 512,
                           gslot * 4 * QB + fb * QB + nb * 512 + 512)
                    P.act(rg(HG, cs=hcs), rg(bank), AF.Silu, tag=f"{etg}.si{fb}{nb}")
                    if e >= 0:
                        P.tt("dve", rg(HG, cs=hcs), rg(HG, cs=hcs),
                             rg(CWBS, cs=(nb * 512, nb * 512 + 512)), OP.mult,
                             tag=f"{etg}.sc{fb}{nb}")
            if e == -1 or (e % EG) == EG - 1:
                glo = 0 if e == -1 else e - EG + 1
                ne = 1 if e == -1 else EG
                for dt in range(NCT):
                    for nb in range(2):
                        bank = pm[4 + (nb % 2)]
                        terms = []
                        for ge in range(ne):
                            wdx = WD[ge] if e >= 0 else WD[0]
                            for fb in range(4):
                                terms.append(
                                    (rg(wdx, cs=(fb * 1024 + dt * 128, fb * 1024 + dt * 128 + 128)),
                                     rg(HG, cs=(ge * 4 * QB + fb * QB + nb * 512,
                                                ge * 4 * QB + fb * QB + nb * 512 + 512))))
                        P.mm_group(rg(bank), terms, tag=f"{etg}.dn{dt}{nb}")
                        P.tt("dve", rg(X[dt], cs=(q0 + nb * 512, q0 + nb * 512 + 512)),
                             rg(X[dt], cs=(q0 + nb * 512, q0 + nb * 512 + 512)),
                             rg(bank), OP.add, tag=f"{etg}.ac{dt}{nb}")
            if e + 1 < E:
                en = e + 1
                P.dma("pool", rg(WU[(en + 1) % 2], cs=(0, 8 * 512)),
                      Rg(d["wup"], en * 1024, (en + 1) * 1024, 0, 512,
                         ap=d["wup"].h[en * 1024:(en + 1) * 1024, :]
                         .rearrange("(r p) f -> p r f", p=128)),
                      tag=f"{etg}.lwu")
                P.dma("pool", rg(WD[en % EG], cs=(0, 4 * 1024)),
                      Rg(d["wdn"], en * 512, (en + 1) * 512, 0, 1024,
                         ap=d["wdn"].h[en * 512:(en + 1) * 512, :]
                         .rearrange("(r p) f -> p r f", p=128)),
                      tag=f"{etg}.lwd")
        _post_ln(P, env, sc,
                 lambda t, c0, c1: rg(X[t], cs=(q0 + c0, q0 + c1)),
                 QB, (q0, q0 + QB), f"{tg}.ln")



def _l1_attn_host(inp):
    g = lambda k: np.asarray(inp[k], np.float32)
    h = {}
    h["l1a_wq"] = g("L1_attn_Wq")
    h["l1a_wk"] = g("L1_attn_Wk")
    h["l1a_wv"] = g("L1_attn_Wv")
    wo = g("L1_attn_Wo").reshape(16, 64, 1024)
    h["l1a_wo2"] = np.ascontiguousarray(
        np.concatenate([wo[:8].transpose(1, 0, 2),
                        wo[8:].transpose(1, 0, 2)], axis=0).reshape(128, 8 * 1024))
    h["l1a_bq"] = _v128(g("L1_attn_bq")[:1024])
    h["l1a_bqg"] = np.ascontiguousarray(g("L1_attn_bq")[1024:].reshape(16, 1))
    h["l1a_bk"] = _v128(g("L1_attn_bk"))
    h["l1a_bvbc"] = np.ascontiguousarray(
        np.broadcast_to(g("L1_attn_bv"), (128, 1024)))
    h["l1a_bo"] = _v128(g("L1_attn_bo"))
    h["l1a_qn"] = _v128(np.tile(g("L1_attn_qn_w"), 16))
    h["l1a_kn"] = _v128(np.tile(g("L1_attn_kn_w"), 16))
    return h


def emit_l1_attn(P, ctx, env):
    X = env["X"]
    QB = 512
    NQ = TOK // QB
    NKT = TOK // 128

    d = {}
    for nm, shp in (("wq", [D, 1040]), ("wk", [D, D]), ("wv", [D, D]),
                    ("wo2", [128, 8 * 1024]), ("bq", [128, 8]),
                    ("bqg", [16, 1]), ("bk", [128, 8]), ("bvbc", [128, 1024]),
                    ("bo", [128, 8]), ("qn", [128, 8]), ("kn", [128, 8])):
        d[nm] = P.dram_in(f"l1a_{nm}", shp)

    BQ = P.sbuf(ctx, "a_BQ", 128, 8, F32)
    BK = P.sbuf(ctx, "a_BK", 128, 8, F32)
    BO = P.sbuf(ctx, "a_BO", 128, 8, F32)
    BQG = P.sbuf(ctx, "a_BQG", 16, 1, F32)
    BVB = P.sbuf(ctx, "a_BVB", 128, 1024, BF16)
    QN = P.sbuf(ctx, "a_QN", 128, 8, F32)
    KN = P.sbuf(ctx, "a_KN", 128, 8, F32)
    for nm, t in (("bq", BQ), ("bk", BK), ("bo", BO), ("bqg", BQG),
                  ("qn", QN), ("kn", KN)):
        P.dma("sp", rg(t), rg(d[nm]))
    P.dma("pool", rg(BVB), rg(d["bvbc"]))
    QNKN = P.sbuf(ctx, "a_QNKN", 128, 8, F32)
    P.tt("dve", rg(QNKN), rg(QN), rg(KN), OP.mult, tag="a.qnkn")

    KH = P.sbuf(ctx, "a_KH", 128, NCT * TOK, BF16)
    VT = P.sbuf(ctx, "a_VT", 128, NKT * NHEAD * 65, BF16)
    XBQ = P.sbuf(ctx, "a_XBQ", 128, NCT * QB, BF16)
    WQB = P.sbuf(ctx, "a_WQB", 128, 8 * 1040, BF16)
    WB1 = Tsr(WQB.h, WQB.name, 128, 8 * 1024, BF16, "sbuf")
    WB2 = P.sbuf(ctx, "a_WB2", 128, 8 * 1024, BF16)
    KR = [P.sbuf(ctx, f"a_KR{i}", 128, 512, F32) for i in range(2)]
    _k2b = P.sbuf(ctx, "a_K2B", 128, 512, BF16)
    K2B = [_k2b, _k2b]
    QH = [P.sbuf(ctx, f"a_QH{i}", 128, 512, BF16) for i in range(2)]
    EK = [P.sbuf(ctx, f"a_EK{i}", 128, 512, BF16) for i in range(4)]
    OTMP = P.sbuf(ctx, "a_OTMP", 64, 512, BF16)
    OC = [P.sbuf(ctx, f"a_OC{i}", 64, 512, BF16) for i in range(2)]
    GRB = P.sbuf(ctx, "a_GRB", 65, 2 * 512, F32)
    SGATE = Tsr(GRB.h, GRB.name, 16, 512, F32, "sbuf")
    RZ = P.sbuf(ctx, "a_RZ", 65, 512, F32)
    ONESF = P.sbuf(ctx, "a_ONES", 128, 128, F32)
    ST1 = [P.sbuf(ctx, f"a_ST1{i}", 2, 512, F32) for i in range(2)]
    ST2 = ST1
    EPSA = P.sbuf(ctx, "a_EPSA", 2, 1, F32)
    R1B = P.sbuf(ctx, "a_R1B", 128, NCT * QB, BF16)
    OT2 = R1B
    R1S = None
    sc = {
        "R1B": (lambda t, c0, c1: rg(R1B, cs=(t * QB + c0, t * QB + c1))),
        "R1S": (lambda t, c0, c1: rg(XBQ, cs=(t * QB + c0, t * QB + c1))),
        "LT1": Tsr(KR[0].h, KR[0].name, 1, QB, F32, "sbuf"),
        "LT2": Tsr(RZ.h, RZ.name, 1, QB, F32, "sbuf"),
        "LT3": Tsr(GRB.h, GRB.name, 1, QB, F32, "sbuf"),
        "EPSLN": P.sbuf(ctx, "a_EPS", 1, 1, F32),
    }
    P.memset("dve", rg(sc["EPSLN"]), 1e-5, tag="a.eps")
    P.memset("dve", rg(EPSA), 1e-6, tag="a.eps2")
    P.memset("dve", rg(ONESF), 1.0, tag="a.ones")

    pm = [P.psum(ctx, f"a_pm{i}", 128, 512, F32) for i in range(8)]
    sc["pmL1"], sc["pmL2"], sc["pmL3"], sc["pmL4"] = pm[0], pm[1], pm[2], pm[3]

    _load_rearr(P, WB1, d["wk"])
    _load_rearr(P, WB2, d["wv"])

    for qb in range(NQ):
        q0 = qb * QB
        tg = f"a1_{qb}"
        for t in range(NCT):
            P.copy("dve", rg(XBQ, cs=(t * QB, (t + 1) * QB)),
                   rg(X[t], cs=(q0, q0 + QB)), tag=f"{tg}.xb{t}")
        for co in range(NCT):
            p2 = co % 2
            bank = pm[4 + p2]
            bst, bbc = pm[0 + p2], pm[2 + p2]
            P.mm_group(rg(bank),
                       [(rg(WB1, cs=(t * 1024 + co * 128, t * 1024 + co * 128 + 128)),
                         rg(XBQ, cs=(t * QB, (t + 1) * QB))) for t in range(NCT)],
                       tag=f"{tg}.k{co}")
            P.ts("dve", rg(KR[p2]), rg(bank), rg(BK, cs=(co, co + 1)), OP.add,
                 tag=f"{tg}.kb{co}")
            P.act(rg(K2B[p2]), rg(KR[p2]), AF.Square, tag=f"{tg}.k2{co}")
            P.mm_group(rg(bst, ps=(0, 2)),
                       [(rg(env["headsel_b"]), rg(K2B[p2]))], tag=f"{tg}.ks{co}")
            P.ts("dve", rg(ST1[p2]), rg(bst, ps=(0, 2)), 1.0 / 64, OP.mult,
                 tag=f"{tg}.km{co}")
            P.act(rg(ST1[p2]), rg(ST1[p2]), AF.Ln, bias=rg(EPSA), tag=f"{tg}.kl{co}")
            P.act(rg(ST1[p2]), rg(ST1[p2]), AF.Exp, scale=-0.5, tag=f"{tg}.ke{co}")
            P.mm_group(rg(bbc), [(rg(env["pat2_f"]), rg(ST1[p2]))],
                       tag=f"{tg}.kc{co}")
            P.stt("dve", rg(KH, cs=(co * TOK + q0, co * TOK + q0 + QB)),
                  rg(KR[p2]), rg(QNKN, cs=(co, co + 1)), rg(bbc), OP.mult,
                  OP.mult, tag=f"{tg}.kh{co}")
        for tt in range(QB // 128):
            gt = qb * 4 + tt
            for nb in range(2):
                bank = pm[4 + (nb % 2)]
                P.mm_group(rg(bank),
                           [(rg(XBQ, cs=(t * QB + tt * 128, t * QB + tt * 128 + 128)),
                             rg(WB2, cs=(t * 1024 + nb * 512, t * 1024 + nb * 512 + 512)))
                            for t in range(NCT)], tag=f"{tg}.v{tt}{nb}")
                vap = VT.h[:, gt * NHEAD * 65:(gt + 1) * NHEAD * 65] \
                    .rearrange("p (h c) -> p h c", h=NHEAD)[:, nb * 8:(nb + 1) * 8, 0:64]
                P.stt("dve",
                      Rg(VT, 0, 128, gt * NHEAD * 65, (gt + 1) * NHEAD * 65, ap=vap),
                      rg(bank), 1.0, rg(BVB, cs=(nb * 512, nb * 512 + 512)),
                      OP.mult, OP.add, tag=f"{tg}.vb{tt}{nb}")
            onesap = VT.h[:, gt * NHEAD * 65:(gt + 1) * NHEAD * 65] \
                .rearrange("p (h c) -> p h c", h=NHEAD)[:, :, 64:65]
            P.memset("dve",
                     Rg(VT, 0, 128, gt * NHEAD * 65, (gt + 1) * NHEAD * 65, ap=onesap),
                     1.0, tag=f"{tg}.vo{tt}")

    _load_rearr(P, WQB, d["wq"], r=8)
    P.dma("pool", rg(WB2, cs=(0, 8 * 1024)), rg(d["wo2"]), tag="a.lwo")
    for qb in range(NQ):
        q0 = qb * QB
        tg = f"a2_{qb}"
        for t in range(NCT):
            P.copy("dve", rg(XBQ, cs=(t * QB, (t + 1) * QB)),
                   rg(X[t], cs=(q0, q0 + QB)), tag=f"{tg}.xb{t}")
        P.mm_group(rg(pm[6], ps=(0, 16)),
                   [(rg(WQB, cs=(t * 1040 + 1024, t * 1040 + 1040)),
                     rg(XBQ, cs=(t * QB, (t + 1) * QB))) for t in range(NCT)],
                   tag=f"{tg}.gt")
        P.act(rg(SGATE), rg(pm[6], ps=(0, 16)), AF.Sigmoid, bias=rg(BQG),
              tag=f"{tg}.gs")
        for ct in range(NCT):
            p2 = ct % 2
            bank = pm[4 + p2]
            P.mm_group(rg(bank),
                       [(rg(WQB, cs=(t * 1040 + ct * 128, t * 1040 + ct * 128 + 128)),
                         rg(XBQ, cs=(t * QB, (t + 1) * QB))) for t in range(NCT)],
                       tag=f"{tg}.q{ct}")
            P.ts("dve", rg(KR[p2]), rg(bank), rg(BQ, cs=(ct, ct + 1)), OP.add,
                 tag=f"{tg}.qb{ct}")
            P.act(rg(K2B[p2]), rg(KR[p2]), AF.Square, tag=f"{tg}.q2{ct}")
            P.mm_group(rg(bank, ps=(0, 2)),
                       [(rg(env["headsel_b"]), rg(K2B[p2]))], tag=f"{tg}.qs{ct}")
            P.ts("dve", rg(ST2[p2]), rg(bank, ps=(0, 2)), 1.0 / 64, OP.mult,
                 tag=f"{tg}.qm{ct}")
            P.act(rg(ST2[p2]), rg(ST2[p2]), AF.Ln, bias=rg(EPSA), tag=f"{tg}.ql{ct}")
            P.act(rg(ST2[p2]), rg(ST2[p2]), AF.Exp, scale=-0.5, tag=f"{tg}.qe{ct}")
            P.mm_group(rg(bank), [(rg(env["pat2_f"]), rg(ST2[p2]))],
                       tag=f"{tg}.qc{ct}")
            P.tt("dve", rg(QH[p2]), rg(KR[p2]), rg(bank), OP.mult,
                 tag=f"{tg}.qh{ct}")
            avb = {0: pm[6], 1: pm[7]}
            sbk = {0: (pm[0], pm[1]), 1: (pm[2], pm[3])}
            for ho in range(2):
                h = 2 * ct + ho
                P.dma("sp",
                      Rg(GRB, 64, 65, ho * QB, (ho + 1) * QB,
                         ap=GRB.h[64:65, ho * QB:(ho + 1) * QB]),
                      Rg(SGATE, h, h + 1, 0, QB, ap=SGATE.h[h:h + 1, 0:QB]),
                      tag=f"{tg}.gx{h}")
            for kt in range(NKT + 1):
                if kt < NKT:
                    for ho in range(2):
                        h = 2 * ct + ho
                        hp = 64 * ho
                        P.mm_group(rg(sbk[ho][kt % 2]),
                                   [(rg(KH, ps=(hp, hp + 64),
                                        cs=(ct * TOK + kt * 128, ct * TOK + kt * 128 + 128)),
                                     rg(QH[ct % 2], ps=(hp, hp + 64)))],
                                   tag=f"{tg}.h{2 * ct + ho}.s{kt}")
                    for ho in range(2):
                        P.act(rg(EK[2 * ho + kt % 2]), rg(sbk[ho][kt % 2]), AF.Exp,
                              scale=0.125, tag=f"{tg}.h{2 * ct + ho}.e{kt}")
                if kt >= 1:
                    kp = kt - 1
                    for ho in range(2):
                        h = 2 * ct + ho
                        P.mm_group(rg(avb[ho], ps=(0, 65)),
                                   [(Rg(VT, 0, 128, kp * NHEAD * 65 + h * 65,
                                        kp * NHEAD * 65 + (h + 1) * 65),
                                     rg(EK[2 * ho + kp % 2]))],
                                   start=(kp == 0), stop=(kp == NKT - 1),
                                   tag=f"{tg}.h{h}.av{kp}")
            for ho in range(2):
                h = 2 * ct + ho
                htg = f"{tg}.h{h}"
                pav = avb[ho]
                rzap = RZ.h[64:65, 0:QB]
                gbap = GRB.h[64:65, ho * QB:(ho + 1) * QB]
                rzrg = Rg(RZ, 64, 65, 0, QB, ap=rzap)
                gbrg = Rg(GRB, 64, 65, ho * QB, (ho + 1) * QB, ap=gbap)
                P.act(rzrg, Rg(pav, 64, 65, 0, 512, ap=pav.h[64:65, 0:512]),
                      AF.Ln, tag=f"{htg}.rzl")
                P.act(rzrg, rzrg, AF.Exp, scale=-1.0, tag=f"{htg}.rze")
                P.tt("dve", gbrg, rzrg, gbrg, OP.mult, tag=f"{htg}.f")
                P.mm_group(rg(sbk[ho][0]),
                           [(Rg(ONESF, 64, 65, 0, 128, ap=ONESF.h[64:65, :]),
                             gbrg)],
                           tag=f"{htg}.fb")
                P.copy("act", rg(OC[ho]), rg(pav, ps=(0, 64)), tag=f"{htg}.oc")
                slot = h % 8
                if h < 8:
                    P.tt("dve", rg(OT2, ps=(0, 64), cs=(slot * 512, slot * 512 + 512)),
                         rg(OC[ho]), rg(sbk[ho][0], ps=(0, 64)), OP.mult,
                         tag=f"{htg}.om")
                else:
                    P.tt("dve", rg(OTMP), rg(OC[ho]), rg(sbk[ho][0], ps=(0, 64)),
                         OP.mult, tag=f"{htg}.om")
                    P.dma("sp",
                          Rg(OT2, 64, 128, slot * 512, slot * 512 + 512,
                             ap=OT2.h[64:128, slot * 512:slot * 512 + 512]),
                          rg(OTMP), tag=f"{htg}.omv")
        for co in range(NCT):
            P.mm_group(rg(pm[4]),
                       [(rg(WB2, ps=(0, 64), cs=(h * 1024 + co * 128, h * 1024 + co * 128 + 128)),
                         rg(OT2, ps=(0, 64), cs=(h * 512, (h + 1) * 512)))
                        for h in range(8)], tag=f"{tg}.wo_lo{co}")
            P.mm_group(rg(pm[5]),
                       [(rg(WB2, ps=(64, 128), cs=(h * 1024 + co * 128, h * 1024 + co * 128 + 128)),
                         rg(OT2, ps=(64, 128), cs=(h * 512, (h + 1) * 512)))
                        for h in range(8)], tag=f"{tg}.wo_hi{co}")
            P.stt("dve", rg(X[co], cs=(q0, q0 + QB)), rg(pm[4]),
                  rg(BO, cs=(co, co + 1)), rg(X[co], cs=(q0, q0 + QB)),
                  OP.add, OP.add, tag=f"{tg}.res1_{co}")
            P.tt("dve", rg(X[co], cs=(q0, q0 + QB)),
                 rg(X[co], cs=(q0, q0 + QB)), rg(pm[5]), OP.add,
                 tag=f"{tg}.res2_{co}")
        _post_ln(P, env, sc,
                 lambda t, c0, c1: rg(X[t], cs=(q0 + c0, q0 + c1)),
                 QB, (q0, q0 + QB), f"{tg}.ln")



def _l1_ffn_host(inp):
    g = lambda k: np.asarray(inp[k], np.float32)
    h = {}
    h["l1f_wq"] = g("L1_ffn_wq")
    h["l1f_wk"] = g("L1_ffn_wk")
    h["l1f_wv"] = g("L1_ffn_wv")
    wo = g("L1_ffn_wo").reshape(16, 64, 1024)
    h["l1f_wo2"] = np.ascontiguousarray(
        np.concatenate([wo[:8].transpose(1, 0, 2),
                        wo[8:].transpose(1, 0, 2)], axis=0).reshape(128, 8 * 1024))
    h["l1f_rw"] = np.ascontiguousarray(g("L1_ffn_router_w"))
    h["l1f_fc1"] = np.ascontiguousarray(g("L1_ffn_exp_fc1").reshape(8 * 1024, 2048))
    h["l1f_fc2"] = np.ascontiguousarray(g("L1_ffn_exp_fc2").reshape(8 * 2048, 1024))
    cm = np.zeros((128, 128), np.float32)
    for k in range(128):
        cm[k, :k] = -1e30
    h["cmask_f"] = cm
    return h


def _rms_qb(P, env, src_of, RB, istds, w, eps_t, pms, sq, tag, rb_w=None):
    pmS, pmB = pms
    if rb_w is None:
        rb_w = w
    for c0 in range(0, w, 512):
        wb = min(512, w - c0)
        for t in range(NCT):
            P.act(rg(sq[t % 2], cs=(0, wb)), src_of(t, c0, c0 + wb), AF.Square,
                  tag=f"{tag}.sq{t}_{c0}")
            P.mm_group(rg(pmS, ps=(0, 1), cs=(0, wb)),
                       [(rg(env["ones_col_b"]), rg(sq[t % 2], cs=(0, wb)))],
                       start=(t == 0), stop=(t == NCT - 1), tag=f"{tag}.ss{t}_{c0}")
        P.ts("dve", rg(istds, cs=(c0, c0 + wb)), rg(pmS, ps=(0, 1), cs=(0, wb)),
             1.0 / D, OP.mult, tag=f"{tag}.v{c0}")
        P.act(rg(istds, cs=(c0, c0 + wb)), rg(istds, cs=(c0, c0 + wb)), AF.Ln,
              bias=eps_t, tag=f"{tag}.l{c0}")
        P.act(rg(istds, cs=(c0, c0 + wb)), rg(istds, cs=(c0, c0 + wb)), AF.Exp,
              scale=-0.5, tag=f"{tag}.e{c0}")
        P.mm_group(rg(pmB, cs=(0, wb)),
                   [(rg(env["ones_row_f"]), rg(istds, cs=(c0, c0 + wb)))],
                   tag=f"{tag}.b{c0}")
        for t in range(NCT):
            P.tt("dve", rg(RB, cs=(t * rb_w + c0, t * rb_w + c0 + wb)),
                 src_of(t, c0, c0 + wb),
                 rg(pmB, cs=(0, wb)), OP.mult, tag=f"{tag}.a{t}_{c0}")


def emit_l1_ffn(P, ctx, env):
    X = env["X"]
    QB = 512
    NQ = TOK // QB
    NKT = TOK // 128

    d = {}
    for nm, shp in (("wq", [D, D]), ("wk", [D, D]), ("wv", [D, D]),
                    ("wo2", [128, 8 * 1024]), ("rw", [D, 8]),
                    ("fc1", [8 * 1024, 2048]), ("fc2", [8 * 2048, 1024]),
                    ("cmask", [128, 128])):
        key = "cmask_f" if nm == "cmask" else f"l1f_{nm}"
        d[nm] = P.dram_in(key, shp)

    Xb = P.sbuf(ctx, "n_Xb", 128, NCT * TOK, BF16)
    for t in range(NCT):
        P.copy("dve", Rg(Xb, 0, 128, t * TOK, (t + 1) * TOK,
                         ap=Xb.h[:, t * TOK:(t + 1) * TOK]),
               rg(X[t]), tag=f"n.xb{t}")
    CM = P.sbuf(ctx, "n_CM", 128, 128, F32)
    P.dma("sp", rg(CM), rg(d["cmask"]))
    EPSR = P.sbuf(ctx, "n_EPSR", 1, 1, F32)
    P.memset("dve", rg(EPSR), 1e-5, tag="n.eps")

    with ExitStack() as atx:
        KB = P.sbuf(atx, "n_KB", 128, NCT * TOK, BF16)
        VT = P.sbuf(atx, "n_VT", 128, NKT * NHEAD * 65, BF16)
        WBUF = P.sbuf(atx, "n_WB", 128, 8 * 1024, BF16)
        WO2x = WBUF
        RB = P.sbuf(atx, "n_RB", 128, NCT * QB, BF16)
        EK = [P.sbuf(atx, f"n_EK{i}", 128, 512, BF16) for i in range(4)]
        OT2 = RB
        OC = [P.sbuf(atx, f"n_OC{i}", 64, 512, BF16) for i in range(2)]
        RZ = P.sbuf(atx, "n_RZ", 65, 2 * 512, F32)
        ONESF = P.sbuf(atx, "n_ONES", 128, 128, F32)
        QH8 = P.sbuf(atx, "n_QH8", 128, 8 * 512, BF16)
        SQ = [EK[0], EK[1]]
        OTMP = Tsr(EK[2].h, EK[2].name, 64, 512, BF16, "sbuf")
        ISTD = Tsr(RZ.h, RZ.name, 1, QB, F32, "sbuf")
        P.memset("dve", rg(ONESF), 1.0, tag="n.ones")
        pm = [P.psum(atx, f"n_pm{i}", 128, 512, F32) for i in range(8)]

        _load_rearr(P, WBUF, d["wk"])
        for qb in range(NQ):
            q0 = qb * QB
            tg = f"nk{qb}"
            _rms_qb(P, env,
                    (lambda t, c0, c1, q0=q0: Rg(Xb, 0, 128, t * TOK + q0 + c0,
                     t * TOK + q0 + c1, ap=Xb.h[:, t * TOK + q0 + c0:t * TOK + q0 + c1])),
                    RB, ISTD, QB, rg(EPSR), (pm[6], pm[7]), SQ, tg)
            for co in range(NCT):
                bank = pm[4 + (co % 2)]
                P.mm_group(rg(bank),
                           [(rg(WBUF, cs=(t * 1024 + co * 128, t * 1024 + co * 128 + 128)),
                             rg(RB, cs=(t * QB, (t + 1) * QB))) for t in range(NCT)],
                           tag=f"{tg}.k{co}")
                P.copy("dve", Rg(KB, 0, 128, co * TOK + q0, co * TOK + q0 + QB,
                                 ap=KB.h[:, co * TOK + q0:co * TOK + q0 + QB]),
                       rg(bank), tag=f"{tg}.ke{co}")
        _load_rearr(P, WBUF, d["wv"])
        for qb in range(NQ):
            q0 = qb * QB
            tg = f"nv{qb}"
            _rms_qb(P, env,
                    (lambda t, c0, c1, q0=q0: Rg(Xb, 0, 128, t * TOK + q0 + c0,
                     t * TOK + q0 + c1, ap=Xb.h[:, t * TOK + q0 + c0:t * TOK + q0 + c1])),
                    RB, ISTD, QB, rg(EPSR), (pm[6], pm[7]), SQ, tg)
            for tt in range(QB // 128):
                gt = qb * 4 + tt
                for nb in range(2):
                    bank = pm[4 + (nb % 2)]
                    P.mm_group(rg(bank),
                               [(rg(RB, cs=(t * QB + tt * 128, t * QB + tt * 128 + 128)),
                                 rg(WBUF, cs=(t * 1024 + nb * 512, t * 1024 + nb * 512 + 512)))
                                for t in range(NCT)], tag=f"{tg}.v{tt}{nb}")
                    vap = VT.h[:, gt * NHEAD * 65:(gt + 1) * NHEAD * 65] \
                        .rearrange("p (h c) -> p h c", h=NHEAD)[:, nb * 8:(nb + 1) * 8, 0:64]
                    P.copy("dve",
                           Rg(VT, 0, 128, gt * NHEAD * 65, (gt + 1) * NHEAD * 65, ap=vap),
                           rg(bank), tag=f"{tg}.ve{tt}{nb}")
                onesap = VT.h[:, gt * NHEAD * 65:(gt + 1) * NHEAD * 65] \
                    .rearrange("p (h c) -> p h c", h=NHEAD)[:, :, 64:65]
                P.memset("dve",
                         Rg(VT, 0, 128, gt * NHEAD * 65, (gt + 1) * NHEAD * 65, ap=onesap),
                         1.0, tag=f"{tg}.vo{tt}")

        for qb in range(NQ):
            q0 = qb * QB
            tg = f"nq{qb}"
            _load_rearr(P, WBUF, d["wq"])
            _rms_qb(P, env,
                    (lambda t, c0, c1, q0=q0: Rg(Xb, 0, 128, t * TOK + q0 + c0,
                     t * TOK + q0 + c1, ap=Xb.h[:, t * TOK + q0 + c0:t * TOK + q0 + c1])),
                    RB, ISTD, QB, rg(EPSR), (pm[6], pm[7]), SQ, tg)
            nkt = 4 * (qb + 1)
            for ct in range(NCT):
                P.mm_group(rg(pm[4 + ct % 2]),
                           [(rg(WBUF, cs=(t * 1024 + ct * 128, t * 1024 + ct * 128 + 128)),
                             rg(RB, cs=(t * QB, (t + 1) * QB))) for t in range(NCT)],
                           tag=f"{tg}.q{ct}")
                P.copy("dve", rg(QH8, cs=(ct * 512, (ct + 1) * 512)),
                       rg(pm[4 + ct % 2]), tag=f"{tg}.qe{ct}")
            _load_rearr(P, WO2x, d["wo2"], direct=True)
            for ct in range(NCT):
                avb = {0: pm[6], 1: pm[7]}
                sbk = {0: (pm[0], pm[1]), 1: (pm[2], pm[3])}
                for kt in range(nkt + 1):
                    if kt < nkt:
                        dj = kt - 4 * qb
                        for ho in range(2):
                            h = 2 * ct + ho
                            hp = 64 * ho
                            P.mm_group(rg(sbk[ho][kt % 2]),
                                       [(rg(KB, ps=(hp, hp + 64),
                                            cs=(ct * TOK + kt * 128, ct * TOK + kt * 128 + 128)),
                                         rg(QH8, ps=(hp, hp + 64),
                                            cs=(ct * 512, (ct + 1) * 512)))],
                                       tag=f"{tg}.h{h}.s{kt}")
                        for ho in range(2):
                            htg = f"{tg}.h{2 * ct + ho}"
                            sb = sbk[ho][kt % 2]
                            ek = EK[2 * ho + kt % 2]
                            if dj >= 0:
                                P.tt("dve",
                                     rg(sb, cs=(dj * 128, dj * 128 + 128)),
                                     rg(sb, cs=(dj * 128, dj * 128 + 128)),
                                     rg(CM), OP.add, tag=f"{htg}.cm{kt}")
                                if dj > 0:
                                    P.memset("dve", rg(ek, cs=(0, dj * 128)),
                                             0.0, tag=f"{htg}.z{kt}")
                                P.act(rg(ek, cs=(dj * 128, 512)),
                                      rg(sb, cs=(dj * 128, 512)), AF.Exp,
                                      scale=0.125, tag=f"{htg}.e{kt}")
                            else:
                                P.act(rg(ek), rg(sb), AF.Exp, scale=0.125,
                                      tag=f"{htg}.e{kt}")
                    if kt >= 1:
                        kp = kt - 1
                        for ho in range(2):
                            h = 2 * ct + ho
                            P.mm_group(rg(avb[ho], ps=(0, 65)),
                                       [(Rg(VT, 0, 128, kp * NHEAD * 65 + h * 65,
                                            kp * NHEAD * 65 + (h + 1) * 65),
                                         rg(EK[2 * ho + kp % 2]))],
                                       start=(kp == 0), stop=(kp == nkt - 1),
                                       tag=f"{tg}.h{h}.av{kp}")
                for ho in range(2):
                    h = 2 * ct + ho
                    htg = f"{tg}.h{h}"
                    pav = avb[ho]
                    rzap = RZ.h[64:65, ho * QB:(ho + 1) * QB]
                    rzrg = Rg(RZ, 64, 65, ho * QB, (ho + 1) * QB, ap=rzap)
                    P.act(rzrg, Rg(pav, 64, 65, 0, 512, ap=pav.h[64:65, 0:512]),
                          AF.Ln, tag=f"{htg}.rzl")
                    P.act(rzrg, rzrg, AF.Exp, scale=-1.0, tag=f"{htg}.rze")
                    P.mm_group(rg(sbk[ho][0]),
                               [(Rg(ONESF, 64, 65, 0, 128, ap=ONESF.h[64:65, :]),
                                 rzrg)],
                               tag=f"{htg}.fb")
                    P.copy("act", rg(OC[ho]), rg(pav, ps=(0, 64)), tag=f"{htg}.oc")
                    slot = h % 8
                    if h < 8:
                        P.tt("dve", rg(OT2, ps=(0, 64), cs=(slot * 512, slot * 512 + 512)),
                             rg(OC[ho]), rg(sbk[ho][0], ps=(0, 64)), OP.mult,
                             tag=f"{htg}.om")
                    else:
                        P.tt("dve", rg(OTMP), rg(OC[ho]), rg(sbk[ho][0], ps=(0, 64)),
                             OP.mult, tag=f"{htg}.om")
                        P.dma("sp",
                              Rg(OT2, 64, 128, slot * 512, slot * 512 + 512,
                                 ap=OT2.h[64:128, slot * 512:slot * 512 + 512]),
                              rg(OTMP), tag=f"{htg}.omv")
            for co in range(NCT):
                P.mm_group(rg(pm[4]),
                           [(rg(WO2x, ps=(0, 64), cs=(h * 1024 + co * 128, h * 1024 + co * 128 + 128)),
                             rg(OT2, ps=(0, 64), cs=(h * 512, (h + 1) * 512)))
                            for h in range(8)], tag=f"{tg}.wo_lo{co}")
                P.mm_group(rg(pm[5]),
                           [(rg(WO2x, ps=(64, 128), cs=(h * 1024 + co * 128, h * 1024 + co * 128 + 128)),
                             rg(OT2, ps=(64, 128), cs=(h * 512, (h + 1) * 512)))
                            for h in range(8)], tag=f"{tg}.wo_hi{co}")
                P.tt("dve", rg(X[co], cs=(q0, q0 + QB)),
                     rg(X[co], cs=(q0, q0 + QB)), rg(pm[4]), OP.add,
                     tag=f"{tg}.r1_{co}")
                P.tt("dve", rg(X[co], cs=(q0, q0 + QB)),
                     rg(X[co], cs=(q0, q0 + QB)), rg(pm[5]), OP.add,
                     tag=f"{tg}.r2_{co}")

    P.barrier(tag="n_moebar")

    with ExitStack() as mtx:
        RW = P.sbuf(mtx, "n_RW", 128, 8 * 8, BF16)
        _load_rearr(P, RW, d["rw"])
        F1 = [P.sbuf(mtx, f"n_F1{i}", 128, 8 * 1024, BF16) for i in range(2)]
        F2 = [P.sbuf(mtx, f"n_F2{i}", 128, 8 * 1024, BF16) for i in range(2)]
        RB = P.sbuf(mtx, "n_RB2", 128, NCT * QB, BF16)
        ISTD = P.sbuf(mtx, "n_IST2", 1, QB, F32)
        SQ = [P.sbuf(mtx, f"n_SQ2{i}", 128, 512, BF16) for i in range(2)]
        H1 = P.sbuf(mtx, "n_H1", 128, 16 * QB, BF16)
        HS = P.sbuf(mtx, "n_HS", 128, 512, BF16)
        HT = P.sbuf(mtx, "n_HT", 128, 512, BF16)
        LGT = P.sbuf(mtx, "n_LG", 128, 4 * 8, F32)
        MXT = P.sbuf(mtx, "n_MX", 128, 1, F32)
        EQT = P.sbuf(mtx, "n_EQ", 128, 8, F32)
        MSK = P.sbuf(mtx, "n_MS", 128, 4 * 8, F32)
        SS2 = P.sbuf(mtx, "n_SS", 128, 1, F32)
        CWE = P.sbuf(mtx, "n_CWE", 8, QB, F32)
        CWR = [P.sbuf(mtx, f"n_CWR{i}", 1, QB, F32) for i in range(2)]
        CWB = P.sbuf(mtx, "n_CWB", 128, QB, BF16)
        pm = [P.psum(mtx, f"n2_pm{i}", 128, 512, F32) for i in range(8)]
        sc = {
            "R1B": (lambda t, c0, c1: rg(RB, cs=(t * QB + c0, t * QB + c1))),
            "R1S": (lambda t, c0, c1: rg(H1, cs=(8 * QB + t * QB + c0,
                                                 8 * QB + t * QB + c1))),
            "LT1": ISTD,
            "LT2": P.sbuf(mtx, "n_LT2", 1, QB, F32),
            "LT3": P.sbuf(mtx, "n_LT3", 1, QB, F32),
            "EPSLN": EPSR,
            "pmL1": pm[0], "pmL2": pm[1], "pmL3": pm[2], "pmL4": pm[3],
        }
        for qb in range(NQ):
            q0 = qb * QB
            tg = f"nm{qb}"
            _rms_qb(P, env,
                    (lambda t, c0, c1, q0=q0: rg(X[t], cs=(q0 + c0, q0 + c1))),
                    RB, ISTD, QB, rg(EPSR), (pm[6], pm[7]), SQ, tg)
            for tt in range(QB // 128):
                P.mm_group(rg(pm[6], cs=(0, 8)),
                           [(rg(RB, cs=(t * QB + tt * 128, t * QB + tt * 128 + 128)),
                             rg(RW, cs=(t * 8, (t + 1) * 8))) for t in range(NCT)],
                           tag=f"{tg}.lg{tt}")
                lcs = (tt * 8, (tt + 1) * 8)
                def _mx(e, pm6=pm[6]):
                    return e.reduce_max(MXT.h[:, 0:1], pm6.h[:, 0:8],
                                        mybir.AxisListType.X, negate=True)
                P.op("dve", _mx, reads=[rg(pm[6])], writes=[rg(MXT)],
                     tag=f"{tg}.mx{tt}")
                P.act(rg(LGT, cs=lcs), rg(pm[6], cs=(0, 8)), AF.Exp,
                      bias=rg(MXT), tag=f"{tg}.ex{tt}")
                P.memset("dve", rg(MSK, cs=lcs), 0.0, tag=f"{tg}.m0{tt}")
                P.copy("dve", rg(EQT), rg(LGT, cs=lcs), tag=f"{tg}.cp{tt}")
                for it in range(2):
                    P.reduce(rg(MXT), rg(EQT), OP.max, tag=f"{tg}.tm{tt}_{it}")
                    P.ts("dve", rg(SQ[0], ps=(0, 128), cs=(0, 8)), rg(EQT),
                         rg(MXT), OP.is_ge, tag=f"{tg}.te{tt}_{it}")
                    P.tt("dve", rg(MSK, cs=lcs), rg(MSK, cs=lcs),
                         rg(SQ[0], ps=(0, 128), cs=(0, 8)), OP.add,
                         tag=f"{tg}.ta{tt}_{it}")
                    P.stt("dve", rg(EQT), rg(SQ[0], ps=(0, 128), cs=(0, 8)),
                          -1e30, rg(EQT), OP.mult, OP.add, tag=f"{tg}.ts{tt}_{it}")
                P.tt("dve", rg(LGT, cs=lcs), rg(LGT, cs=lcs), rg(MSK, cs=lcs),
                     OP.mult, tag=f"{tg}.wm{tt}")
                P.reduce(rg(SS2), rg(LGT, cs=lcs), OP.add, tag=f"{tg}.sw{tt}")
                P.recip(rg(SS2), rg(SS2), tag=f"{tg}.sr{tt}")
                P.ts("dve", rg(LGT, cs=lcs), rg(LGT, cs=lcs), rg(SS2), OP.mult,
                     tag=f"{tg}.cn{tt}")
                P.transpose(rg(pm[7], ps=(0, 8), cs=(0, 128)),
                            rg(LGT, cs=lcs), rg(env["identity_f"]),
                            tag=f"{tg}.tr{tt}")
                P.copy("dve", rg(CWE, ps=(0, 8), cs=(tt * 128, (tt + 1) * 128)),
                       rg(pm[7], ps=(0, 8), cs=(0, 128)), tag=f"{tg}.tre{tt}")
            P.dma("pool", rg(F1[0], cs=(0, 8 * 1024)),
                  Rg(d["fc1"], 0, 1024, 0, 1024,
                     ap=d["fc1"].h[0:1024, 0:1024]
                     .rearrange("(r p) f -> p r f", p=128)),
                  tag=f"{tg}.l10")
            for fhn in range(2):
                P.dma("pool", rg(F2[fhn], cs=(0, 8 * 1024)),
                      Rg(d["fc2"], fhn * 1024, (fhn + 1) * 1024, 0, 1024,
                         ap=d["fc2"].h[fhn * 1024:(fhn + 1) * 1024, :]
                         .rearrange("(r p) f -> p r f", p=128)),
                      tag=f"{tg}.l20_{fhn}")
            for e in range(8):
                etg = f"{tg}.e{e}"
                P.dma("sp", rg(CWR[e % 2]),
                      Rg(CWE, e, e + 1, 0, QB, ap=CWE.h[e:e + 1, :]),
                      tag=f"{etg}.cwx")
                P.mm_group(rg(pm[7]),
                           [(rg(env["ones_row_f"]), rg(CWR[e % 2]))],
                           tag=f"{etg}.cb")
                P.copy("dve", rg(CWB), rg(pm[7]), tag=f"{etg}.cbe")
                for fh in range(2):
                    par = (2 * e + fh) % 2
                    f1 = F1[par]
                    for ft in range(8):
                        bank = pm[4 + (ft % 2)]
                        P.mm_group(rg(bank),
                                   [(rg(f1, cs=(t * 1024 + ft * 128, t * 1024 + ft * 128 + 128)),
                                     rg(RB, cs=(t * QB, (t + 1) * QB)))
                                    for t in range(NCT)], tag=f"{etg}.up{fh}{ft}")
                        P.act(rg(HS), rg(bank), AF.Relu, tag=f"{etg}.rl{fh}{ft}")
                        P.tt("dve", rg(HT), rg(HS), rg(CWB), OP.mult,
                             tag=f"{etg}.s1{fh}{ft}")
                        P.tt("dve", rg(H1, cs=((fh * 8 + ft) * QB, (fh * 8 + ft + 1) * QB)),
                             rg(HT), rg(HS), OP.mult, tag=f"{etg}.s2{fh}{ft}")
                    nxt = 2 * e + fh + 1
                    if nxt < 16:
                        en, fhn = nxt // 2, nxt % 2
                        P.dma("pool", rg(F1[nxt % 2], cs=(0, 8 * 1024)),
                              Rg(d["fc1"], en * 1024, (en + 1) * 1024,
                                 fhn * 1024, (fhn + 1) * 1024,
                                 ap=d["fc1"].h[en * 1024:(en + 1) * 1024,
                                               fhn * 1024:(fhn + 1) * 1024]
                                 .rearrange("(r p) f -> p r f", p=128)),
                              tag=f"{etg}.l1n{fh}")
                for dth in range(2):
                    for dq in range(4):
                        dt = dth * 4 + dq
                        bank = pm[dq]
                        P.mm_group(rg(bank),
                                   [(rg(F2[fh],
                                        cs=(ft * 1024 + dt * 128, ft * 1024 + dt * 128 + 128)),
                                     rg(H1, cs=((fh * 8 + ft) * QB, (fh * 8 + ft + 1) * QB)))
                                    for fh in range(2) for ft in range(8)],
                                   tag=f"{etg}.dn{dt}")
                        P.tt("dve", rg(X[dt], cs=(q0, q0 + QB)),
                             rg(X[dt], cs=(q0, q0 + QB)), rg(bank), OP.add,
                             tag=f"{etg}.ac{dt}")
                if e + 1 < 8:
                    en = e + 1
                    for fhn in range(2):
                        P.dma("pool", rg(F2[fhn], cs=(0, 8 * 1024)),
                              Rg(d["fc2"], en * 2048 + fhn * 1024,
                                 en * 2048 + (fhn + 1) * 1024, 0, 1024,
                                 ap=d["fc2"].h[en * 2048 + fhn * 1024:
                                               en * 2048 + (fhn + 1) * 1024, :]
                                 .rearrange("(r p) f -> p r f", p=128)),
                              tag=f"{etg}.l2n{fhn}")
            for t in range(NCT):
                P.tt("dve", rg(X[t], cs=(q0, q0 + QB)),
                     rg(X[t], cs=(q0, q0 + QB)),
                     Rg(Xb, 0, 128, t * TOK + q0, t * TOK + q0 + QB,
                        ap=Xb.h[:, t * TOK + q0:t * TOK + q0 + QB]),
                     OP.add, tag=f"{tg}.xr{t}")
            _post_ln(P, env, sc,
                     (lambda t, c0, c1, q0=q0: rg(X[t], cs=(q0 + c0, q0 + c1))),
                     QB, (q0, q0 + QB), f"{tg}.ln")
